# version 21
# baseline (speedup 1.0000x reference)
import numpy as np
import ml_dtypes

import concourse.bass as bass
import concourse.mybir as mybir
from concourse.bass_utils import run_bass_kernel_spmd

F32 = mybir.dt.float32
BF16 = mybir.dt.bfloat16
AF = mybir.ActivationFunctionType
ALU = mybir.AluOpType

D = 1024
T = 2048
NCT = D // 128
NTT = T // 128
HEADS = 16
HD = 64
CHUNK = 64
DH = 4096
LN_EPS = 1e-5
GN_EPS = 1e-5 * 64
RMS_EPS = 1e-6
BASE_G, BASE_L = 1.0e6, 1.0e4
KVH = 4
LAT = 256

bf = ml_dtypes.bfloat16

SEMS = ("WD", "XD", "XP", "OD", "TE", "AC", "DV", "PO")
ENGS = ("tensor", "scalar", "vector", "sync", "gpsimd")



def _lhsT_strips(w_eff, sm):
    K, M = w_eff.shape
    kt, ns = K // 128, M // sm
    a = w_eff.reshape(kt, 128, ns, sm).transpose(2, 1, 0, 3)
    return np.ascontiguousarray(a.reshape(ns, 128, kt * sm))


def _vec_tiles(v):
    return np.ascontiguousarray(v.reshape(-1, 128).T)


def _host_consts():
    c = {}
    c["identity128"] = np.eye(128, dtype=bf)
    c["ones_bf"] = np.ones((128, 1), dtype=bf)
    c["ones_f32"] = np.ones((128, 1), dtype=np.float32)
    c["onesrow_f32"] = np.ones((1, 128), dtype=np.float32)
    m = np.zeros((128, CHUNK), dtype=bf)
    base = (np.arange(CHUNK)[:, None] < np.arange(CHUNK)[None, :]).astype(bf)
    m[0:64] = base
    m[64:128] = base
    c["wkv_mask"] = m
    gm = np.zeros((4, 128, 512), dtype=bf)
    kk = np.arange(128)[:, None]
    qq = np.arange(512)[None, :]
    for j in range(4):
        gm[j] = (qq >= kk + j * 128).astype(bf)
    c["gmask"] = gm.reshape(4 * 128, 512)
    lm = np.zeros((6, 128, 512), dtype=bf)
    kk = np.arange(128)[:, None]
    qq = np.arange(512)[None, :]
    for rel in range(-2, 4):
        diff = qq - (rel * 128 + kk)
        lm[rel + 2] = ((diff >= 0) & (diff < 256)).astype(bf)
    c["lmask"] = lm.reshape(6 * 128, 512)
    for name, base_f in (("g", BASE_G), ("l", BASE_L)):
        inv = base_f ** (-np.arange(0, HD, 2, dtype=np.float64) / HD)
        fr = np.arange(T, dtype=np.float64)[:, None] * inv[None, :]
        cs = np.cos(fr).astype(np.float32).reshape(NTT, 128, 32).transpose(1, 0, 2)
        sn = np.sin(fr).astype(np.float32).reshape(NTT, 128, 32).transpose(1, 0, 2)
        c[f"cos_{name}"] = np.ascontiguousarray(cs.reshape(128, NTT * 32))
        c[f"sin_{name}"] = np.ascontiguousarray(sn.reshape(128, NTT * 32))
    return c


def prep_global(inp):
    g = {}
    w1s = _lhsT_strips(np.ascontiguousarray(inp["L0_ffn_w1_w"].T), 128)
    w2s = _lhsT_strips(np.ascontiguousarray(inp["L0_ffn_w2_w"].T), 128)
    g["ffn_w12"] = np.ascontiguousarray(np.concatenate([w1s, w2s], axis=2))
    g["ffn_h1"] = _lhsT_strips(np.ascontiguousarray(inp["L0_ffn_h1_w"].T), 256)
    g["ffn_h2"] = _lhsT_strips(np.ascontiguousarray(inp["L0_ffn_h2_w"].T), 256)
    g["ffn_w3"] = _lhsT_strips(np.ascontiguousarray(inp["L0_ffn_w3_w"].T), 256)
    g["ffn_s1"] = _vec_tiles(inp["L0_ffn_w1_s"])
    g["ffn_s2"] = _vec_tiles(inp["L0_ffn_w2_s"])
    g["ffn_h1s"] = _vec_tiles(inp["L0_ffn_h1_s"])
    g["ffn_h2s"] = _vec_tiles(inp["L0_ffn_h2_s"])
    g["ffn_w3s"] = _vec_tiles(inp["L0_ffn_w3_s"])
    for nm in ("W_r", "W_k", "W_v", "W_g", "W_o"):
        g[f"aa_{nm}"] = _lhsT_strips(inp[f"L0_attn_a_{nm}"], 256)
    g["aa_W_v_r"] = np.ascontiguousarray(inp["L0_attn_a_W_v"].reshape(8, 128, 1024))
    g["aa_maa_w1"] = _lhsT_strips(inp["L0_attn_a_time_maa_w1"], 160)
    w2 = inp["L0_attn_a_time_maa_w2"]
    w2p1 = np.zeros((128, 1024), dtype=np.float32)
    for f in range(3):
        w2p1[f * 32:(f + 1) * 32] = w2[f]
    w2p2 = np.zeros((64, 1024), dtype=np.float32)
    w2p2[0:32] = w2[3]
    w2p2[32:64] = w2[4]
    g["aa_w2p1"] = w2p1
    g["aa_w2p2"] = w2p2
    g["aa_dec_w1"] = _lhsT_strips(inp["L0_attn_a_time_decay_w1"], 64)
    g["aa_dec_w2"] = np.ascontiguousarray(inp["L0_attn_a_time_decay_w2"])
    bd2 = np.zeros((128, 2), dtype=bf)
    bd2[0:64, 0] = 1
    bd2[64:128, 1] = 1
    g["aa_bd2"] = bd2
    ex2 = np.zeros((2, 128), dtype=np.float32)
    ex2[0, 0:64] = 1
    ex2[1, 64:128] = 1
    g["aa_ex2"] = ex2
    for nm in ("time_maa_x", "time_maa_w", "time_maa_k", "time_maa_v", "time_maa_r",
               "time_maa_g", "time_decay"):
        g[f"aa_{nm}"] = _vec_tiles(inp[f"L0_attn_a_{nm}"])
    g["aa_ln_gamma"] = _vec_tiles(inp["L0_attn_a_ln_gamma"])
    g["aa_ln_beta"] = _vec_tiles(inp["L0_attn_a_ln_beta"])
    u = inp["L0_attn_a_time_faaaa"].reshape(D)
    Bu = np.zeros((NCT, 128, 128), dtype=np.float32)
    for t in range(NCT):
        for p in range(128):
            blk = (p // 64) * 64
            Bu[t, p, blk:blk + 64] = u[t * 128 + p]
    g["aa_Bu"] = np.ascontiguousarray(Bu.reshape(NCT * 128, 128))
    g["ab_g_wq_r"] = np.ascontiguousarray(inp["L0_attn_b_g_wq"].reshape(8, 128, 1024))
    g["ab_g_kvd"] = _lhsT_strips(inp["L0_attn_b_g_kv_down"], 256)
    g["ab_g_kup_r"] = np.ascontiguousarray(inp["L0_attn_b_g_k_up"].reshape(2, 128, 256))
    g["ab_g_vup_r"] = np.ascontiguousarray(inp["L0_attn_b_g_v_up"].reshape(2, 128, 256))
    g["ab_g_wo"] = _lhsT_strips(inp["L0_attn_b_g_wo"], 512)
    g["ab_l_wq_r"] = np.ascontiguousarray(inp["L0_attn_b_l_wq"].reshape(8, 128, 1024))
    g["ab_l_wk_r"] = np.ascontiguousarray(inp["L0_attn_b_l_wk"].reshape(8, 128, 256))
    g["ab_l_wv_r"] = np.ascontiguousarray(inp["L0_attn_b_l_wv"].reshape(8, 128, 256))
    g["ab_l_wo"] = _lhsT_strips(inp["L0_attn_b_l_wo"], 512)
    g["ab_norm1"] = _vec_tiles(inp["L0_attn_b_norm1_w"])
    g["ab_norm2"] = _vec_tiles(inp["L0_attn_b_norm2_w"])
    g.update(_host_consts())
    return g



class Prog:
    def __init__(self, stages=("aa", "ffn1", "ab", "ffn2")):
        self.stages = stages
        self.nc = bass.Bass("TRN2", target_bir_lowering=False,
                            detect_race_conditions=False)
        self.cnt = {}
        self.sems = {}
        self._mgrs = []
        self.dram = {}
        self._build_tensors()

    def _enter(self, mgr):
        self._mgrs.append(mgr)
        return mgr.__enter__()

    def din(self, name, shape, dtype=F32):
        ap = self.nc.dram_tensor(name, list(shape), dtype, kind="ExternalInput").ap()
        self.dram[name] = ap
        return ap

    def sbuf(self, name, shape, dtype):
        return self._enter(self.nc.sbuf_tensor(name, list(shape), dtype))

    def psum(self, name, shape, dtype=F32):
        return self._enter(self.nc.psum_tensor(name, list(shape), dtype))

    def close(self):
        for m in reversed(self._mgrs):
            m.__exit__(None, None, None)

    def inc(self, name, instr, n=1):
        instr.then_inc(self.sems[name], n)
        self.cnt[name] += n
        return self.cnt[name]

    def w(self, eng, name, val=None):
        v = self.cnt[name] if val is None else val
        if v > 0:
            eng.wait_ge(self.sems[name], v)

    def barrier(self, block, skip=()):
        snap = dict(self.cnt)
        for engname in ENGS:
            sec = getattr(block, engname)

            @sec
            def _(e, snap=snap):
                for s in SEMS:
                    if snap[s] and s not in skip:
                        e.wait_ge(self.sems[s], snap[s])

    def _build_tensors(self):
        nc = self.nc
        self.din("xin", (D, T))
        self.dram["out"] = nc.dram_tensor("out", [D, T], F32, kind="ExternalOutput").ap()
        self.dram["xres"] = nc.dram_tensor("xres", [D, T], F32, kind="Internal").ap()
        self.dram["ures"] = nc.dram_tensor("ures", [D, T], F32, kind="Internal").ap()

        self.din("ffn_w12", (32, 128, 2048))
        for nm in ("ffn_h1", "ffn_h2"):
            self.din(nm, (16, 128, 32 * 256))
        self.din("ffn_w3", (4, 128, 32 * 256))
        for nm in ("ffn_s1", "ffn_s2", "ffn_h1s", "ffn_h2s"):
            self.din(nm, (128, 32))
        self.din("ffn_w3s", (128, 8))

        for nm in ("aa_W_r", "aa_W_k", "aa_W_v", "aa_W_g", "aa_W_o"):
            self.din(nm, (4, 128, 8 * 256))
        self.din("aa_W_v_r", (8, 128, 1024))
        self.din("aa_maa_w1", (1, 128, 8 * 160))
        self.din("aa_w2p1", (128, 1024))
        self.din("aa_w2p2", (64, 1024))
        self.din("aa_dec_w1", (1, 128, 8 * 64))
        self.din("aa_dec_w2", (64, 1024))
        self.din("aa_bd2", (128, 2), BF16)
        self.din("aa_ex2", (2, 128))
        for nm in ("aa_time_maa_x", "aa_time_maa_w", "aa_time_maa_k", "aa_time_maa_v",
                   "aa_time_maa_r", "aa_time_maa_g", "aa_time_decay",
                   "aa_ln_gamma", "aa_ln_beta"):
            self.din(nm, (128, NCT))
        self.din("aa_Bu", (NCT * 128, 128))

        self.din("ab_g_wq_r", (8, 128, 1024))
        self.din("ab_g_kvd", (1, 128, 8 * 256))
        self.din("ab_g_kup_r", (2, 128, 256))
        self.din("ab_g_vup_r", (2, 128, 256))
        self.din("ab_g_wo", (2, 128, 8 * 512))
        self.din("ab_l_wq_r", (8, 128, 1024))
        self.din("ab_l_wk_r", (8, 128, 256))
        self.din("ab_l_wv_r", (8, 128, 256))
        self.din("ab_l_wo", (2, 128, 8 * 512))
        self.din("ab_norm1", (128, NCT))
        self.din("ab_norm2", (128, NCT))

        self.din("identity128", (128, 128), BF16)
        self.din("ones_bf", (128, 1), BF16)
        self.din("ones_f32", (128, 1), F32)
        self.din("onesrow_f32", (1, 128), F32)
        self.din("wkv_mask", (128, CHUNK), BF16)
        self.din("gmask", (4 * 128, 512), BF16)
        self.din("lmask", (6 * 128, 512), BF16)
        for nm in ("cos_g", "sin_g", "cos_l", "sin_l"):
            self.din(nm, (128, NTT * 32))

        for s in SEMS:
            self.sems[s] = self._enter(self.nc.semaphore(s))
            self.cnt[s] = 0

        sb = self.sbuf
        self.id128 = sb("id128", (128, 128), BF16)
        self.ones_b = sb("ones_b", (128, 1), BF16)
        self.ones_f = sb("ones_f", (128, 1), F32)
        self.ones_row = sb("ones_row", (1, 128), F32)
        self.AR0 = sb("AR0", (128, 32768), BF16)
        self.AR1 = sb("AR1", (128, 32768), BF16)
        self.WS = [sb("WS0", (128, 8192), BF16),
                   sb("WS1", (128, 8192), BF16)]
        self.XN = sb("XN", (128, 8, 1024), BF16)
        self.XG = sb("XG", (128, 8, 512), F32)
        self.ROWS = sb("ROWS", (1, 1536), F32)
        self.EV = [sb(f"EV{i}", (128, 512), BF16) for i in range(4)]
        self.TMPF = [sb(f"TMPF{i}", (128, 512), F32) for i in range(2)]
        self.EPSC = sb("EPSC", (128, 4), F32)
        self.SCA = sb("SCA", (128, 32), F32)
        self.SCB = sb("SCB", (128, 32), F32)
        self.SCC = sb("SCC", (128, 8), F32)

        self.PS = [self.psum(f"ps{i}", (128, 512), F32) for i in range(4)]
        self.PSB = [self.psum(f"psb{i}", (128, 512), F32) for i in range(4)]

    def ar_f32(self, which, off_bytes, shape):
        ar = (self.AR0, self.AR1)[which]
        n = int(np.prod(shape))
        v = ar[:, off_bytes // 2: off_bytes // 2 + 2 * n].bitcast(F32)
        return self._shape_view(v, shape)

    def ar_bf(self, which, off_bytes, shape):
        ar = (self.AR0, self.AR1)[which]
        n = int(np.prod(shape))
        v = ar[:, off_bytes // 2: off_bytes // 2 + n]
        return self._shape_view(v, shape)

    @staticmethod
    def _shape_view(v, shape):
        if len(shape) == 1:
            return v
        if len(shape) == 2:
            return v.rearrange("p (a b) -> p a b", a=shape[0], b=shape[1])
        if len(shape) == 3:
            return v.rearrange("p (a b c) -> p a b c", a=shape[0], b=shape[1], c=shape[2])
        return v.rearrange("p (a b c d) -> p a b c d", a=shape[0], b=shape[1],
                           c=shape[2], d=shape[3])

    def emit_consts(self, block):
        P = self

        @block.vector
        def _(v):
            v.memset(P.EPSC[:, 0:1], LN_EPS)
            v.memset(P.EPSC[:, 1:2], RMS_EPS)
            ins = v.memset(P.EPSC[:, 2:3], GN_EPS)
            P.inc("DV", ins)

        @block.gpsimd
        def _(g):
            P.inc("WD", g.dma_start(out=P.id128[:], in_=P.dram["identity128"][:]), 16)
            P.inc("WD", g.dma_start(out=P.ones_b[:], in_=P.dram["ones_bf"][:]), 16)
            P.inc("WD", g.dma_start(out=P.ones_f[:], in_=P.dram["ones_f32"][:]), 16)
            P.inc("WD", g.dma_start(out=P.ones_row[:], in_=P.dram["onesrow_f32"][:]), 16)

    def emit_ln_group(self, block, rin, g0, gw, xn_view, *, rms=False, normw_ap=None,
                      eps_col=0, xg=None):
        P = self
        xg = xg if xg is not None else self.XG

        @block.sync
        def _(sy):
            P.w(sy, "DV")
            src = rin[:, g0:g0 + gw].rearrange("(ct p) c -> p ct c", p=128)
            ins = sy.dma_start(out=xg[:, :, 0:gw], in_=src)
            P.inc("XD", ins, 16)
        xd = P.cnt["XD"]

        te0 = P.cnt["TE"]
        ac0 = P.cnt["AC"]

        @block.scalar
        def _(sc):
            P.w(sc, "XD", xd)
            for ct in range(NCT):
                if ct >= 2:
                    P.w(sc, "TE", te0 + (ct - 1))
                ins = sc.activation(P.TMPF[ct % 2][:, 0:gw], xg[:, ct, 0:gw], AF.Square)
                P.inc("AC", ins)

        @block.tensor
        def _(te):
            P.w(te, "XD", xd)
            P.w(te, "DV")
            if not rms:
                for ct in range(NCT):
                    te.matmul(P.PSB[2][0:1, 0:gw], P.ones_f[:], xg[:, ct, 0:gw],
                              start=(ct == 0), stop=(ct == NCT - 1))
            for ct in range(NCT):
                P.w(te, "AC", ac0 + ct + 1)
                ins = te.matmul(P.PSB[3][0:1, 0:gw], P.ones_f[:],
                                P.TMPF[ct % 2][:, 0:gw],
                                start=(ct == 0), stop=(ct == NCT - 1))
                P.inc("TE", ins)
        tes = P.cnt["TE"]

        @block.vector
        def _(v):
            P.w(v, "TE", tes)
            P.w(v, "PO")
            R = P.ROWS
            ins = v.tensor_scalar_mul(R[0:1, 512:512 + gw], P.PSB[3][0:1, 0:gw],
                                      1.0 / D)
            if not rms:
                v.tensor_scalar_mul(R[0:1, 0:gw], P.PSB[2][0:1, 0:gw], 1.0 / D)
                v.tensor_tensor(R[0:1, 1024:1024 + gw], R[0:1, 0:gw], R[0:1, 0:gw], ALU.mult)
                ins = v.tensor_sub(R[0:1, 512:512 + gw], R[0:1, 512:512 + gw],
                                   R[0:1, 1024:1024 + gw])
            P.inc("DV", ins)
        dvs = P.cnt["DV"]

        @block.scalar
        def _(sc):
            P.w(sc, "DV", dvs)
            R = P.ROWS
            sc.activation(R[0:1, 512:512 + gw], R[0:1, 512:512 + gw], AF.Ln,
                          bias=P.EPSC[0:1, eps_col:eps_col + 1])
            ins = sc.activation(R[0:1, 512:512 + gw], R[0:1, 512:512 + gw], AF.Exp, scale=-0.5)
            P.inc("AC", ins)
        ac2 = P.cnt["AC"]

        @block.tensor
        def _(te):
            P.w(te, "AC", ac2)
            P.w(te, "DV")
            ins = te.matmul(P.PSB[1][:, 0:gw], P.ones_row[:], P.ROWS[0:1, 512:512 + gw],
                            start=True, stop=True)
            if not rms:
                ins = te.matmul(P.PSB[0][:, 0:gw], P.ones_row[:], P.ROWS[0:1, 0:gw],
                                start=True, stop=True)
            P.inc("TE", ins)
        bc_done = P.cnt["TE"]

        @block.vector
        def _(v):
            P.w(v, "TE", bc_done)
            for ct in range(NCT):
                if rms:
                    ins = v.scalar_tensor_tensor(xn_view[:, ct, 0:gw], xg[:, ct, 0:gw],
                                                 normw_ap[:, ct:ct + 1], P.PSB[1][:, 0:gw],
                                                 ALU.mult, ALU.mult)
                else:
                    v.tensor_sub(P.TMPF[ct % 2][:, 0:gw], xg[:, ct, 0:gw], P.PSB[0][:, 0:gw])
                    ins = v.tensor_tensor(xn_view[:, ct, 0:gw], P.TMPF[ct % 2][:, 0:gw],
                                          P.PSB[1][:, 0:gw], ALU.mult)
            P.inc("DV", ins)

    def emit_ffn(self, block, rin, rout, half_range=(0, 1)):
        P = self
        HB0 = self.ar_bf(0, 0, (32, 1024))
        HB1 = self.ar_bf(1, 0, (32, 1024))

        @block.sync
        def _(sy):
            P.w(sy, "AC")
            ins = sy.dma_start(out=P.SCC[:], in_=P.dram["ffn_w3s"][:])
            P.inc("XD", ins, 16)

        for half in half_range:
            t0 = half * 1024
            self.barrier(block)

            @block.sync
            def _(sy):
                P.inc("XD", sy.dma_start(out=P.SCA[:], in_=P.dram["ffn_s1"][:]), 16)
                P.inc("XD", sy.dma_start(out=P.SCB[:], in_=P.dram["ffn_s2"][:]), 16)
            scl_ab = P.cnt["XD"]
            for g in range(2):
                self.emit_ln_group(block, rin, t0 + g * 512, 512,
                                   self.XN[:, :, g * 512:(g + 1) * 512])
            xn_ready = P.cnt["DV"]

            wd0, te0, ac0, dv0 = (P.cnt[s] for s in ("WD", "TE", "AC", "DV"))
            items = [(s, g) for s in range(32) for g in range(2)]

            @block.gpsimd
            def _(g_):
                for s in range(32):
                    if s >= 2:
                        P.w(g_, "TE", te0 + 2 * (s - 1))
                    ins = g_.dma_start(out=P.WS[s % 2][:, 0:2048], in_=P.dram["ffn_w12"][s])
                    P.inc("WD", ins, 16)

            @block.tensor
            def _(te):
                P.w(te, "DV", xn_ready)
                for idx, (s, g) in enumerate(items):
                    ws = P.WS[s % 2]
                    if g == 0:
                        P.w(te, "WD", wd0 + 16 * (s + 1))
                    pa = P.PS[2 * (idx % 2)]
                    pb = P.PS[2 * (idx % 2) + 1]
                    if idx >= 2:
                        P.w(te, "AC", ac0 + 2 * (idx - 1))
                    for kt in range(8):
                        te.matmul(pa[:, 0:512], ws[:, kt * 128:(kt + 1) * 128],
                                  P.XN[:, kt, g * 512:(g + 1) * 512],
                                  start=(kt == 0), stop=(kt == 7))
                    for kt in range(8):
                        ins = te.matmul(pb[:, 0:512], ws[:, 1024 + kt * 128:1024 + (kt + 1) * 128],
                                        P.XN[:, kt, g * 512:(g + 1) * 512],
                                        start=(kt == 0), stop=(kt == 7))
                    P.inc("TE", ins)

            @block.scalar
            def _(sc):
                P.w(sc, "XD", scl_ab)
                for idx, (s, g) in enumerate(items):
                    P.w(sc, "TE", te0 + idx + 1)
                    if idx >= 2:
                        P.w(sc, "DV", dv0 + (idx - 1))
                    pa = P.PS[2 * (idx % 2)]
                    pb = P.PS[2 * (idx % 2) + 1]
                    e0, e1 = P.EV[2 * (idx % 2)], P.EV[2 * (idx % 2) + 1]
                    sc.activation(e0[:], pa[:, 0:512], AF.Silu, scale=P.SCA[:, s:s + 1])
                    ins = sc.activation(e1[:], pb[:, 0:512], AF.Copy, scale=P.SCB[:, s:s + 1])
                    P.inc("AC", ins, 2)

            @block.vector
            def _(v):
                for idx, (s, g) in enumerate(items):
                    P.w(v, "AC", ac0 + 2 * (idx + 1))
                    e0, e1 = P.EV[2 * (idx % 2)], P.EV[2 * (idx % 2) + 1]
                    ins = v.tensor_tensor(HB0[:, s, g * 512:(g + 1) * 512], e0[:], e1[:],
                                          ALU.mult)
                    P.inc("DV", ins)

            for phs, (wname, sname, src, dst, sc_ap) in enumerate(
                    (("ffn_h1", "ffn_h1s", HB0, HB1, self.SCA),
                     ("ffn_h2", "ffn_h2s", HB1, HB0, self.SCB))):
                wd0, te0, ac0 = (P.cnt[s] for s in ("WD", "TE", "AC"))
                dv_in = P.cnt["DV"]

                @block.sync
                def _(sy, sname=sname, sc_ap=sc_ap):
                    P.w(sy, "AC")
                    ins = sy.dma_start(out=sc_ap[:], in_=P.dram[sname][:])
                    P.inc("XD", ins, 16)
                scl = P.cnt["XD"]

                @block.gpsimd
                def _(g_, wname=wname, te0=te0):
                    for s in range(16):
                        if s >= 2:
                            P.w(g_, "TE", te0 + 4 * (s - 1))
                        ins = g_.dma_start(out=P.WS[s % 2][:], in_=P.dram[wname][s])
                        P.inc("WD", ins, 16)

                items = [(s, m2, g) for s in range(16) for m2 in range(2) for g in range(2)]

                @block.tensor
                def _(te, items=items, wd0=wd0, ac0=ac0, dv_in=dv_in, src=src):
                    P.w(te, "DV", dv_in)
                    for idx, (s, m2, g) in enumerate(items):
                        ws = P.WS[s % 2]
                        if m2 == 0 and g == 0:
                            P.w(te, "WD", wd0 + 16 * (s + 1))
                        pa = P.PS[idx % 3]
                        if idx >= 3:
                            P.w(te, "AC", ac0 + (idx - 2))
                        for kt in range(32):
                            ins = te.matmul(pa[:, 0:512],
                                            ws[:, kt * 256 + m2 * 128:kt * 256 + m2 * 128 + 128],
                                            src[:, kt, g * 512:(g + 1) * 512],
                                            start=(kt == 0), stop=(kt == 31))
                        P.inc("TE", ins)

                @block.scalar
                def _(sc, items=items, te0=te0, scl=scl, dst=dst, sc_ap=sc_ap):
                    P.w(sc, "XD", scl)
                    for idx, (s, m2, g) in enumerate(items):
                        m_abs = s * 2 + m2
                        P.w(sc, "TE", te0 + idx + 1)
                        pa = P.PS[idx % 3]
                        ins = sc.activation(dst[:, m_abs, g * 512:(g + 1) * 512], pa[:, 0:512],
                                            AF.Gelu, scale=sc_ap[:, m_abs:m_abs + 1])
                        P.inc("AC", ins)

                @block.vector
                def _(v):
                    P.w(v, "AC")
                    ins = v.nop()
                    P.inc("DV", ins)

            wd0, te0 = P.cnt["WD"], P.cnt["TE"]
            dv_in = P.cnt["DV"]
            XG2 = self.ar_f32(1, 0, (8, 512))

            @block.gpsimd
            def _(g_):
                for s in range(4):
                    if s >= 2:
                        P.w(g_, "TE", te0 + 4 * (s - 1))
                    ins = g_.dma_start(out=P.WS[s % 2][:], in_=P.dram["ffn_w3"][s])
                    P.inc("WD", ins, 16)

            @block.sync
            def _(sy):
                P.w(sy, "DV", dv_in)
                src0 = rin[:, t0:t0 + 512].rearrange("(ct p) c -> p ct c", p=128)
                ins = sy.dma_start(out=P.XG[:, :, :], in_=src0)
                P.inc("XD", ins, 16)
                src1 = rin[:, t0 + 512:t0 + 1024].rearrange("(ct p) c -> p ct c", p=128)
                ins = sy.dma_start(out=XG2[:, :, :], in_=src1)
                P.inc("XD", ins, 16)
            xd_d = P.cnt["XD"]

            items = [(s, m2, g) for s in range(4) for m2 in range(2) for g in range(2)]

            @block.tensor
            def _(te):
                P.w(te, "DV", dv_in)
                for idx, (s, m2, g) in enumerate(items):
                    ws = P.WS[s % 2]
                    if m2 == 0 and g == 0:
                        P.w(te, "WD", wd0 + 16 * (s + 1))
                    pa = P.PS[idx % 3]
                    if idx >= 3:
                        P.w(te, "DV", dv_in + (idx - 2))
                    for kt in range(32):
                        ins = te.matmul(pa[:, 0:512],
                                        ws[:, kt * 256 + m2 * 128:kt * 256 + m2 * 128 + 128],
                                        HB0[:, kt, g * 512:(g + 1) * 512],
                                        start=(kt == 0), stop=(kt == 31))
                    P.inc("TE", ins)

            @block.vector
            def _(v):
                P.w(v, "XD", xd_d)
                for idx, (s, m2, g) in enumerate(items):
                    ct = s * 2 + m2
                    P.w(v, "TE", te0 + idx + 1)
                    pa = P.PS[idx % 3]
                    xgb = P.XG if g == 0 else XG2
                    ins = v.scalar_tensor_tensor(xgb[:, ct, 0:512], pa[:, 0:512],
                                                 P.SCC[:, ct:ct + 1], xgb[:, ct, 0:512],
                                                 ALU.mult, ALU.add)
                    P.inc("DV", ins)
            dvw = P.cnt["DV"]

            @block.sync
            def _(sy):
                P.w(sy, "DV", dvw)
                dst0 = rout[:, t0:t0 + 512].rearrange("(ct p) c -> p ct c", p=128)
                ins = sy.dma_start(out=dst0, in_=P.XG[:, :, :])
                P.inc("OD", ins, 16)
                dst1 = rout[:, t0 + 512:t0 + 1024].rearrange("(ct p) c -> p ct c", p=128)
                ins = sy.dma_start(out=dst1, in_=XG2[:, :, :])
                P.inc("OD", ins, 16)
                P.w(sy, "OD")



    def emit_attna(self, block, rin, rout):
        P = self
        GW = 256
        NG = getattr(self, 'aa_ng', T // GW)
        XN1 = self.ar_f32(0, 0, (8, GW))
        XX = self.ar_f32(0, 8192, (8, GW))
        MIX = [self.ar_bf(0, 16384, (8, GW)), self.ar_bf(0, 20480, (8, GW))]
        CBUF = self.ar_bf(0, 24576, (8, GW))
        EWF = self.ar_f32(0, 28672, (8, GW))
        LA = self.ar_f32(0, 36864, (8, GW))
        EP = self.ar_f32(0, 45056, (8, GW))
        RT = self.ar_bf(0, 53248, (8, GW))
        KT = self.ar_bf(0, 57344, (8, GW))
        MAA1 = self.ar_bf(0, 61440, (256,))
        MAA2 = self.ar_bf(0, 61952, (256,))
        PREVC = self.ar_f32(0, 62464, (8, 1))
        RR = self.ar_bf(1, 0, (8, GW))
        RK = self.ar_bf(1, 4096, (8, GW))
        KHF = self.ar_bf(1, 8192, (8, GW))
        KHT = self.ar_bf(1, 12288, (2, 1024))
        VTM = self.ar_bf(1, 16384, (2, 1024))
        VF = self.ar_bf(1, 20480, (8, GW))
        GF = self.ar_bf(1, 24576, (8, GW))
        DB = self.ar_bf(1, 28672, (8, GW))
        YSB = self.ar_bf(1, 32768, (8, GW))
        YNG = self.ar_bf(1, 36864, (8, GW))
        S = self.ar_f32(1, 40960, (8, 64))
        ALST = self.ar_f32(1, 43008, (8, 4))
        W1M = self.ar_bf(1, 43136, (8, 160))
        DW1 = self.ar_bf(1, 45696, (8, 64))
        DW2 = self.ar_bf(1, 46720, (1024,))
        W2P1 = self.ar_bf(1, 48768, (1024,))
        W2P2 = self.ar_bf(1, 50816, (1024,))
        BUS = self.ar_bf(1, 52864, (8, 128))
        RW2 = self.ar_f32(1, 54912, (512,))
        TMV = self.ar_f32(1, 56960, (7, 8))
        GB = self.ar_f32(1, 57184, (2, 8))
        WKM = self.ar_bf(1, 57248, (64,))
        BD2 = self.ar_bf(1, 57376, (2,))
        EX2 = self.ar_f32(1, 57408, (128,))
        YSQ = self.ar_bf(1, 57920, (8, GW))
        PTSB = self.ar_bf(1, 62016, (2, 512))
        SBF = self.ar_bf(1, 64064, (8, 64))
        ZROW = self.ar_bf(0, 62496, (640,))
        PT0 = self.PS[3][:, 0:64].bitcast(BF16)
        XNW = self.XN[:, :, :].rearrange("p a b -> p (a b)")
        SL = [self.WS[0], self.WS[1], XNW]
        WMATS = ["aa_W_r", "aa_W_k", "aa_W_v_r", "aa_W_v", "aa_W_g", "aa_W_o"]
        wd_after = {}
        od_after = {}
        dv_resid = {}

        def emit_wload(block, J, te_wait):
            nm = WMATS[J % 6]
            slot = SL[J % 3]
            a = 8 if nm == "aa_W_v_r" else 4

            @block.gpsimd
            def _(g_, J=J, nm=nm, slot=slot, a=a, te_wait=te_wait):
                P.w(g_, "TE", te_wait)
                ins = g_.dma_start(
                    out=slot[:, 0:8192].rearrange("p (a b) -> p a b", a=a),
                    in_=P.dram[nm].rearrange("a p b -> p a b"))
                wd_after[J] = P.inc("WD", ins, 16)

        self.barrier(block)

        @block.gpsimd
        def _(g_):
            P.w(g_, "OD")
            src0 = rin[:, 0:GW].rearrange("(ct p) c -> p ct c", p=128)
            P.inc("XP", g_.dma_start(out=P.XG[:, :, 0:GW], in_=src0), 16)
        self._aa_xd = {0: P.cnt["XP"]}
        for J0 in range(3):
            emit_wload(block, J0, 0)

        @block.gpsimd
        def _(g_):
            P.inc("WD", g_.dma_start(out=W1M, in_=P.dram["aa_maa_w1"][0]
                                     .rearrange("p (a b) -> p a b", b=160)), 16)
            P.inc("WD", g_.dma_start(out=DW1, in_=P.dram["aa_dec_w1"][0]
                                     .rearrange("p (a b) -> p a b", b=64)), 16)
            P.inc("WD", g_.dma_start(out=DW2[0:64, :], in_=P.dram["aa_dec_w2"][:]), 16)
            P.inc("WD", g_.dma_start(out=W2P1, in_=P.dram["aa_w2p1"][:]), 16)
            P.inc("WD", g_.dma_start(out=W2P2[0:64, :], in_=P.dram["aa_w2p2"][:]), 16)
            P.inc("WD", g_.dma_start(out=BUS, in_=P.dram["aa_Bu"]
                                     .rearrange("(a p) b -> p a b", p=128)), 16)
            for i, nm in enumerate(("aa_time_maa_x", "aa_time_maa_w", "aa_time_maa_k",
                                    "aa_time_maa_v", "aa_time_maa_r", "aa_time_maa_g",
                                    "aa_time_decay")):
                P.inc("WD", g_.dma_start(out=TMV[:, i, :], in_=P.dram[nm][:]), 16)
            P.inc("WD", g_.dma_start(out=GB[:, 0, :], in_=P.dram["aa_ln_gamma"][:]), 16)
            P.inc("WD", g_.dma_start(out=GB[:, 1, :], in_=P.dram["aa_ln_beta"][:]), 16)
            P.inc("WD", g_.dma_start(out=WKM, in_=P.dram["wkv_mask"][:]), 16)
            P.inc("WD", g_.dma_start(out=BD2, in_=P.dram["aa_bd2"][:]), 16)
            P.inc("WD", g_.dma_start(out=EX2[0:2, :], in_=P.dram["aa_ex2"][:]), 16)
        smalls_wd = P.cnt["WD"]

        @block.vector
        def _(v):
            v.memset(S[:], 0.0)
            v.memset(SBF[:], 0.0)
            v.memset(ZROW[0:1, :], 0.0)
            ins = v.memset(PREVC[:], 0.0)
            P.inc("DV", ins)

        self._aa_tes = {}
        self._aa_xxx = {}

        def aa_lnA(g):
            half = (g % 2) * GW
            XGH = P.XG[:, :, half:half + GW]
            xd = self._aa_xd[g]
            te0 = P.cnt["TE"]
            ac0 = P.cnt["AC"]

            @block.scalar
            def _(sc, xd=xd, te0=te0, XGH=XGH):
                P.w(sc, "XP", xd)
                P.w(sc, "DV")
                for ct in range(NCT):
                    if ct >= 2:
                        P.w(sc, "TE", te0 + (ct - 1))
                    P.inc("AC", sc.activation(P.TMPF[ct % 2][:, 0:GW], XGH[:, ct, :],
                                              AF.Square))

            @block.tensor
            def _(te, xd=xd, ac0=ac0):
                P.w(te, "XP", xd)
                P.w(te, "DV")
                for ct in range(NCT):
                    te.matmul(P.PSB[2][0:1, 0:GW], P.ones_f[:], XGH[:, ct, :],
                              start=(ct == 0), stop=(ct == NCT - 1))
                for ct in range(NCT):
                    P.w(te, "AC", ac0 + ct + 1)
                    ins = te.matmul(P.PSB[3][0:1, 0:GW], P.ones_f[:],
                                    P.TMPF[ct % 2][:, 0:GW],
                                    start=(ct == 0), stop=(ct == NCT - 1))
                    P.inc("TE", ins)
            self._aa_tes[g] = P.cnt["TE"]

        def aa_lnB(g):
            half = (g % 2) * GW
            XGH = P.XG[:, :, half:half + GW]
            tes = self._aa_tes[g]

            @block.vector
            def _(v, tes=tes):
                P.w(v, "TE", tes)
                R = P.ROWS
                v.tensor_scalar_mul(R[0:1, 512:512 + GW], P.PSB[3][0:1, 0:GW], 1.0 / D)
                v.tensor_scalar_mul(R[0:1, 0:GW], P.PSB[2][0:1, 0:GW], 1.0 / D)
                v.tensor_tensor(R[0:1, 1024:1024 + GW], R[0:1, 0:GW], R[0:1, 0:GW], ALU.mult)
                ins = v.tensor_sub(R[0:1, 512:512 + GW], R[0:1, 512:512 + GW],
                                   R[0:1, 1024:1024 + GW])
                P.inc("DV", ins)
            dvs = P.cnt["DV"]

            @block.scalar
            def _(sc, dvs=dvs):
                P.w(sc, "DV", dvs)
                R = P.ROWS
                sc.activation(R[0:1, 512:512 + GW], R[0:1, 512:512 + GW], AF.Ln,
                              bias=P.EPSC[0:1, 0:1])
                ins = sc.activation(R[0:1, 512:512 + GW], R[0:1, 512:512 + GW], AF.Exp,
                                    scale=-0.5)
                P.inc("AC", ins)
            ac2 = P.cnt["AC"]

            @block.tensor
            def _(te, ac2=ac2):
                P.w(te, "AC", ac2)
                P.w(te, "DV")
                te.matmul(P.PSB[1][:, 0:GW], P.ones_row[:], P.ROWS[0:1, 512:512 + GW],
                          start=True, stop=True)
                ins = te.matmul(P.PSB[0][:, 0:GW], P.ones_row[:], P.ROWS[0:1, 0:GW],
                                start=True, stop=True)
                P.inc("TE", ins)
            bcd = P.cnt["TE"]

            @block.vector
            def _(v, bcd=bcd, g=g):
                P.w(v, "TE", bcd)
                if g == 0:
                    P.w(v, "WD", smalls_wd)
                else:
                    v.tensor_copy(PREVC[:, :, 0:1], XN1[:, :, GW - 1:GW])
                for ct in range(NCT):
                    v.tensor_sub(P.TMPF[ct % 2][:, 0:GW], XGH[:, ct, :],
                                 P.PSB[0][:, 0:GW])
                    v.tensor_tensor(XN1[:, ct, :], P.TMPF[ct % 2][:, 0:GW],
                                    P.PSB[1][:, 0:GW], ALU.mult)
                for ct in range(NCT):
                    v.tensor_sub(XX[:, ct, 1:GW], XN1[:, ct, 0:GW - 1], XN1[:, ct, 1:GW])
                    v.tensor_sub(XX[:, ct, 0:1], PREVC[:, ct, :], XN1[:, ct, 0:1])
                for ct in range(NCT):
                    ins = v.scalar_tensor_tensor(MIX[0][:, ct, :], XX[:, ct, :],
                                                 TMV[:, 0, ct:ct + 1], XN1[:, ct, :],
                                                 ALU.mult, ALU.add)
                P.inc("DV", ins)
            self._aa_xxx[g] = P.cnt["DV"]

        for g in range(NG):
            g0 = g * GW
            half = (g % 2) * GW
            XGH = P.XG[:, :, half:half + GW]
            if g == 0:
                aa_lnA(0)
                aa_lnB(0)
            xxx_dv = self._aa_xxx[g]

            te0 = P.cnt["TE"]

            @block.tensor
            def _(te, xxx_dv=xxx_dv, g=g):
                P.w(te, "DV", xxx_dv)
                P.w(te, "AC")
                if g == 0:
                    P.w(te, "WD", smalls_wd)
                for mt, (mw, pb) in enumerate(((64, 0), (64, 1), (32, 2))):
                    for kt in range(NCT):
                        ins = te.matmul(P.PS[pb][0:mw, 0:GW],
                                        W1M[:, kt, mt * 64:mt * 64 + mw],
                                        MIX[0][:, kt, :],
                                        start=(kt == 0), stop=(kt == NCT - 1))
                    P.inc("TE", ins)
            m1_te = P.cnt["TE"]

            @block.scalar
            def _(sc, m1_te=m1_te, g=g):
                P.w(sc, "TE", m1_te)
                if g == 0:
                    P.w(sc, "WD", smalls_wd)
                sc.activation(MAA1[0:64, :], P.PS[0][0:64, 0:GW], AF.Tanh)
                sc.activation(MAA1[64:96, :], P.PS[1][0:32, 0:GW], AF.Tanh)
                sc.activation(MAA2[0:32, :], P.PS[1][32:64, 0:GW], AF.Tanh)
                ins = sc.activation(MAA2[32:64, :], P.PS[2][0:32, 0:GW], AF.Tanh)
                P.inc("AC", ins)
            m1_ac = P.cnt["AC"]

            MSLOT = [MIX[0], MIX[1], CBUF, YSB, YNG]
            forder = [(0, 1, 0), (3, 4, 1), (1, 2, 2), (2, 3, 3), (4, 5, 4)]
            m2cfg = []
            for f, tmcol, slot in forder:
                if f == 0:
                    wt, mb = W2P1, 0
                elif f == 1:
                    wt, mb = W2P1, 32
                elif f == 2:
                    wt, mb = W2P1, 64
                elif f == 3:
                    wt, mb = W2P2, 0
                else:
                    wt, mb = W2P2, 32
                m2cfg.append((f, tmcol, MSLOT[slot], wt, mb, MAA1 if f < 3 else MAA2))
            m2_te0 = P.cnt["TE"]
            m2_dv0 = P.cnt["DV"]

            @block.tensor
            def _(te, m2cfg=m2cfg, m2_te0=m2_te0, m2_dv0=m2_dv0):
                P.w(te, "AC")
                P.w(te, "DV")
                idx = 0
                for f, tmcol, slot, wt, mb, mrow in m2cfg:
                    for mt in range(NCT):
                        if idx >= 3:
                            P.w(te, "DV", m2_dv0 + 3 * (idx - 2))
                        ins = te.matmul(P.PS[idx % 3][:, 0:GW],
                                        wt[mb:mb + 32, mt * 128:(mt + 1) * 128],
                                        mrow[mb:mb + 32, :],
                                        start=True, stop=True)
                        P.inc("TE", ins)
                        idx += 1

            @block.vector
            def _(v, m2cfg=m2cfg, m2_te0=m2_te0):
                idx = 0
                for f, tmcol, slot, wt, mb, mrow in m2cfg:
                    for mt in range(NCT):
                        P.w(v, "TE", m2_te0 + idx + 1)
                        v.tensor_scalar(P.TMPF[idx % 2][:, 0:GW], P.PS[idx % 3][:, 0:GW],
                                        TMV[:, tmcol, mt:mt + 1], None, ALU.add)
                        v.tensor_tensor(P.TMPF[idx % 2][:, 0:GW], P.TMPF[idx % 2][:, 0:GW],
                                        XX[:, mt, :], ALU.mult)
                        ins = v.tensor_add(slot[:, mt, :], P.TMPF[idx % 2][:, 0:GW],
                                           XN1[:, mt, :])
                        P.inc("DV", ins, 3)
                        idx += 1
            mixall_dv = P.cnt["DV"]

            mixw_dv = mixall_dv
            te0 = P.cnt["TE"]

            @block.tensor
            def _(te, mixw_dv=mixw_dv, m2_dv0=m2_dv0):
                P.w(te, "DV", m2_dv0 + 24)
                for kt in range(NCT):
                    ins = te.matmul(P.PSB[1][0:64, 0:GW], DW1[:, kt, :], MIX[0][:, kt, :],
                                    start=(kt == 0), stop=(kt == NCT - 1))
                P.inc("TE", ins)
            z_te = P.cnt["TE"]

            @block.scalar
            def _(sc, z_te=z_te):
                P.w(sc, "TE", z_te)
                ins = sc.activation(P.EV[0][0:64, 0:GW], P.PSB[1][0:64, 0:GW], AF.Tanh)
                P.inc("AC", ins)
            z_ac = P.cnt["AC"]
            te0 = P.cnt["TE"]

            @block.tensor
            def _(te, z_ac=z_ac):
                P.w(te, "AC", z_ac)
                for mt in range(NCT):
                    if mt >= 3:
                        P.w(te, "AC", z_ac + (mt - 2))
                    ins = te.matmul(P.PSB[1 + mt % 3][:, 0:GW],
                                    DW2[0:64, mt * 128:(mt + 1) * 128],
                                    P.EV[0][0:64, 0:GW], start=True, stop=True)
                    P.inc("TE", ins)

            @block.scalar
            def _(sc, te0=te0):
                for mt in range(NCT):
                    P.w(sc, "TE", te0 + mt + 1)
                    ins = sc.activation(EWF[:, mt, :], P.PSB[1 + mt % 3][:, 0:GW], AF.Exp,
                                        bias=TMV[:, 6, mt:mt + 1])
                    P.inc("AC", ins)
            ew_ac = P.cnt["AC"]

            @block.vector
            def _(v, ew_ac=ew_ac):
                P.w(v, "AC", ew_ac)
                for ct in range(NCT):
                    for c in range(4):
                        ins = v.tensor_tensor_scan(LA[:, ct, c * 64:(c + 1) * 64],
                                                   EWF[:, ct, c * 64:(c + 1) * 64],
                                                   EWF[:, ct, c * 64:(c + 1) * 64],
                                                   0.0, ALU.add, ALU.bypass)
                ep4 = self.ar_f32(0, 45056, (8, 4, 64))
                ins = v.memset(ep4[:, :, :, 0:1], 1.0)
                P.inc("DV", ins)
            la_dv = P.cnt["DV"]

            @block.scalar
            def _(sc, la_dv=la_dv):
                P.w(sc, "DV", la_dv)
                la4 = self.ar_f32(0, 36864, (8, 4, 64))
                ep4 = self.ar_f32(0, 45056, (8, 4, 64))
                for ct in range(NCT):
                    sc.activation(EWF[:, ct, :], LA[:, ct, :], AF.Exp)
                    sc.activation(ep4[:, ct, :, 1:64], la4[:, ct, :, 0:63], AF.Exp,
                                  scale=-1.0)
                    ins = sc.activation(ALST[:, ct, :], la4[:, ct, :, 63:64], AF.Exp,
                                        scale=-1.0)
                    P.inc("AC", ins)
            ef_ac = P.cnt["AC"]
            EN = EWF

            mixr_dv = m2_dv0 + 48
            te0 = P.cnt["TE"]

            rev_base = P.cnt["DV"]

            @block.tensor
            def _(te, mixr_dv=mixr_dv, g=g, rev_base=rev_base):
                P.w(te, "DV", mixr_dv)
                P.w(te, "AC")
                P.w(te, "WD", wd_after[6 * g + 0])
                for mt in range(8):
                    s, mj = mt // 2, mt % 2
                    if mt >= 3:
                        P.w(te, "DV", rev_base + 2 * (mt - 2))
                    for kt in range(NCT):
                        ins = te.matmul(P.PS[mt % 3][:, 0:GW],
                                        SL[0][:, s * 2048 + kt * 256 + mj * 128:
                                              s * 2048 + kt * 256 + mj * 128 + 128],
                                        MIX[1][:, kt, :],
                                        start=(kt == 0), stop=(kt == NCT - 1))
                    P.inc("TE", ins)
            emit_wload(block, 6 * g + 3, P.cnt["TE"])

            @block.vector
            def _(v, te0=te0, ef_ac=ef_ac):
                P.w(v, "AC", ef_ac)
                for mt in range(8):
                    P.w(v, "TE", te0 + mt + 1)
                    v.tensor_tensor(RT[:, mt, :], P.PS[mt % 3][:, 0:GW], EP[:, mt, :],
                                    ALU.mult)
                    ins = v.tensor_copy(RR[:, mt, :], P.PS[mt % 3][:, 0:GW])
                    P.inc("DV", ins, 2)

            mixk_dv = m2_dv0 + 72
            te0 = P.cnt["TE"]

            @block.tensor
            def _(te, mixk_dv=mixk_dv, g=g):
                P.w(te, "DV")
                P.w(te, "WD", wd_after[6 * g + 1])
                for mt in range(8):
                    s, mj = mt // 2, mt % 2
                    if mt >= 3:
                        P.w(te, "DV", P.cnt["DV"] + 3 * (mt - 2))
                    for kt in range(NCT):
                        ins = te.matmul(P.PS[mt % 3][:, 0:GW],
                                        SL[1][:, s * 2048 + kt * 256 + mj * 128:
                                              s * 2048 + kt * 256 + mj * 128 + 128],
                                        CBUF[:, kt, :],
                                        start=(kt == 0), stop=(kt == NCT - 1))
                    P.inc("TE", ins)
            emit_wload(block, 6 * g + 4, P.cnt["TE"])

            @block.vector
            def _(v, te0=te0):
                for mt in range(8):
                    P.w(v, "TE", te0 + mt + 1)
                    v.tensor_tensor(KT[:, mt, :], P.PS[mt % 3][:, 0:GW], EN[:, mt, :],
                                    ALU.mult)
                    v.tensor_tensor(RK[:, mt, :], RR[:, mt, :], P.PS[mt % 3][:, 0:GW],
                                    ALU.mult)
                    for c in range(4):
                        ins = v.tensor_scalar(KHF[:, mt, c * 64:(c + 1) * 64],
                                              KT[:, mt, c * 64:(c + 1) * 64],
                                              ALST[:, mt, c:c + 1], None, ALU.mult)
                    P.inc("DV", ins, 3)
            khf_dv = P.cnt["DV"]

            PT1 = P.PSB[0][:, 0:64].bitcast(BF16)
            PT2 = P.PSB[2][:, 0:64].bitcast(BF16)
            PT3 = P.PSB[3][:, 0:64].bitcast(BF16)
            PTS4 = [PT0, PT1, PT2, PT3]
            items_t = [(pr, mt) for pr in range(2) for mt in range(8)]
            for j in range(0, 16, 4):
                quad = items_t[j:j + 4]

                @block.tensor
                def _(te, quad=quad):
                    P.w(te, "DV")
                    P.w(te, "AC")
                    for qi, (pr, mt) in enumerate(quad):
                        ins = te.transpose(PTS4[qi], KHF[:, mt, pr * 128:(pr + 1) * 128],
                                           P.id128[:])
                    P.inc("TE", ins)
                trd = P.cnt["TE"]

                @block.vector
                def _(v, quad=quad, trd=trd):
                    P.w(v, "TE", trd)
                    for qi, (pr, mt) in enumerate(quad):
                        ins = v.tensor_copy(KHT[:, pr, mt * 128:(mt + 1) * 128], PTS4[qi][:])
                    P.inc("DV", ins)

            mixv_dv = m2_dv0 + 96
            te0 = P.cnt["TE"]

            vac0 = P.cnt["AC"]

            @block.tensor
            def _(te, mixv_dv=mixv_dv, vac0=vac0, g=g):
                P.w(te, "DV")
                P.w(te, "AC", vac0)
                P.w(te, "WD", wd_after[6 * g + 2])
                for tsub in range(2):
                    for n2 in range(2):
                        i = tsub * 2 + n2
                        if i >= 3:
                            P.w(te, "AC", vac0 + (i - 2))
                        for kt in range(NCT):
                            ins = te.matmul(P.PS[i % 3][:, 0:512],
                                            YSB[:, kt, tsub * 128:(tsub + 1) * 128],
                                            SL[2][:, kt * 1024 + n2 * 512:
                                                  kt * 1024 + (n2 + 1) * 512],
                                            start=(kt == 0), stop=(kt == NCT - 1))
                        P.inc("TE", ins)
            emit_wload(block, 6 * g + 5, P.cnt["TE"])

            @block.scalar
            def _(sc, te0=te0):
                for tsub in range(2):
                    for n2 in range(2):
                        i = tsub * 2 + n2
                        P.w(sc, "TE", te0 + i + 1)
                        ins = sc.copy(VTM[:, tsub, n2 * 512:(n2 + 1) * 512],
                                      P.PS[i % 3][:, 0:512])
                        P.inc("AC", ins)
            vtm_ac = P.cnt["AC"]

            vte0 = P.cnt["TE"]
            vac1 = P.cnt["AC"]

            @block.tensor
            def _(te, vte0=vte0, vac1=vac1, vtm_ac=vtm_ac, g=g):
                P.w(te, "WD", wd_after[6 * g + 3])
                P.w(te, "AC", vtm_ac)
                for mt in range(8):
                    s, mj = mt // 2, mt % 2
                    if mt >= 3:
                        P.w(te, "AC", vac1 + (mt - 2))
                    for kt in range(NCT):
                        ins = te.matmul(P.PS[mt % 3][:, 0:GW],
                                        SL[0][:, s * 2048 + kt * 256 + mj * 128:
                                              s * 2048 + kt * 256 + mj * 128 + 128],
                                        YSB[:, kt, :],
                                        start=(kt == 0), stop=(kt == NCT - 1))
                    P.inc("TE", ins)
            if 6 * g + 6 < 6 * NG:
                emit_wload(block, 6 * g + 6, P.cnt["TE"])

            @block.scalar
            def _(sc, vte0=vte0):
                for mt in range(8):
                    P.w(sc, "TE", vte0 + mt + 1)
                    ins = sc.copy(VF[:, mt, :], P.PS[mt % 3][:, 0:GW])
                    P.inc("AC", ins)

            mixg_dv = m2_dv0 + 120
            te0 = P.cnt["TE"]
            gac = P.cnt["AC"]

            @block.tensor
            def _(te, mixg_dv=mixg_dv, gac=gac, g=g):
                P.w(te, "DV")
                P.w(te, "AC", gac)
                P.w(te, "WD", wd_after[6 * g + 4])
                for mt in range(8):
                    s, mj = mt // 2, mt % 2
                    if mt >= 3:
                        P.w(te, "AC", gac + (mt - 2))
                    for kt in range(NCT):
                        ins = te.matmul(P.PS[mt % 3][:, 0:GW],
                                        SL[1][:, s * 2048 + kt * 256 + mj * 128:
                                              s * 2048 + kt * 256 + mj * 128 + 128],
                                        YNG[:, kt, :],
                                        start=(kt == 0), stop=(kt == NCT - 1))
                    P.inc("TE", ins)
            if 6 * g + 7 < 6 * NG:
                emit_wload(block, 6 * g + 7, P.cnt["TE"])

            @block.scalar
            def _(sc, te0=te0):
                for mt in range(8):
                    P.w(sc, "TE", te0 + mt + 1)
                    ins = sc.activation(GF[:, mt, :], P.PS[mt % 3][:, 0:GW], AF.Silu)
                    P.inc("AC", ins)

            te0 = P.cnt["TE"]

            dac0 = P.cnt["AC"]

            @block.tensor
            def _(te, khf_dv=khf_dv, dac0=dac0):
                P.w(te, "DV")
                P.w(te, "AC", dac0)
                for ct in range(NCT):
                    if ct >= 3:
                        P.w(te, "AC", dac0 + (ct - 2))
                    ins = te.matmul(P.PS[ct % 3][:, 0:GW], BUS[:, ct, :], RK[:, ct, :],
                                    start=True, stop=True)
                    P.inc("TE", ins)

            @block.scalar
            def _(sc, te0=te0):
                for ct in range(NCT):
                    P.w(sc, "TE", te0 + ct + 1)
                    ins = sc.copy(DB[:, ct, :], P.PS[ct % 3][:, 0:GW])
                    P.inc("AC", ins)
            db_ac = P.cnt["AC"]

            if g + 1 < NG:
                @block.gpsimd
                def _(g_, g=g):
                    g1 = (g + 1) * GW
                    h1 = ((g + 1) % 2) * GW
                    if g >= 1:
                        P.w(g_, "DV", dv_resid[g - 1])
                        P.w(g_, "OD", od_after[g - 1])
                    src = rin[:, g1:g1 + GW].rearrange("(ct p) c -> p ct c", p=128)
                    P.inc("XP", g_.dma_start(out=P.XG[:, :, h1:h1 + GW], in_=src), 16)
                self._aa_xd[g + 1] = P.cnt["XP"]
            xres_xd = self._aa_xd[g]

            for cig in range(4):
                par = (cig % 2) * 64
                dv0 = P.cnt["DV"]
                fill_done = [0]
                gstop_done = [0]

                @block.tensor
                def _(te, cig=cig, par=par, dv0=dv0, db_ac=db_ac,
                      fill_done=fill_done, gstop_done=gstop_done):
                    P.w(te, "DV", dv0)
                    P.w(te, "AC", db_ac)
                    Z = ZROW
                    if cig == 0:
                        for bank in range(4):
                            ins = te.matmul(P.PS[bank][:, 0:512], Z[0:1, 0:128],
                                            Z[0:1, 128:640], start=True, stop=False)
                            P.inc("TE", ins)
                    ins = te.matmul(P.PSB[1][:, 0:512], Z[0:1, 0:128], Z[0:1, 128:640],
                                    start=True, stop=False)
                    P.inc("TE", ins)
                    for h in range(HEADS):
                        ph = (h % 2) * 64
                        bank, col = h // 4, ((h // 2) % 2) * 256 + cig * 64
                        ins = te.matmul(P.PS[bank][ph:ph + 64, col:col + 64],
                                        SBF[ph:ph + 64, h // 2, :],
                                        RT[ph:ph + 64, h // 2, cig * 64:(cig + 1) * 64],
                                        start=False, stop=False)
                        P.inc("TE", ins)
                    for h in range(HEADS):
                        ph = (h % 2) * 64
                        bank = 2 + h % 2
                        first = h < 2
                        last = h >= 14
                        ins = te.matmul(P.PSB[bank][0:64, (h // 2) * 64:(h // 2) * 64 + 64],
                                        KT[ph:ph + 64, h // 2, cig * 64:(cig + 1) * 64],
                                        RT[ph:ph + 64, h // 2, cig * 64:(cig + 1) * 64],
                                        start=first, stop=last)
                        fill_done[0] = P.inc("TE", ins)
                    P.w(te, "DV", dv0 + 2)
                    for h in range(HEADS):
                        ph = (h % 2) * 64
                        bank, col = h // 4, ((h // 2) % 2) * 256 + cig * 64
                        ins = te.matmul(P.PS[bank][ph:ph + 64, col:col + 64],
                                        VTM[par:par + 64, cig // 2, h * 64:(h + 1) * 64],
                                        PTSB[par:par + 64, h % 2, (h // 2) * 64:(h // 2) * 64 + 64],
                                        start=False, stop=False)
                        P.inc("TE", ins)
                        ins = te.matmul(P.PSB[1][ph:ph + 64, (h // 2) * 64:(h // 2) * 64 + 64],
                                        KHT[par:par + 64, cig // 2, h * 64:(h + 1) * 64],
                                        VTM[par:par + 64, cig // 2, h * 64:(h + 1) * 64],
                                        start=False, stop=False)
                        P.inc("TE", ins)
                    ins = te.matmul(P.PSB[1][:, 0:512], Z[0:1, 0:128], Z[0:1, 128:640],
                                    start=False, stop=True)
                    gstop_done[0] = P.inc("TE", ins)
                    if cig == 3:
                        for bank in range(4):
                            ins = te.matmul(P.PS[bank][:, 0:512], Z[0:1, 0:128],
                                            Z[0:1, 128:640], start=False, stop=True)
                            P.inc("TE", ins)

                @block.vector
                def _(v, cig=cig, par=par, fill_done=fill_done, gstop_done=gstop_done):
                    P.w(v, "TE", fill_done[0])
                    wkb = WKM[par:par + 64, :].unsqueeze(1).broadcast_to([64, 8, 64])
                    for sl in range(2):
                        pview = P.PSB[2 + sl][0:64, 0:512].rearrange(
                            "p (a b) -> p a b", a=8)
                        dview = PTSB[par:par + 64, sl, :].rearrange(
                            "p (a b) -> p a b", a=8)
                        ins = v.tensor_tensor(dview, pview, wkb, ALU.mult)
                        P.inc("DV", ins)
                    P.w(v, "TE", gstop_done[0])
                    adec = ALST[:, :, cig:cig + 1].broadcast_to([128, 8, 64])
                    t3 = P.TMPF[0][:, 0:512].rearrange("p (a b) -> p a b", a=8)
                    g3 = P.PSB[1][:, 0:512].rearrange("p (a b) -> p a b", a=8)
                    v.tensor_tensor(t3, S, adec, ALU.mult)
                    v.tensor_tensor(S, t3, g3, ALU.add)
                    ins = v.tensor_copy(SBF, S)
                    P.inc("DV", ins)

            wkv_te = P.cnt["TE"]
            wkv_dv = P.cnt["DV"]

            @block.vector
            def _(v, wkv_te=wkv_te):
                P.w(v, "TE", wkv_te)
                for ct in range(NCT):
                    bank, col = ct // 2, (ct % 2) * 256
                    v.tensor_tensor(P.TMPF[ct % 2][:, 0:GW], DB[:, ct, :], VF[:, ct, :],
                                    ALU.mult)
                    ins = v.tensor_add(YSB[:, ct, :], P.TMPF[ct % 2][:, 0:GW],
                                       P.PS[bank][:, col:col + GW])
                    P.inc("DV", ins)
            yev_dv = P.cnt["DV"]

            @block.scalar
            def _(sc, yev_dv=yev_dv):
                P.w(sc, "DV", yev_dv)
                for ct in range(NCT):
                    ins = sc.activation(YSQ[:, ct, :], YSB[:, ct, :], AF.Square)
                    P.inc("AC", ins)
            ysq_ac = P.cnt["AC"]

            RW8 = self.ar_f32(0, 16384, (8, 512))
            banks = [P.PS[0], P.PS[1], P.PS[2], P.PS[3],
                     P.PSB[0], P.PSB[1], P.PSB[2], P.PSB[3]]
            gn_te0 = P.cnt["TE"]

            @block.tensor
            def _(te, ysq_ac=ysq_ac, banks=banks):
                P.w(te, "AC", ysq_ac)
                P.w(te, "DV")
                for ct in range(NCT):
                    te.matmul(banks[ct][0:2, 0:GW], BD2[:, 0:2], YSB[:, ct, :],
                              start=True, stop=False)
                    ins = te.matmul(banks[ct][0:2, 256:256 + GW], BD2[:, 0:2],
                                    YSQ[:, ct, :], start=False, stop=True)
                P.inc("TE", ins)
            gn_st = P.cnt["TE"]

            @block.vector
            def _(v, gn_st=gn_st, banks=banks):
                P.w(v, "TE", gn_st)
                for ct in range(NCT):
                    v.tensor_scalar_mul(RW8[0:2, ct, 0:GW], banks[ct][0:2, 0:GW],
                                        1.0 / 64)
                    v.tensor_scalar_mul(RW8[0:2, ct, 256:256 + GW],
                                        banks[ct][0:2, 256:256 + GW], 1.0 / 64)
                    v.tensor_tensor(P.TMPF[0][0:2, 0:GW], RW8[0:2, ct, 0:GW],
                                    RW8[0:2, ct, 0:GW], ALU.mult)
                    ins = v.tensor_sub(RW8[0:2, ct, 256:256 + GW],
                                       RW8[0:2, ct, 256:256 + GW],
                                       P.TMPF[0][0:2, 0:GW])
                P.inc("DV", ins)
            gn_rows = P.cnt["DV"]

            @block.scalar
            def _(sc, gn_rows=gn_rows):
                P.w(sc, "DV", gn_rows)
                sc.activation(RW8[0:2, :, 256:512], RW8[0:2, :, 256:512], AF.Ln,
                              bias=P.EPSC[0:2, 2:3])
                ins = sc.activation(RW8[0:2, :, 256:512], RW8[0:2, :, 256:512], AF.Exp,
                                    scale=-0.5)
                P.inc("AC", ins)
            gn_rstd = P.cnt["AC"]

            @block.tensor
            def _(te, gn_rstd=gn_rstd, banks=banks):
                P.w(te, "AC", gn_rstd)
                P.w(te, "DV")
                for ct in range(NCT):
                    te.matmul(banks[ct][:, 0:GW], EX2[0:2, :], RW8[0:2, ct, 0:GW],
                              start=True, stop=False)
                    ins = te.matmul(banks[ct][:, 256:256 + GW], EX2[0:2, :],
                                    RW8[0:2, ct, 256:256 + GW], start=False, stop=True)
                P.inc("TE", ins)
            gn_bc = P.cnt["TE"]

            @block.vector
            def _(v, gn_bc=gn_bc, banks=banks):
                P.w(v, "TE", gn_bc)
                for ct in range(NCT):
                    v.tensor_sub(P.TMPF[ct % 2][:, 0:GW], YSB[:, ct, :],
                                 banks[ct][:, 0:GW])
                    v.tensor_tensor(P.TMPF[ct % 2][:, 0:GW], P.TMPF[ct % 2][:, 0:GW],
                                    banks[ct][:, 256:256 + GW], ALU.mult)
                    v.tensor_scalar(P.TMPF[ct % 2][:, 0:GW], P.TMPF[ct % 2][:, 0:GW],
                                    GB[:, 0, ct:ct + 1], GB[:, 1, ct:ct + 1],
                                    ALU.mult, ALU.add)
                    ins = v.tensor_tensor(YNG[:, ct, :], P.TMPF[ct % 2][:, 0:GW],
                                          GF[:, ct, :], ALU.mult)
                P.inc("DV", ins)
            yng_dv = P.cnt["DV"]

            if g + 1 < NG:
                aa_lnA(g + 1)
            woff = 2 if g + 1 < NG else 0
            boff = 1 if g + 1 < NG else 0
            te0 = P.cnt["TE"]

            @block.tensor
            def _(te, yng_dv=yng_dv, te0=te0, g=g):
                P.w(te, "DV", yng_dv)
                P.w(te, "WD", wd_after[6 * g + 5])
                for mt in range(3):
                    s, mj = mt // 2, mt % 2
                    for kt in range(NCT):
                        ins = te.matmul(P.PS[mt % 3][:, 0:GW],
                                        SL[2][:, s * 2048 + kt * 256 + mj * 128:
                                              s * 2048 + kt * 256 + mj * 128 + 128],
                                        YNG[:, kt, :],
                                        start=(kt == 0), stop=(kt == NCT - 1))
                    P.inc("TE", ins)

            if g + 1 < NG:
                aa_lnB(g + 1)

            @block.tensor
            def _(te, yng_dv=yng_dv, te0=te0, g=g, woff=woff):
                for mt in range(3, 8):
                    s, mj = mt // 2, mt % 2
                    P.w(te, "DV", yng_dv + woff + (mt - 2))
                    for kt in range(NCT):
                        ins = te.matmul(P.PS[mt % 3][:, 0:GW],
                                        SL[2][:, s * 2048 + kt * 256 + mj * 128:
                                              s * 2048 + kt * 256 + mj * 128 + 128],
                                        YNG[:, kt, :],
                                        start=(kt == 0), stop=(kt == NCT - 1))
                    P.inc("TE", ins)
            if 6 * g + 8 < 6 * NG:
                emit_wload(block, 6 * g + 8, P.cnt["TE"])

            @block.vector
            def _(v, te0=te0, xres_xd=xres_xd, XGH=XGH, boff=boff):
                P.w(v, "XP", xres_xd)
                for mt in range(8):
                    P.w(v, "TE", te0 + mt + 1 + (boff if mt >= 3 else 0))
                    ins = v.tensor_add(XGH[:, mt, :], XGH[:, mt, :],
                                       P.PS[mt % 3][:, 0:GW])
                    P.inc("DV", ins)
            out_dv = P.cnt["DV"]
            dv_resid[g] = out_dv

            @block.sync
            def _(sy, g0=g0, out_dv=out_dv, XGH=XGH):
                P.w(sy, "DV", out_dv)
                dst = rout[:, g0:g0 + GW].rearrange("(ct p) c -> p ct c", p=128)
                P.inc("OD", sy.dma_start(out=dst, in_=XGH[:, :, :]), 16)
            od_after[g] = P.cnt["OD"]
    def _rope_qknorm(self, block, src_view4, nh, tt, out_bf4, COS, SIN):
        P = self
        T1 = self.ar_f32(1, 47616, (nh, 32))
        T2 = self.ar_f32(1, 49664, (nh, 32))
        T3 = self.ar_f32(1, 51712, (nh, 32))
        S2 = self._shape_view(
            self.WS[1][:, :].bitcast(F32)[:, 1024:1024 + nh * 32], (nh, 32))
        RSTQ = self.ar_f32(1, 53760, (16,))
        e = src_view4[:, :, :, 0]
        o = src_view4[:, :, :, 1]
        cosB = COS[:, tt * 32:(tt + 1) * 32].unsqueeze(1).broadcast_to([128, nh, 32])
        sinB = SIN[:, tt * 32:(tt + 1) * 32].unsqueeze(1).broadcast_to([128, nh, 32])

        @block.vector
        def _(v):
            P.w(v, "AC")
            P.w(v, "TE")
            v.tensor_tensor(T1, e, cosB, ALU.mult)
            v.tensor_tensor(T2, e, sinB, ALU.mult)
            v.tensor_tensor(T3, o, sinB, ALU.mult)
            v.tensor_sub(T1, T1, T3)
            v.tensor_tensor(T3, o, cosB, ALU.mult)
            v.tensor_add(T2, T2, T3)
            v.tensor_tensor(T3, T1, T1, ALU.mult)
            v.tensor_tensor(S2, T2, T2, ALU.mult)
            v.tensor_add(T3, T3, S2)
            ins = v.reduce_sum(RSTQ[:, 0:nh], T3, axis=mybir.AxisListType.X)
            P.inc("DV", ins)
        dv1 = P.cnt["DV"]

        @block.scalar
        def _(sc):
            P.w(sc, "DV", dv1)
            sc.activation(RSTQ[:, 0:nh], RSTQ[:, 0:nh], AF.Ln,
                          bias=P.EPSC[:, 1:2], scale=1.0 / 64.0)
            ins = sc.activation(RSTQ[:, 0:nh], RSTQ[:, 0:nh], AF.Exp, scale=-0.5)
            P.inc("AC", ins)
        ac1 = P.cnt["AC"]

        @block.vector
        def _(v):
            P.w(v, "AC", ac1)
            rstqB = RSTQ[:, 0:nh].unsqueeze(2).broadcast_to([128, nh, 32])
            v.tensor_tensor(out_bf4[:, :, :, 0], T1, rstqB, ALU.mult)
            ins = v.tensor_tensor(out_bf4[:, :, :, 1], T2, rstqB, ALU.mult)
            P.inc("DV", ins)

    def emit_attnb(self, block, rin, rout):
        P = self
        LATF = self.ar_bf(0, 0, (2, 2048))
        KF2 = self.ar_bf(0, 8192, (4, 2048))
        VT = self.ar_bf(0, 24576, (16, 260))
        QFS = [self.ar_bf(0, 32896, (8, 512)), self.ar_bf(0, 41088, (8, 512))]
        PTS = self.ar_bf(0, 58496, (4, 512))
        YF = self.ar_bf(0, 49280, (8, 512))
        YSB = self.ar_bf(0, 57472, (512,))
        WQR = self.ar_bf(1, 0, (8, 1024))
        WO = self.ar_bf(1, 16384, (2, 4096))
        KVD = self.ar_bf(1, 32768, (2048,))
        KUPR = self.ar_bf(1, 36864, (2, 256))
        VUPR = self.ar_bf(1, 37888, (2, 256))
        COS = self.ar_f32(1, 38912, (512,))
        SIN = self.ar_f32(1, 40960, (512,))
        GM = self.ar_bf(1, 43008, (4, 512))
        LWKR = self.ar_bf(1, 32768, (8, 256))
        LWVR = self.ar_bf(1, 43008, (8, 256))
        LM = self.ar_bf(1, 53824 + 2048, (6, 512))
        QNB = self.ar_bf(1, 53824, (1024,))
        WS1F = self.WS[1][:, :].bitcast(F32)
        QTF = WS1F[:, 0:1024]
        RSC = WS1F[0:1, 1536:2048]
        FHB = [self.ar_f32(0, 0, (4, 512)),
               self._shape_view(WS1F[:, 2048:4096], (4, 512))]
        KTF = self.ar_f32(0, 62592, (256,))
        PT0 = self.PS[3][:, 0:64].bitcast(BF16)
        PT1B = self.PS[2][:, 0:64].bitcast(BF16)
        PT2B = self.PS[3][:, 64:128].bitcast(BF16)
        PT3B = self.PS[2][:, 64:128].bitcast(BF16)
        XG2 = self.WS[0][:, :].bitcast(F32).rearrange("p (a b) -> p a b", a=8)
        ab_od = {}

        for phase in getattr(self, "ab_phases", ("g", "l")):
            self.barrier(block)

            @block.gpsimd
            def _(g_, phase=phase):
                if phase == "g":
                    P.inc("WD", g_.dma_start(out=WQR, in_=P.dram["ab_g_wq_r"]
                                             .rearrange("a p b -> p a b")), 16)
                    P.inc("WD", g_.dma_start(out=KVD, in_=P.dram["ab_g_kvd"][0]), 16)
                    P.inc("WD", g_.dma_start(out=KUPR, in_=P.dram["ab_g_kup_r"]
                                             .rearrange("a p b -> p a b")), 16)
                    P.inc("WD", g_.dma_start(out=VUPR, in_=P.dram["ab_g_vup_r"]
                                             .rearrange("a p b -> p a b")), 16)
                    P.inc("WD", g_.dma_start(out=WO, in_=P.dram["ab_g_wo"]
                                             .rearrange("a p b -> p a b")), 16)
                    P.inc("WD", g_.dma_start(out=COS, in_=P.dram["cos_g"]), 16)
                    P.inc("WD", g_.dma_start(out=SIN, in_=P.dram["sin_g"]), 16)
                    P.inc("WD", g_.dma_start(
                        out=GM, in_=P.dram["gmask"].rearrange("(a p) b -> p a b", p=128)), 16)
                    P.inc("WD", g_.dma_start(out=P.SCC[:], in_=P.dram["ab_norm1"][:]), 16)
                else:
                    P.inc("WD", g_.dma_start(out=WQR, in_=P.dram["ab_l_wq_r"]
                                             .rearrange("a p b -> p a b")), 16)
                    P.inc("WD", g_.dma_start(out=LWKR, in_=P.dram["ab_l_wk_r"]
                                             .rearrange("a p b -> p a b")), 16)
                    P.inc("WD", g_.dma_start(out=LWVR, in_=P.dram["ab_l_wv_r"]
                                             .rearrange("a p b -> p a b")), 16)
                    P.inc("WD", g_.dma_start(out=WO, in_=P.dram["ab_l_wo"]
                                             .rearrange("a p b -> p a b")), 16)
                    P.inc("WD", g_.dma_start(out=COS, in_=P.dram["cos_l"]), 16)
                    P.inc("WD", g_.dma_start(out=SIN, in_=P.dram["sin_l"]), 16)
                    P.inc("WD", g_.dma_start(
                        out=LM, in_=P.dram["lmask"].rearrange("(a p) b -> p a b", p=128)), 16)
                    P.inc("WD", g_.dma_start(out=P.SCC[:], in_=P.dram["ab_norm2"][:]), 16)
            wts = P.cnt["WD"]

            @block.vector
            def _(v):
                P.w(v, "TE")
                ins = v.memset(self._shape_view(VT.rearrange("p a b -> p (a b)"),
                                                (16, 4, 65))[:, :, :, 64:65], 1.0)
                P.inc("DV", ins)


            self._ab_xd = {}
            self._prep_done = {}
            self._yh_rd = {0: [], 1: []}
            self._psb_rd = {0: [], 1: []}
            self._prev_evac = [None]
            self._pend_bc = [None]
            self._pend_norm = [None]
            self._bc_te = {}
            self._norm_dv = {}

            def emit_load(g, phase=phase):
                XGB = (P.XG, XG2)[g % 2]
                srcT = rin if phase == "g" else self.dram["ures"]

                @block.gpsimd
                def _(g_, g=g, XGB=XGB, srcT=srcT):
                    if g >= 2:
                        P.w(g_, "OD", ab_od[(phase, g - 2)])
                    else:
                        P.w(g_, "OD")
                    src = srcT[:, g * 512:(g + 1) * 512].rearrange(
                        "(ct p) c -> p ct c", p=128)
                    P.inc("XP", g_.dma_start(out=XGB[:, :, :], in_=src), 16)
                self._ab_xd[g] = P.cnt["XP"]

            def emit_fx_load(g):
                @block.sync
                def _(sy, g=g):
                    for hf in range(2):
                        src = rin[hf * 512:(hf + 1) * 512, g * 512:(g + 1) * 512] \
                            .rearrange("(ct p) c -> p ct c", p=128)
                        P.inc("XD", sy.dma_start(out=FHB[hf][:, :, :], in_=src), 16)
                        self._fx_xd[(g, hf)] = P.cnt["XD"]

            def prep_pieces(g, phase=phase):
                g0 = g * 512
                XGB = (P.XG, XG2)[g % 2]
                QF = QFS[g % 2]
                pieces = []

                def p_ln():
                    xd = self._ab_xd[g]
                    if phase == "g":
                        te0 = P.cnt["TE"]
                        ac0 = P.cnt["AC"]

                        @block.scalar
                        def _(sc, xd=xd, te0=te0, XGB=XGB):
                            P.w(sc, "XP", xd)
                            P.w(sc, "DV")
                            for ct in range(NCT):
                                if ct >= 2:
                                    P.w(sc, "TE", te0 + (ct - 1))
                                P.inc("AC", sc.activation(P.TMPF[ct % 2][:],
                                                          XGB[:, ct, :], AF.Square))

                        @block.tensor
                        def _(te, xd=xd, ac0=ac0, XGB=XGB):
                            P.w(te, "XP", xd)
                            P.w(te, "DV")
                            for ct in range(NCT):
                                te.matmul(P.PS[2][0:1, 0:512], P.ones_f[:],
                                          XGB[:, ct, :],
                                          start=(ct == 0), stop=(ct == NCT - 1))
                            for ct in range(NCT):
                                P.w(te, "AC", ac0 + ct + 1)
                                ins = te.matmul(P.PS[3][0:1, 0:512], P.ones_f[:],
                                                P.TMPF[ct % 2][:],
                                                start=(ct == 0), stop=(ct == NCT - 1))
                                P.inc("TE", ins)
                        tes = P.cnt["TE"]

                        @block.vector
                        def _(v, tes=tes):
                            P.w(v, "TE", tes)
                            R = P.ROWS
                            v.tensor_scalar_mul(R[0:1, 512:1024], P.PS[3][0:1, 0:512],
                                                1.0 / D)
                            v.tensor_scalar_mul(RSC, P.PS[2][0:1, 0:512], 1.0 / D)
                            v.tensor_tensor(R[0:1, 1024:1536], RSC, RSC, ALU.mult)
                            ins = v.tensor_sub(R[0:1, 512:1024], R[0:1, 512:1024],
                                               R[0:1, 1024:1536])
                            P.inc("DV", ins)
                        dvs = P.cnt["DV"]

                        @block.scalar
                        def _(sc, dvs=dvs):
                            P.w(sc, "DV", dvs)
                            R = P.ROWS
                            sc.activation(R[0:1, 512:1024], R[0:1, 512:1024], AF.Ln,
                                          bias=P.EPSC[0:1, 0:1])
                            ins = sc.activation(R[0:1, 512:1024], R[0:1, 512:1024],
                                                AF.Exp, scale=-0.5)
                            P.inc("AC", ins)
                        ac2 = P.cnt["AC"]

                        @block.tensor
                        def _(te, ac2=ac2):
                            P.w(te, "AC", ac2)
                            P.w(te, "DV")
                            te.matmul(P.PS[3][:, 0:512], P.ones_row[:],
                                      P.ROWS[0:1, 512:1024], start=True, stop=True)
                            ins = te.matmul(P.PS[2][:, 0:512], P.ones_row[:],
                                            RSC, start=True, stop=True)
                            P.inc("TE", ins)
                        bcd = P.cnt["TE"]

                        @block.vector
                        def _(v, bcd=bcd, XGB=XGB):
                            P.w(v, "TE", bcd)
                            for ct in range(NCT):
                                v.tensor_sub(P.TMPF[ct % 2][:], XGB[:, ct, :],
                                             P.PS[2][:, 0:512])
                                ins = v.tensor_tensor(XGB[:, ct, :], P.TMPF[ct % 2][:],
                                                      P.PS[3][:, 0:512], ALU.mult)
                            P.inc("DV", ins)

                def p_rms():
                    xd = self._ab_xd[g]
                    te0 = P.cnt["TE"]
                    ac0 = P.cnt["AC"]
                    dv0 = P.cnt["DV"]

                    @block.scalar
                    def _(sc, xd=xd, te0=te0, dv0=dv0, XGB=XGB):
                        P.w(sc, "XP", xd)
                        P.w(sc, "DV", dv0)
                        for ct in range(NCT):
                            if ct >= 2:
                                P.w(sc, "TE", te0 + (ct - 1))
                            P.inc("AC", sc.activation(P.TMPF[ct % 2][:], XGB[:, ct, :],
                                                      AF.Square))

                    @block.tensor
                    def _(te, ac0=ac0, dv0=dv0):
                        P.w(te, "DV", dv0)
                        for ct in range(NCT):
                            P.w(te, "AC", ac0 + ct + 1)
                            ins = te.matmul(P.PS[3][0:1, 0:512], P.ones_f[:],
                                            P.TMPF[ct % 2][:],
                                            start=(ct == 0), stop=(ct == NCT - 1))
                            P.inc("TE", ins)
                    tes = P.cnt["TE"]

                    @block.vector
                    def _(v, tes=tes):
                        P.w(v, "TE", tes)
                        ins = v.tensor_scalar_mul(P.ROWS[0:1, 512:1024],
                                                  P.PS[3][0:1, 0:512], 1.0 / D)
                        P.inc("DV", ins)
                    dvs = P.cnt["DV"]

                    @block.scalar
                    def _(sc, dvs=dvs):
                        P.w(sc, "DV", dvs)
                        R = P.ROWS
                        sc.activation(R[0:1, 512:1024], R[0:1, 512:1024], AF.Ln,
                                      bias=P.EPSC[0:1, 1:2])
                        ins = sc.activation(R[0:1, 512:1024], R[0:1, 512:1024], AF.Exp,
                                            scale=-0.5)
                        P.inc("AC", ins)
                    ac2 = P.cnt["AC"]

                    @block.tensor
                    def _(te, ac2=ac2):
                        P.w(te, "AC", ac2)
                        P.w(te, "DV")
                        ins = te.matmul(P.PS[3][:, 0:512], P.ones_row[:],
                                        P.ROWS[0:1, 512:1024], start=True, stop=True)
                        P.inc("TE", ins)
                    bcd = P.cnt["TE"]

                    @block.vector
                    def _(v, bcd=bcd, XGB=XGB):
                        P.w(v, "TE", bcd)
                        P.w(v, "WD", wts)
                        for ct in range(NCT):
                            ins = v.scalar_tensor_tensor(P.XN[:, ct, 0:512],
                                                         XGB[:, ct, :],
                                                         P.SCC[:, ct:ct + 1],
                                                         P.PS[3][:, 0:512],
                                                         ALU.mult, ALU.mult)
                        P.inc("DV", ins)

                def p_latent():
                    if phase != "g":
                        return
                    xn_ok = P.cnt["DV"]

                    @block.tensor
                    def _(te, xn_ok=xn_ok, g0=g0):
                        P.w(te, "DV", xn_ok)
                        P.w(te, "AC")
                        for lt in range(2):
                            for kt in range(NCT):
                                ins = te.matmul(
                                    P.PS[2 + lt][:, 0:512],
                                    KVD[:, kt * 256 + lt * 128:kt * 256 + lt * 128 + 128],
                                    P.XN[:, kt, 0:512],
                                    start=(kt == 0), stop=(kt == NCT - 1))
                            P.inc("TE", ins)
                    lat_te = P.cnt["TE"]

                    @block.vector
                    def _(v, lat_te=lat_te, g0=g0):
                        P.w(v, "TE", lat_te)
                        for lt in range(2):
                            ins = v.tensor_copy(LATF[:, lt, g0:g0 + 512],
                                                P.PS[2 + lt][:, 0:512])
                            P.inc("DV", ins)

                def mk_tsub(tsub):
                    tt = g * 4 + tsub
                    rec = {}

                    def p_kv_a(tsub=tsub, tt=tt, rec=rec):
                        @block.tensor
                        def _(te, tsub=tsub, g0=g0):
                            P.w(te, "DV")
                            if phase == "g":
                                for lt in range(2):
                                    ins = te.matmul(
                                        P.PS[2][:, 0:256],
                                        LATF[:, lt, g0 + tsub * 128:g0 + tsub * 128 + 128],
                                        KUPR[:, lt, :],
                                        start=(lt == 0), stop=(lt == 1))
                            else:
                                for kt in range(NCT):
                                    ins = te.matmul(
                                        P.PS[2][:, 0:256],
                                        P.XN[:, kt, tsub * 128:tsub * 128 + 128],
                                        LWKR[:, kt, :],
                                        start=(kt == 0), stop=(kt == NCT - 1))
                            P.inc("TE", ins)
                        kf_te = P.cnt["TE"]

                        @block.vector
                        def _(v, kf_te=kf_te):
                            P.w(v, "TE", kf_te)
                            ins = v.tensor_copy(KTF, P.PS[2][:, 0:256])
                            P.inc("DV", ins)
                        rec["kcp"] = P.cnt["DV"]

                        kview = self._shape_view(KTF, (4, 32, 2))
                        kout = self._shape_view(P.EV[0][:, 0:256], (4, 32, 2))
                        self._rope_qknorm(block, kview, 4, tt, kout, COS, SIN)

                    def p_q_a(tsub=tsub, tt=tt, rec=rec):
                        @block.tensor
                        def _(te, tsub=tsub, rec=rec):
                            P.w(te, "DV", rec["kcp"])
                            for n2 in range(2):
                                for kt in range(NCT):
                                    ins = te.matmul(
                                        P.PS[2 + n2][:, 0:512],
                                        P.XN[:, kt, tsub * 128:tsub * 128 + 128],
                                        WQR[:, kt, n2 * 512:(n2 + 1) * 512],
                                        start=(kt == 0), stop=(kt == NCT - 1))
                                P.inc("TE", ins)
                        qte = P.cnt["TE"]

                        @block.vector
                        def _(v, qte=qte):
                            P.w(v, "TE", qte)
                            v.tensor_copy(QTF[:, 0:512], P.PS[2][:, 0:512])
                            ins = v.tensor_copy(QTF[:, 512:1024], P.PS[3][:, 0:512])
                            P.inc("DV", ins)
                        rec["qtf"] = P.cnt["DV"]

                        qview = self._shape_view(QTF, (16, 32, 2))
                        qout = self._shape_view(QNB, (16, 32, 2))
                        self._rope_qknorm(block, qview, 16, tt, qout, COS, SIN)

                    def p_kv_b(tsub=tsub, tt=tt, rec=rec):
                        @block.tensor
                        def _(te, rec=rec):
                            P.w(te, "DV", rec["qtf"])
                            te.transpose(PT0, P.EV[0][:, 0:128], P.id128[:])
                            ins = te.transpose(PT1B, P.EV[0][:, 128:256], P.id128[:])
                            P.inc("TE", ins)
                        trd = P.cnt["TE"]

                        @block.vector
                        def _(v, trd=trd, tt=tt):
                            P.w(v, "TE", trd)
                            c0 = tt * 128
                            for j, PT in ((0, PT0), (1, PT1B)):
                                v.tensor_copy(KF2[0:64, 2 * j, c0:c0 + 128], PT[0:64, :])
                                v.tensor_copy(KF2[64:128, 2 * j, c0:c0 + 128], PT[0:64, :])
                                v.tensor_copy(KF2[0:64, 2 * j + 1, c0:c0 + 128],
                                              PT[64:128, :])
                                ins = v.tensor_copy(KF2[64:128, 2 * j + 1, c0:c0 + 128],
                                                    PT[64:128, :])
                            P.inc("DV", ins)
                        rec["kf2"] = P.cnt["DV"]

                        @block.tensor
                        def _(te, tsub=tsub, g0=g0, rec=rec):
                            P.w(te, "DV", rec["kf2"])
                            if phase == "g":
                                for lt in range(2):
                                    ins = te.matmul(
                                        P.PS[3][:, 0:256],
                                        LATF[:, lt, g0 + tsub * 128:g0 + tsub * 128 + 128],
                                        VUPR[:, lt, :],
                                        start=(lt == 0), stop=(lt == 1))
                            else:
                                for kt in range(NCT):
                                    ins = te.matmul(
                                        P.PS[3][:, 0:256],
                                        P.XN[:, kt, tsub * 128:tsub * 128 + 128],
                                        LWVR[:, kt, :],
                                        start=(kt == 0), stop=(kt == NCT - 1))
                            P.inc("TE", ins)
                        vte = P.cnt["TE"]

                        @block.vector
                        def _(v, vte=vte, tt=tt):
                            P.w(v, "TE", vte)
                            for h in range(4):
                                ins = v.tensor_copy(VT[:, tt, h * 65:h * 65 + 64],
                                                    P.PS[3][:, h * 64:(h + 1) * 64])
                            P.inc("DV", ins)
                        rec["vev"] = P.cnt["DV"]

                    def p_q_b(tsub=tsub, tt=tt, rec=rec):
                        for j in range(0, 8, 2):
                            @block.tensor
                            def _(te, j=j, rec=rec):
                                if j == 0:
                                    P.w(te, "DV", rec["vev"])
                                else:
                                    P.w(te, "DV")
                                te.transpose(PT0, QNB[:, j * 128:(j + 1) * 128],
                                             P.id128[:])
                                ins = te.transpose(PT1B,
                                                   QNB[:, (j + 1) * 128:(j + 2) * 128],
                                                   P.id128[:])
                                P.inc("TE", ins)
                            trd = P.cnt["TE"]

                            @block.vector
                            def _(v, j=j, trd=trd, tsub=tsub, QF=QF):
                                P.w(v, "TE", trd)
                                v.tensor_copy(QF[:, j, tsub * 128:tsub * 128 + 128],
                                              PT0[:])
                                ins = v.tensor_copy(
                                    QF[:, j + 1, tsub * 128:tsub * 128 + 128], PT1B[:])
                                P.inc("DV", ins)
                        if tsub == 3:
                            self._prep_done[(phase, g)] = P.cnt["DV"]
                    return p_kv_a, p_q_a, p_kv_b, p_q_b

                pieces.append(p_ln)
                pieces.append(p_rms)
                pieces.append(p_latent)
                for tsub in range(4):
                    pieces.extend(mk_tsub(tsub))
                return pieces

            def emit_attention(g, feeder, phase=phase):
                g0 = g * 512
                XGB = (P.XG, XG2)[g % 2]
                QF = QFS[g % 2]
                nfeed = len(feeder)
                fed = [0]

                def feed(k):
                    while fed[0] < min(nfeed, k):
                        feeder[fed[0]]()
                        fed[0] += 1

                for h in range(HEADS):
                    kvh = h // 4
                    b = (h % 2) * 64
                    if phase == "g":
                        kts = list(range(0, 4 * g + 4))
                    else:
                        kts = [kt for kt in range(4 * g - 2, 4 * g + 4) if kt >= 0]
                    n = len(kts)
                    nb = (n + 1) // 2
                    te0 = P.cnt["TE"]
                    ac0 = P.cnt["AC"]
                    dv0 = P.cnt["DV"]
                    YH = P.PSB[2 + h % 2][0:65, 0:512]
                    masked = [i for i in range(n)
                              if phase == "l" or kts[i] >= 4 * g]
                    evac_ac = {i: ac0 + i + 1 for i in range(n)}
                    mask_dv = {i: dv0 + masked.index(i) + 1 for i in masked}
                    fill_te = {}
                    pv_te = {}

                    @block.tensor
                    def _(te, h=h, g=g, kvh=kvh, b=b, kts=kts, n=n, nb=nb,
                          evac_ac=evac_ac, mask_dv=mask_dv, fill_te=fill_te,
                          pv_te=pv_te, YH=YH, QF=QF, phase=phase):
                        if h == 0:
                            P.w(te, "DV", self._prep_done[(phase, g)])
                        if self._prev_evac[0] is not None:
                            P.w(te, "AC", self._prev_evac[0])
                        def fills(bi):
                            for i in range(bi * 2, min(n, bi * 2 + 2)):
                                if i >= 2:
                                    P.w(te, "AC", evac_ac[i - 2])
                                ins = te.matmul(
                                    P.PS[i % 2][:, 0:512],
                                    KF2[b:b + 64, kvh, kts[i] * 128:(kts[i] + 1) * 128],
                                    QF[b:b + 64, h // 2, 0:512],
                                    start=True, stop=True)
                                fill_te[i] = P.inc("TE", ins)
                        def pvs(bi):
                            for i in range(bi * 2, min(n, bi * 2 + 2)):
                                if i == 0:
                                    for sem, cnt in self._yh_rd[h % 2]:
                                        P.w(te, sem, cnt)
                                if i in mask_dv:
                                    P.w(te, "DV", mask_dv[i])
                                else:
                                    P.w(te, "AC", evac_ac[i])
                                ins = te.matmul(YH[:, 0:512],
                                                VT[:, kts[i], kvh * 65:kvh * 65 + 65],
                                                PTS[:, i % 4, 0:512],
                                                start=(i == 0), stop=(i == n - 1))
                                pv_te[i] = P.inc("TE", ins)
                        fills(0)
                        for bi in range(nb):
                            if bi + 1 < nb:
                                fills(bi + 1)
                            pvs(bi)
                            if bi == nb - 2 and self._pend_bc[0] is not None:
                                self._pend_bc[0](te)
                                self._pend_bc[0] = None
                    self._prev_evac[0] = evac_ac[n - 1]

                    @block.scalar
                    def _(sc, kts=kts, n=n, h=h, fill_te=fill_te, pv_te=pv_te):
                        for i in range(n):
                            P.w(sc, "TE", fill_te[i])
                            if i >= 4:
                                P.w(sc, "TE", pv_te[i - 4])
                            ins = sc.activation(PTS[:, i % 4, 0:512],
                                                P.PS[i % 2][:, 0:512],
                                                AF.Exp, scale=0.125)
                            P.inc("AC", ins)

                    @block.vector
                    def _(v, kts=kts, n=n, g=g, phase=phase, evac_ac=evac_ac,
                          masked=masked):
                        for i in masked:
                            kt = kts[i]
                            P.w(v, "AC", evac_ac[i])
                            if phase == "g":
                                ins = v.tensor_tensor(PTS[:, i % 4, :], PTS[:, i % 4, :],
                                                      GM[:, kt - 4 * g, :], ALU.mult)
                            else:
                                ins = v.tensor_tensor(PTS[:, i % 4, :], PTS[:, i % 4, :],
                                                      LM[:, kt - 4 * g + 2, :], ALU.mult)
                            P.inc("DV", ins)

                    if self._pend_norm[0] is not None:
                        self._pend_norm[0]()
                        self._pend_norm[0] = None

                    pv_done = P.cnt["TE"]

                    @block.scalar
                    def _(sc, pv_done=pv_done, h=h, YH=YH):
                        P.w(sc, "TE", pv_done)
                        sc.activation(P.ROWS[0:1, 0:512], YH[64:65, :], AF.Ln)
                        ins = sc.activation(P.ROWS[0:1, 0:512], P.ROWS[0:1, 0:512],
                                            AF.Exp, scale=-1.0)
                        P.inc("AC", ins)
                    sums_ac = P.cnt["AC"]

                    @block.vector
                    def _(v, pv_done=pv_done, h=h, YH=YH):
                        P.w(v, "TE", pv_done)
                        ins = v.tensor_copy(YSB[0:64, 0:512], YH[0:64, :])
                        P.inc("DV", ins)
                    rec_dv = P.cnt["DV"]
                    self._yh_rd[h % 2] = [("DV", rec_dv), ("AC", sums_ac)]

                    def mk_bc(h=h, rec_dv=rec_dv, sums_ac=sums_ac):
                        def emit_bc(te):
                            P.w(te, "DV", rec_dv)
                            P.w(te, "AC", sums_ac)
                            for sem, cnt in self._psb_rd[h % 2]:
                                P.w(te, sem, cnt)
                            ins = te.matmul(P.PSB[h % 2][0:64, 0:512],
                                            P.ones_row[0:1, 0:64],
                                            P.ROWS[0:1, 0:512], start=True, stop=True)
                            P.inc("TE", ins)
                            self._bc_te[h] = P.cnt["TE"]
                        return emit_bc

                    def mk_norm(h=h):
                        def emit_norm():
                            @block.vector
                            def _(v, h=h):
                                P.w(v, "TE", self._bc_te[h])
                                ins = v.tensor_tensor(
                                    YF[(h % 2) * 64:(h % 2) * 64 + 64, h // 2, :],
                                    YSB[0:64, 0:512],
                                    P.PSB[h % 2][0:64, 0:512], ALU.mult)
                                P.inc("DV", ins)
                            self._norm_dv[h] = P.cnt["DV"]
                            self._psb_rd[h % 2] = [("DV", self._norm_dv[h])]
                        return emit_norm

                    self._pend_bc[0] = mk_bc()
                    self._pend_norm[0] = mk_norm()

                    if h >= 5:
                        feed(((h - 4) * nfeed + (HEADS - 6)) // (HEADS - 5))

                @block.tensor
                def _(te):
                    self._pend_bc[0](te)
                    self._pend_bc[0] = None
                self._pend_norm[0]()
                self._pend_norm[0] = None

                feed(nfeed)

                te0 = P.cnt["TE"]
                dvy0 = P.cnt["DV"]
                norm_last = self._norm_dv[HEADS - 1]

                @block.tensor
                def _(te, te0=te0, dvy0=dvy0, norm_last=norm_last):
                    P.w(te, "DV", norm_last)
                    for mt in range(8):
                        s, mj = mt // 4, mt % 4
                        pa = P.PSB[mt % 2]
                        if mt >= 2:
                            P.w(te, "DV", dvy0 + mt - 1)
                        for kt in range(NCT):
                            ins = te.matmul(
                                pa[:, 0:512],
                                WO[:, s, kt * 512 + mj * 128:kt * 512 + mj * 128 + 128],
                                YF[:, kt, 0:512],
                                start=(kt == 0), stop=(kt == NCT - 1))
                        P.inc("TE", ins)

                @block.vector
                def _(v, te0=te0, XGB=XGB):
                    for mt in range(8):
                        P.w(v, "TE", te0 + mt + 1)
                        ins = v.tensor_add(XGB[:, mt, :], XGB[:, mt, :],
                                           P.PSB[mt % 2][:, 0:512])
                        P.inc("DV", ins)
                dvu = P.cnt["DV"]
                self._psb_rd[0] = [("DV", dvy0 + 7)]
                self._psb_rd[1] = [("DV", dvy0 + 8)]

                if phase == "g":
                    @block.sync
                    def _(sy, g0=g0, dvu=dvu, XGB=XGB):
                        P.w(sy, "DV", dvu)
                        dst = self.dram["ures"][:, g0:g0 + 512].rearrange(
                            "(ct p) c -> p ct c", p=128)
                        P.inc("OD", sy.dma_start(out=dst, in_=XGB[:, :, :]), 16)
                else:
                    @block.vector
                    def _(v, g=g, XGB=XGB):
                        for hf in range(2):
                            P.w(v, "XD", self._fx_xd[(g, hf)])
                            for j in range(4):
                                ins = v.tensor_add(XGB[:, hf * 4 + j, :],
                                                   XGB[:, hf * 4 + j, :],
                                                   FHB[hf][:, j, :])
                        P.inc("DV", ins)
                    dvf = P.cnt["DV"]

                    @block.sync
                    def _(sy, g0=g0, dvf=dvf, XGB=XGB):
                        P.w(sy, "DV", dvf)
                        dst = rout[:, g0:g0 + 512].rearrange(
                            "(ct p) c -> p ct c", p=128)
                        P.inc("OD", sy.dma_start(out=dst, in_=XGB[:, :, :]), 16)
                    if g + 1 < 4:
                        emit_fx_load(g + 1)
                ab_od[(phase, g)] = P.cnt["OD"]

            if phase == "l":
                self._fx_xd = {}
                emit_fx_load(0)
            emit_load(0)
            emit_load(1)
            for p in prep_pieces(0):
                p()
            for g in range(4):
                feeder = prep_pieces(g + 1) if g + 1 < 4 else []
                emit_attention(g, feeder)
                if g + 2 < 4:
                    emit_load(g + 2)
    def emit_tail(self, block):
        P = self

        @block.sync
        def _(sy):
            P.w(sy, "OD")
            P.w(sy, "WD")
            P.w(sy, "XD")

    def build(self):
        with self.nc.Block() as block:
            self.emit_consts(block)
            for st in self.stages:
                if st == "ffn_only":
                    self.emit_ffn(block, self.dram["xin"], self.dram["out"])
                elif st == "aa_only":
                    self.emit_attna(block, self.dram["xin"], self.dram["out"])
                elif st == "aa":
                    self.emit_attna(block, self.dram["xin"], self.dram["xres"])
                elif st == "ab_only":
                    self.emit_attnb(block, self.dram["xin"], self.dram["out"])
                elif st == "ab":
                    self.emit_attnb(block, self.dram["xres"], self.dram["xres"])
                elif st == "ffn1":
                    self.emit_ffn(block, self.dram["xres"], self.dram["xres"])
                elif st == "ffn2":
                    self.emit_ffn(block, self.dram["xres"], self.dram["out"])
            self.emit_tail(block)
        self.close()
        return self.nc


def kernel(**inputs):
    g = prep_global(inputs)
    x = inputs["L0_attn_a_x"]
    xt = np.ascontiguousarray(x.transpose(0, 2, 1))
    prog = Prog(stages=("aa", "ffn1", "ab", "ffn2"))
    nc = prog.build()
    in_maps = []
    for b in range(8):
        m = dict(g)
        m["xin"] = xt[b]
        in_maps.append(m)
    res = run_bass_kernel_spmd(nc, in_maps, core_ids=list(range(8)))
    outs = np.stack([r["out"] for r in res.results], axis=0)
    return np.ascontiguousarray(outs.transpose(0, 2, 1))


# version 22
# speedup vs baseline: 1.1664x; 1.1664x over previous
import numpy as np
import ml_dtypes

import concourse.bass as bass
import concourse.mybir as mybir
from concourse.bass_utils import run_bass_kernel_spmd

F32 = mybir.dt.float32
BF16 = mybir.dt.bfloat16
AF = mybir.ActivationFunctionType
ALU = mybir.AluOpType

D = 1024
T = 2048
NCT = D // 128
NTT = T // 128
HEADS = 16
HD = 64
CHUNK = 64
DH = 4096
LN_EPS = 1e-5
GN_EPS = 1e-5 * 64
RMS_EPS = 1e-6
BASE_G, BASE_L = 1.0e6, 1.0e4
KVH = 4
LAT = 256

bf = ml_dtypes.bfloat16

SEMS = ("WD", "XD", "XP", "OD", "TE", "AC", "DV", "PO")
ENGS = ("tensor", "scalar", "vector", "sync", "gpsimd")



def _lhsT_strips(w_eff, sm):
    K, M = w_eff.shape
    kt, ns = K // 128, M // sm
    a = w_eff.reshape(kt, 128, ns, sm).transpose(2, 1, 0, 3)
    return np.ascontiguousarray(a.reshape(ns, 128, kt * sm))


def _vec_tiles(v):
    return np.ascontiguousarray(v.reshape(-1, 128).T)


def _host_consts():
    c = {}
    c["identity128"] = np.eye(128, dtype=bf)
    c["ones_bf"] = np.ones((128, 1), dtype=bf)
    c["ones_f32"] = np.ones((128, 1), dtype=np.float32)
    c["onesrow_f32"] = np.ones((1, 128), dtype=np.float32)
    m = np.zeros((128, CHUNK), dtype=bf)
    base = (np.arange(CHUNK)[:, None] < np.arange(CHUNK)[None, :]).astype(bf)
    m[0:64] = base
    m[64:128] = base
    c["wkv_mask"] = m
    gm = np.zeros((4, 128, 512), dtype=bf)
    kk = np.arange(128)[:, None]
    qq = np.arange(512)[None, :]
    for j in range(4):
        gm[j] = (qq >= kk + j * 128).astype(bf)
    c["gmask"] = gm.reshape(4 * 128, 512)
    lm = np.zeros((6, 128, 512), dtype=bf)
    kk = np.arange(128)[:, None]
    qq = np.arange(512)[None, :]
    for rel in range(-2, 4):
        diff = qq - (rel * 128 + kk)
        lm[rel + 2] = ((diff >= 0) & (diff < 256)).astype(bf)
    c["lmask"] = lm.reshape(6 * 128, 512)
    for name, base_f in (("g", BASE_G), ("l", BASE_L)):
        inv = base_f ** (-np.arange(0, HD, 2, dtype=np.float64) / HD)
        fr = np.arange(T, dtype=np.float64)[:, None] * inv[None, :]
        cs = np.cos(fr).astype(np.float32).reshape(NTT, 128, 32).transpose(1, 0, 2)
        sn = np.sin(fr).astype(np.float32).reshape(NTT, 128, 32).transpose(1, 0, 2)
        c[f"cos_{name}"] = np.ascontiguousarray(cs.reshape(128, NTT * 32))
        c[f"sin_{name}"] = np.ascontiguousarray(sn.reshape(128, NTT * 32))
    return c


def prep_global(inp):
    g = {}
    w1s = _lhsT_strips(np.ascontiguousarray(inp["L0_ffn_w1_w"].T), 128)
    w2s = _lhsT_strips(np.ascontiguousarray(inp["L0_ffn_w2_w"].T), 128)
    g["ffn_w12"] = np.ascontiguousarray(np.concatenate([w1s, w2s], axis=2))
    g["ffn_h1"] = _lhsT_strips(np.ascontiguousarray(inp["L0_ffn_h1_w"].T), 256)
    g["ffn_h2"] = _lhsT_strips(np.ascontiguousarray(inp["L0_ffn_h2_w"].T), 256)
    g["ffn_w3"] = _lhsT_strips(np.ascontiguousarray(inp["L0_ffn_w3_w"].T), 256)
    g["ffn_s1"] = _vec_tiles(inp["L0_ffn_w1_s"])
    g["ffn_s2"] = _vec_tiles(inp["L0_ffn_w2_s"])
    g["ffn_h1s"] = _vec_tiles(inp["L0_ffn_h1_s"])
    g["ffn_h2s"] = _vec_tiles(inp["L0_ffn_h2_s"])
    g["ffn_w3s"] = _vec_tiles(inp["L0_ffn_w3_s"])
    for nm in ("W_r", "W_k", "W_v", "W_g", "W_o"):
        g[f"aa_{nm}"] = _lhsT_strips(inp[f"L0_attn_a_{nm}"], 256)
    g["aa_W_v_r"] = np.ascontiguousarray(inp["L0_attn_a_W_v"].reshape(8, 128, 1024))
    g["aa_maa_w1"] = _lhsT_strips(inp["L0_attn_a_time_maa_w1"], 160)
    w2 = inp["L0_attn_a_time_maa_w2"]
    w2p1 = np.zeros((128, 1024), dtype=np.float32)
    for f in range(3):
        w2p1[f * 32:(f + 1) * 32] = w2[f]
    w2p2 = np.zeros((64, 1024), dtype=np.float32)
    w2p2[0:32] = w2[3]
    w2p2[32:64] = w2[4]
    g["aa_w2p1"] = w2p1
    g["aa_w2p2"] = w2p2
    g["aa_dec_w1"] = _lhsT_strips(inp["L0_attn_a_time_decay_w1"], 64)
    g["aa_dec_w2"] = np.ascontiguousarray(inp["L0_attn_a_time_decay_w2"])
    bd2 = np.zeros((128, 2), dtype=bf)
    bd2[0:64, 0] = 1
    bd2[64:128, 1] = 1
    g["aa_bd2"] = bd2
    ex2 = np.zeros((2, 128), dtype=np.float32)
    ex2[0, 0:64] = 1
    ex2[1, 64:128] = 1
    g["aa_ex2"] = ex2
    for nm in ("time_maa_x", "time_maa_w", "time_maa_k", "time_maa_v", "time_maa_r",
               "time_maa_g", "time_decay"):
        g[f"aa_{nm}"] = _vec_tiles(inp[f"L0_attn_a_{nm}"])
    g["aa_ln_gamma"] = _vec_tiles(inp["L0_attn_a_ln_gamma"])
    g["aa_ln_beta"] = _vec_tiles(inp["L0_attn_a_ln_beta"])
    u = inp["L0_attn_a_time_faaaa"].reshape(D)
    Bu = np.zeros((NCT, 128, 128), dtype=np.float32)
    for t in range(NCT):
        for p in range(128):
            blk = (p // 64) * 64
            Bu[t, p, blk:blk + 64] = u[t * 128 + p]
    g["aa_Bu"] = np.ascontiguousarray(Bu.reshape(NCT * 128, 128))
    g["ab_g_wq_r"] = np.ascontiguousarray(inp["L0_attn_b_g_wq"].reshape(8, 128, 1024))
    g["ab_g_kvd"] = _lhsT_strips(inp["L0_attn_b_g_kv_down"], 256)
    g["ab_g_kup_r"] = np.ascontiguousarray(inp["L0_attn_b_g_k_up"].reshape(2, 128, 256))
    g["ab_g_vup_r"] = np.ascontiguousarray(inp["L0_attn_b_g_v_up"].reshape(2, 128, 256))
    g["ab_g_wo"] = _lhsT_strips(inp["L0_attn_b_g_wo"], 512)
    g["ab_l_wq_r"] = np.ascontiguousarray(inp["L0_attn_b_l_wq"].reshape(8, 128, 1024))
    g["ab_l_wk_r"] = np.ascontiguousarray(inp["L0_attn_b_l_wk"].reshape(8, 128, 256))
    g["ab_l_wv_r"] = np.ascontiguousarray(inp["L0_attn_b_l_wv"].reshape(8, 128, 256))
    g["ab_l_wo"] = _lhsT_strips(inp["L0_attn_b_l_wo"], 512)
    g["ab_norm1"] = _vec_tiles(inp["L0_attn_b_norm1_w"])
    g["ab_norm2"] = _vec_tiles(inp["L0_attn_b_norm2_w"])
    g.update(_host_consts())
    return g



class Prog:
    def __init__(self, stages=("aa", "ffn1", "ab", "ffn2")):
        self.stages = stages
        self.nc = bass.Bass("TRN2", target_bir_lowering=False,
                            detect_race_conditions=False)
        self.cnt = {}
        self.sems = {}
        self._mgrs = []
        self.dram = {}
        self._build_tensors()

    def _enter(self, mgr):
        self._mgrs.append(mgr)
        return mgr.__enter__()

    def din(self, name, shape, dtype=F32):
        ap = self.nc.dram_tensor(name, list(shape), dtype, kind="ExternalInput").ap()
        self.dram[name] = ap
        return ap

    def sbuf(self, name, shape, dtype):
        return self._enter(self.nc.sbuf_tensor(name, list(shape), dtype))

    def psum(self, name, shape, dtype=F32):
        return self._enter(self.nc.psum_tensor(name, list(shape), dtype))

    def close(self):
        for m in reversed(self._mgrs):
            m.__exit__(None, None, None)

    def inc(self, name, instr, n=1):
        instr.then_inc(self.sems[name], n)
        self.cnt[name] += n
        return self.cnt[name]

    def w(self, eng, name, val=None):
        v = self.cnt[name] if val is None else val
        if v > 0:
            eng.wait_ge(self.sems[name], v)

    def barrier(self, block, skip=()):
        snap = dict(self.cnt)
        for engname in ENGS:
            sec = getattr(block, engname)

            @sec
            def _(e, snap=snap):
                for s in SEMS:
                    if snap[s] and s not in skip:
                        e.wait_ge(self.sems[s], snap[s])

    def _build_tensors(self):
        nc = self.nc
        self.din("xin", (D, T))
        self.dram["out"] = nc.dram_tensor("out", [D, T], F32, kind="ExternalOutput").ap()
        self.dram["xres"] = nc.dram_tensor("xres", [D, T], F32, kind="Internal").ap()
        self.dram["ures"] = nc.dram_tensor("ures", [D, T], F32, kind="Internal").ap()

        self.din("ffn_w12", (32, 128, 2048))
        for nm in ("ffn_h1", "ffn_h2"):
            self.din(nm, (16, 128, 32 * 256))
        self.din("ffn_w3", (4, 128, 32 * 256))
        for nm in ("ffn_s1", "ffn_s2", "ffn_h1s", "ffn_h2s"):
            self.din(nm, (128, 32))
        self.din("ffn_w3s", (128, 8))

        for nm in ("aa_W_r", "aa_W_k", "aa_W_v", "aa_W_g", "aa_W_o"):
            self.din(nm, (4, 128, 8 * 256))
        self.din("aa_W_v_r", (8, 128, 1024))
        self.din("aa_maa_w1", (1, 128, 8 * 160))
        self.din("aa_w2p1", (128, 1024))
        self.din("aa_w2p2", (64, 1024))
        self.din("aa_dec_w1", (1, 128, 8 * 64))
        self.din("aa_dec_w2", (64, 1024))
        self.din("aa_bd2", (128, 2), BF16)
        self.din("aa_ex2", (2, 128))
        for nm in ("aa_time_maa_x", "aa_time_maa_w", "aa_time_maa_k", "aa_time_maa_v",
                   "aa_time_maa_r", "aa_time_maa_g", "aa_time_decay",
                   "aa_ln_gamma", "aa_ln_beta"):
            self.din(nm, (128, NCT))
        self.din("aa_Bu", (NCT * 128, 128))

        self.din("ab_g_wq_r", (8, 128, 1024))
        self.din("ab_g_kvd", (1, 128, 8 * 256))
        self.din("ab_g_kup_r", (2, 128, 256))
        self.din("ab_g_vup_r", (2, 128, 256))
        self.din("ab_g_wo", (2, 128, 8 * 512))
        self.din("ab_l_wq_r", (8, 128, 1024))
        self.din("ab_l_wk_r", (8, 128, 256))
        self.din("ab_l_wv_r", (8, 128, 256))
        self.din("ab_l_wo", (2, 128, 8 * 512))
        self.din("ab_norm1", (128, NCT))
        self.din("ab_norm2", (128, NCT))

        self.din("identity128", (128, 128), BF16)
        self.din("ones_bf", (128, 1), BF16)
        self.din("ones_f32", (128, 1), F32)
        self.din("onesrow_f32", (1, 128), F32)
        self.din("wkv_mask", (128, CHUNK), BF16)
        self.din("gmask", (4 * 128, 512), BF16)
        self.din("lmask", (6 * 128, 512), BF16)
        for nm in ("cos_g", "sin_g", "cos_l", "sin_l"):
            self.din(nm, (128, NTT * 32))

        for s in SEMS:
            self.sems[s] = self._enter(self.nc.semaphore(s))
            self.cnt[s] = 0

        sb = self.sbuf
        self.id128 = sb("id128", (128, 128), BF16)
        self.ones_b = sb("ones_b", (128, 1), BF16)
        self.ones_f = sb("ones_f", (128, 1), F32)
        self.ones_row = sb("ones_row", (1, 128), F32)
        self.AR0 = sb("AR0", (128, 32768), BF16)
        self.AR1 = sb("AR1", (128, 32768), BF16)
        self.WS = [sb("WS0", (128, 8192), BF16),
                   sb("WS1", (128, 8192), BF16)]
        self.XN = sb("XN", (128, 8, 1024), BF16)
        self.XG = sb("XG", (128, 8, 512), F32)
        self.ROWS = sb("ROWS", (1, 1536), F32)
        self.EV = [sb(f"EV{i}", (128, 512), BF16) for i in range(4)]
        self.TMPF = [sb(f"TMPF{i}", (128, 512), F32) for i in range(2)]
        self.EPSC = sb("EPSC", (128, 4), F32)
        self.SCA = sb("SCA", (128, 32), F32)
        self.SCB = sb("SCB", (128, 32), F32)
        self.SCC = sb("SCC", (128, 8), F32)

        self.PS = [self.psum(f"ps{i}", (128, 512), F32) for i in range(4)]
        self.PSB = [self.psum(f"psb{i}", (128, 512), F32) for i in range(4)]

    def ar_f32(self, which, off_bytes, shape):
        ar = (self.AR0, self.AR1)[which]
        n = int(np.prod(shape))
        v = ar[:, off_bytes // 2: off_bytes // 2 + 2 * n].bitcast(F32)
        return self._shape_view(v, shape)

    def ar_bf(self, which, off_bytes, shape):
        ar = (self.AR0, self.AR1)[which]
        n = int(np.prod(shape))
        v = ar[:, off_bytes // 2: off_bytes // 2 + n]
        return self._shape_view(v, shape)

    @staticmethod
    def _shape_view(v, shape):
        if len(shape) == 1:
            return v
        if len(shape) == 2:
            return v.rearrange("p (a b) -> p a b", a=shape[0], b=shape[1])
        if len(shape) == 3:
            return v.rearrange("p (a b c) -> p a b c", a=shape[0], b=shape[1], c=shape[2])
        return v.rearrange("p (a b c d) -> p a b c d", a=shape[0], b=shape[1],
                           c=shape[2], d=shape[3])

    def emit_consts(self, block):
        P = self

        @block.vector
        def _(v):
            v.memset(P.EPSC[:, 0:1], LN_EPS)
            v.memset(P.EPSC[:, 1:2], RMS_EPS)
            ins = v.memset(P.EPSC[:, 2:3], GN_EPS)
            P.inc("DV", ins)

        @block.gpsimd
        def _(g):
            P.inc("WD", g.dma_start(out=P.id128[:], in_=P.dram["identity128"][:]), 16)
            P.inc("WD", g.dma_start(out=P.ones_b[:], in_=P.dram["ones_bf"][:]), 16)
            P.inc("WD", g.dma_start(out=P.ones_f[:], in_=P.dram["ones_f32"][:]), 16)
            P.inc("WD", g.dma_start(out=P.ones_row[:], in_=P.dram["onesrow_f32"][:]), 16)

    def emit_ln_group(self, block, rin, g0, gw, xn_view, *, rms=False, normw_ap=None,
                      eps_col=0, xg=None):
        P = self
        xg = xg if xg is not None else self.XG

        @block.sync
        def _(sy):
            P.w(sy, "DV")
            src = rin[:, g0:g0 + gw].rearrange("(ct p) c -> p ct c", p=128)
            ins = sy.dma_start(out=xg[:, :, 0:gw], in_=src)
            P.inc("XD", ins, 16)
        xd = P.cnt["XD"]

        te0 = P.cnt["TE"]
        ac0 = P.cnt["AC"]

        @block.scalar
        def _(sc):
            P.w(sc, "XD", xd)
            for ct in range(NCT):
                if ct >= 2:
                    P.w(sc, "TE", te0 + (ct - 1))
                ins = sc.activation(P.TMPF[ct % 2][:, 0:gw], xg[:, ct, 0:gw], AF.Square)
                P.inc("AC", ins)

        @block.tensor
        def _(te):
            P.w(te, "XD", xd)
            P.w(te, "DV")
            if not rms:
                for ct in range(NCT):
                    te.matmul(P.PSB[2][0:1, 0:gw], P.ones_f[:], xg[:, ct, 0:gw],
                              start=(ct == 0), stop=(ct == NCT - 1))
            for ct in range(NCT):
                P.w(te, "AC", ac0 + ct + 1)
                ins = te.matmul(P.PSB[3][0:1, 0:gw], P.ones_f[:],
                                P.TMPF[ct % 2][:, 0:gw],
                                start=(ct == 0), stop=(ct == NCT - 1))
                P.inc("TE", ins)
        tes = P.cnt["TE"]

        @block.vector
        def _(v):
            P.w(v, "TE", tes)
            P.w(v, "PO")
            R = P.ROWS
            ins = v.tensor_scalar_mul(R[0:1, 512:512 + gw], P.PSB[3][0:1, 0:gw],
                                      1.0 / D)
            if not rms:
                v.tensor_scalar_mul(R[0:1, 0:gw], P.PSB[2][0:1, 0:gw], 1.0 / D)
                v.tensor_tensor(R[0:1, 1024:1024 + gw], R[0:1, 0:gw], R[0:1, 0:gw], ALU.mult)
                ins = v.tensor_sub(R[0:1, 512:512 + gw], R[0:1, 512:512 + gw],
                                   R[0:1, 1024:1024 + gw])
            P.inc("DV", ins)
        dvs = P.cnt["DV"]

        @block.scalar
        def _(sc):
            P.w(sc, "DV", dvs)
            R = P.ROWS
            sc.activation(R[0:1, 512:512 + gw], R[0:1, 512:512 + gw], AF.Ln,
                          bias=P.EPSC[0:1, eps_col:eps_col + 1])
            ins = sc.activation(R[0:1, 512:512 + gw], R[0:1, 512:512 + gw], AF.Exp, scale=-0.5)
            P.inc("AC", ins)
        ac2 = P.cnt["AC"]

        @block.tensor
        def _(te):
            P.w(te, "AC", ac2)
            P.w(te, "DV")
            ins = te.matmul(P.PSB[1][:, 0:gw], P.ones_row[:], P.ROWS[0:1, 512:512 + gw],
                            start=True, stop=True)
            if not rms:
                ins = te.matmul(P.PSB[0][:, 0:gw], P.ones_row[:], P.ROWS[0:1, 0:gw],
                                start=True, stop=True)
            P.inc("TE", ins)
        bc_done = P.cnt["TE"]

        @block.vector
        def _(v):
            P.w(v, "TE", bc_done)
            for ct in range(NCT):
                if rms:
                    ins = v.scalar_tensor_tensor(xn_view[:, ct, 0:gw], xg[:, ct, 0:gw],
                                                 normw_ap[:, ct:ct + 1], P.PSB[1][:, 0:gw],
                                                 ALU.mult, ALU.mult)
                else:
                    v.tensor_sub(P.TMPF[ct % 2][:, 0:gw], xg[:, ct, 0:gw], P.PSB[0][:, 0:gw])
                    ins = v.tensor_tensor(xn_view[:, ct, 0:gw], P.TMPF[ct % 2][:, 0:gw],
                                          P.PSB[1][:, 0:gw], ALU.mult)
            P.inc("DV", ins)

    def emit_ffn(self, block, rin, rout, half_range=(0, 1)):
        P = self
        HB0 = self.ar_bf(0, 0, (32, 1024))
        HB1 = self.ar_bf(1, 0, (32, 1024))

        @block.sync
        def _(sy):
            P.w(sy, "AC")
            ins = sy.dma_start(out=P.SCC[:], in_=P.dram["ffn_w3s"][:])
            P.inc("XD", ins, 16)

        for half in half_range:
            t0 = half * 1024
            self.barrier(block)

            @block.sync
            def _(sy):
                P.inc("XD", sy.dma_start(out=P.SCA[:], in_=P.dram["ffn_s1"][:]), 16)
                P.inc("XD", sy.dma_start(out=P.SCB[:], in_=P.dram["ffn_s2"][:]), 16)
            scl_ab = P.cnt["XD"]
            for g in range(2):
                self.emit_ln_group(block, rin, t0 + g * 512, 512,
                                   self.XN[:, :, g * 512:(g + 1) * 512])
            xn_ready = P.cnt["DV"]

            wd0, te0, ac0, dv0 = (P.cnt[s] for s in ("WD", "TE", "AC", "DV"))
            items = [(s, g) for s in range(32) for g in range(2)]

            @block.gpsimd
            def _(g_):
                for s in range(32):
                    if s >= 2:
                        P.w(g_, "TE", te0 + 2 * (s - 1))
                    ins = g_.dma_start(out=P.WS[s % 2][:, 0:2048], in_=P.dram["ffn_w12"][s])
                    P.inc("WD", ins, 16)

            @block.tensor
            def _(te):
                P.w(te, "DV", xn_ready)
                for idx, (s, g) in enumerate(items):
                    ws = P.WS[s % 2]
                    if g == 0:
                        P.w(te, "WD", wd0 + 16 * (s + 1))
                    pa = P.PS[2 * (idx % 2)]
                    pb = P.PS[2 * (idx % 2) + 1]
                    if idx >= 2:
                        P.w(te, "AC", ac0 + 2 * (idx - 1))
                    for kt in range(8):
                        te.matmul(pa[:, 0:512], ws[:, kt * 128:(kt + 1) * 128],
                                  P.XN[:, kt, g * 512:(g + 1) * 512],
                                  start=(kt == 0), stop=(kt == 7))
                    for kt in range(8):
                        ins = te.matmul(pb[:, 0:512], ws[:, 1024 + kt * 128:1024 + (kt + 1) * 128],
                                        P.XN[:, kt, g * 512:(g + 1) * 512],
                                        start=(kt == 0), stop=(kt == 7))
                    P.inc("TE", ins)

            @block.scalar
            def _(sc):
                P.w(sc, "XD", scl_ab)
                for idx, (s, g) in enumerate(items):
                    P.w(sc, "TE", te0 + idx + 1)
                    if idx >= 2:
                        P.w(sc, "DV", dv0 + (idx - 1))
                    pa = P.PS[2 * (idx % 2)]
                    pb = P.PS[2 * (idx % 2) + 1]
                    e0, e1 = P.EV[2 * (idx % 2)], P.EV[2 * (idx % 2) + 1]
                    sc.activation(e0[:], pa[:, 0:512], AF.Silu, scale=P.SCA[:, s:s + 1])
                    ins = sc.activation(e1[:], pb[:, 0:512], AF.Copy, scale=P.SCB[:, s:s + 1])
                    P.inc("AC", ins, 2)

            @block.vector
            def _(v):
                for idx, (s, g) in enumerate(items):
                    P.w(v, "AC", ac0 + 2 * (idx + 1))
                    e0, e1 = P.EV[2 * (idx % 2)], P.EV[2 * (idx % 2) + 1]
                    ins = v.tensor_tensor(HB0[:, s, g * 512:(g + 1) * 512], e0[:], e1[:],
                                          ALU.mult)
                    P.inc("DV", ins)

            for phs, (wname, sname, src, dst, sc_ap) in enumerate(
                    (("ffn_h1", "ffn_h1s", HB0, HB1, self.SCA),
                     ("ffn_h2", "ffn_h2s", HB1, HB0, self.SCB))):
                wd0, te0, ac0 = (P.cnt[s] for s in ("WD", "TE", "AC"))
                dv_in = P.cnt["DV"]

                @block.sync
                def _(sy, sname=sname, sc_ap=sc_ap):
                    P.w(sy, "AC")
                    ins = sy.dma_start(out=sc_ap[:], in_=P.dram[sname][:])
                    P.inc("XD", ins, 16)
                scl = P.cnt["XD"]

                @block.gpsimd
                def _(g_, wname=wname, te0=te0):
                    for s in range(16):
                        if s >= 2:
                            P.w(g_, "TE", te0 + 4 * (s - 1))
                        ins = g_.dma_start(out=P.WS[s % 2][:], in_=P.dram[wname][s])
                        P.inc("WD", ins, 16)

                items = [(s, m2, g) for s in range(16) for m2 in range(2) for g in range(2)]

                @block.tensor
                def _(te, items=items, wd0=wd0, ac0=ac0, dv_in=dv_in, src=src):
                    P.w(te, "DV", dv_in)
                    for idx, (s, m2, g) in enumerate(items):
                        ws = P.WS[s % 2]
                        if m2 == 0 and g == 0:
                            P.w(te, "WD", wd0 + 16 * (s + 1))
                        pa = P.PS[idx % 3]
                        if idx >= 3:
                            P.w(te, "AC", ac0 + (idx - 2))
                        for kt in range(32):
                            ins = te.matmul(pa[:, 0:512],
                                            ws[:, kt * 256 + m2 * 128:kt * 256 + m2 * 128 + 128],
                                            src[:, kt, g * 512:(g + 1) * 512],
                                            start=(kt == 0), stop=(kt == 31))
                        P.inc("TE", ins)

                @block.scalar
                def _(sc, items=items, te0=te0, scl=scl, dst=dst, sc_ap=sc_ap):
                    P.w(sc, "XD", scl)
                    for idx, (s, m2, g) in enumerate(items):
                        m_abs = s * 2 + m2
                        P.w(sc, "TE", te0 + idx + 1)
                        pa = P.PS[idx % 3]
                        ins = sc.activation(dst[:, m_abs, g * 512:(g + 1) * 512], pa[:, 0:512],
                                            AF.Gelu, scale=sc_ap[:, m_abs:m_abs + 1])
                        P.inc("AC", ins)

                @block.vector
                def _(v):
                    P.w(v, "AC")
                    ins = v.nop()
                    P.inc("DV", ins)

            wd0, te0 = P.cnt["WD"], P.cnt["TE"]
            dv_in = P.cnt["DV"]
            XG2 = self.ar_f32(1, 0, (8, 512))

            @block.gpsimd
            def _(g_):
                for s in range(4):
                    if s >= 2:
                        P.w(g_, "TE", te0 + 4 * (s - 1))
                    ins = g_.dma_start(out=P.WS[s % 2][:], in_=P.dram["ffn_w3"][s])
                    P.inc("WD", ins, 16)

            @block.sync
            def _(sy):
                P.w(sy, "DV", dv_in)
                src0 = rin[:, t0:t0 + 512].rearrange("(ct p) c -> p ct c", p=128)
                ins = sy.dma_start(out=P.XG[:, :, :], in_=src0)
                P.inc("XD", ins, 16)
                src1 = rin[:, t0 + 512:t0 + 1024].rearrange("(ct p) c -> p ct c", p=128)
                ins = sy.dma_start(out=XG2[:, :, :], in_=src1)
                P.inc("XD", ins, 16)
            xd_d = P.cnt["XD"]

            items = [(s, m2, g) for s in range(4) for m2 in range(2) for g in range(2)]

            @block.tensor
            def _(te):
                P.w(te, "DV", dv_in)
                for idx, (s, m2, g) in enumerate(items):
                    ws = P.WS[s % 2]
                    if m2 == 0 and g == 0:
                        P.w(te, "WD", wd0 + 16 * (s + 1))
                    pa = P.PS[idx % 3]
                    if idx >= 3:
                        P.w(te, "DV", dv_in + (idx - 2))
                    for kt in range(32):
                        ins = te.matmul(pa[:, 0:512],
                                        ws[:, kt * 256 + m2 * 128:kt * 256 + m2 * 128 + 128],
                                        HB0[:, kt, g * 512:(g + 1) * 512],
                                        start=(kt == 0), stop=(kt == 31))
                    P.inc("TE", ins)

            @block.vector
            def _(v):
                P.w(v, "XD", xd_d)
                for idx, (s, m2, g) in enumerate(items):
                    ct = s * 2 + m2
                    P.w(v, "TE", te0 + idx + 1)
                    pa = P.PS[idx % 3]
                    xgb = P.XG if g == 0 else XG2
                    ins = v.scalar_tensor_tensor(xgb[:, ct, 0:512], pa[:, 0:512],
                                                 P.SCC[:, ct:ct + 1], xgb[:, ct, 0:512],
                                                 ALU.mult, ALU.add)
                    P.inc("DV", ins)
            dvw = P.cnt["DV"]

            @block.sync
            def _(sy):
                P.w(sy, "DV", dvw)
                dst0 = rout[:, t0:t0 + 512].rearrange("(ct p) c -> p ct c", p=128)
                ins = sy.dma_start(out=dst0, in_=P.XG[:, :, :])
                P.inc("OD", ins, 16)
                dst1 = rout[:, t0 + 512:t0 + 1024].rearrange("(ct p) c -> p ct c", p=128)
                ins = sy.dma_start(out=dst1, in_=XG2[:, :, :])
                P.inc("OD", ins, 16)
                P.w(sy, "OD")



    def emit_attna(self, block, rin, rout):
        P = self
        GW = 256
        NG = getattr(self, 'aa_ng', T // GW)
        XN1 = self.ar_f32(0, 0, (8, GW))
        XX = self.ar_f32(0, 8192, (8, GW))
        MIX = [self.ar_bf(0, 16384, (8, GW)), self.ar_bf(0, 20480, (8, GW))]
        CBUF = self.ar_bf(0, 24576, (8, GW))
        EWF = self.ar_f32(0, 28672, (8, GW))
        LA = self.ar_f32(0, 36864, (8, GW))
        EP = self.ar_f32(0, 45056, (8, GW))
        RT = self.ar_bf(0, 53248, (8, GW))
        KT = self.ar_bf(0, 57344, (8, GW))
        MAA1 = self.ar_bf(0, 61440, (256,))
        MAA2 = self.ar_bf(0, 61952, (256,))
        PREVC = self.ar_f32(0, 62464, (8, 1))
        RR = self.ar_bf(1, 0, (8, GW))
        RK = self.ar_bf(1, 4096, (8, GW))
        KHF = self.ar_bf(1, 8192, (8, GW))
        KHT = self.ar_bf(1, 12288, (2, 1024))
        VTM = self.ar_bf(1, 16384, (2, 1024))
        VF = self.ar_bf(1, 20480, (8, GW))
        GF = self.ar_bf(1, 24576, (8, GW))
        DB = self.ar_bf(1, 28672, (8, GW))
        YSB = self.ar_bf(1, 32768, (8, GW))
        YNG = self.ar_bf(1, 36864, (8, GW))
        S = self.ar_f32(1, 40960, (8, 64))
        ALST = self.ar_f32(1, 43008, (8, 4))
        W1M = self.ar_bf(1, 43136, (8, 160))
        DW1 = self.ar_bf(1, 45696, (8, 64))
        DW2 = self.ar_bf(1, 46720, (1024,))
        W2P1 = self.ar_bf(1, 48768, (1024,))
        W2P2 = self.ar_bf(1, 50816, (1024,))
        BUS = self.ar_bf(1, 52864, (8, 128))
        RW2 = self.ar_f32(1, 54912, (512,))
        TMV = self.ar_f32(1, 56960, (7, 8))
        GB = self.ar_f32(1, 57184, (2, 8))
        WKM = self.ar_bf(1, 57248, (64,))
        BD2 = self.ar_bf(1, 57376, (2,))
        EX2 = self.ar_f32(1, 57408, (128,))
        YSQ = self.ar_bf(1, 57920, (8, GW))
        PTSB = self.ar_bf(1, 62016, (2, 512))
        SBF = self.ar_bf(1, 64064, (8, 64))
        ZROW = self.ar_bf(0, 62496, (640,))
        PT0 = self.PS[3][:, 0:64].bitcast(BF16)
        XNW = self.XN[:, :, :].rearrange("p a b -> p (a b)")
        SL = [self.WS[0], self.WS[1], XNW]
        WMATS = ["aa_W_r", "aa_W_k", "aa_W_v_r", "aa_W_v", "aa_W_g", "aa_W_o"]
        wd_after = {}
        od_after = {}
        dv_resid = {}

        def emit_wload(block, J, te_wait):
            nm = WMATS[J % 6]
            slot = SL[J % 3]
            a = 8 if nm == "aa_W_v_r" else 4

            @block.gpsimd
            def _(g_, J=J, nm=nm, slot=slot, a=a, te_wait=te_wait):
                P.w(g_, "TE", te_wait)
                ins = g_.dma_start(
                    out=slot[:, 0:8192].rearrange("p (a b) -> p a b", a=a),
                    in_=P.dram[nm].rearrange("a p b -> p a b"))
                wd_after[J] = P.inc("WD", ins, 16)

        self.barrier(block)

        @block.gpsimd
        def _(g_):
            P.w(g_, "OD")
            src0 = rin[:, 0:GW].rearrange("(ct p) c -> p ct c", p=128)
            P.inc("XP", g_.dma_start(out=P.XG[:, :, 0:GW], in_=src0), 16)
        self._aa_xd = {0: P.cnt["XP"]}
        for J0 in range(3):
            emit_wload(block, J0, 0)

        @block.gpsimd
        def _(g_):
            P.inc("WD", g_.dma_start(out=W1M, in_=P.dram["aa_maa_w1"][0]
                                     .rearrange("p (a b) -> p a b", b=160)), 16)
            P.inc("WD", g_.dma_start(out=DW1, in_=P.dram["aa_dec_w1"][0]
                                     .rearrange("p (a b) -> p a b", b=64)), 16)
            P.inc("WD", g_.dma_start(out=DW2[0:64, :], in_=P.dram["aa_dec_w2"][:]), 16)
            P.inc("WD", g_.dma_start(out=W2P1, in_=P.dram["aa_w2p1"][:]), 16)
            P.inc("WD", g_.dma_start(out=W2P2[0:64, :], in_=P.dram["aa_w2p2"][:]), 16)
            P.inc("WD", g_.dma_start(out=BUS, in_=P.dram["aa_Bu"]
                                     .rearrange("(a p) b -> p a b", p=128)), 16)
            for i, nm in enumerate(("aa_time_maa_x", "aa_time_maa_w", "aa_time_maa_k",
                                    "aa_time_maa_v", "aa_time_maa_r", "aa_time_maa_g",
                                    "aa_time_decay")):
                P.inc("WD", g_.dma_start(out=TMV[:, i, :], in_=P.dram[nm][:]), 16)
            P.inc("WD", g_.dma_start(out=GB[:, 0, :], in_=P.dram["aa_ln_gamma"][:]), 16)
            P.inc("WD", g_.dma_start(out=GB[:, 1, :], in_=P.dram["aa_ln_beta"][:]), 16)
            P.inc("WD", g_.dma_start(out=WKM, in_=P.dram["wkv_mask"][:]), 16)
            P.inc("WD", g_.dma_start(out=BD2, in_=P.dram["aa_bd2"][:]), 16)
            P.inc("WD", g_.dma_start(out=EX2[0:2, :], in_=P.dram["aa_ex2"][:]), 16)
        smalls_wd = P.cnt["WD"]

        @block.vector
        def _(v):
            v.memset(S[:], 0.0)
            v.memset(SBF[:], 0.0)
            v.memset(ZROW[0:1, :], 0.0)
            ins = v.memset(PREVC[:], 0.0)
            P.inc("DV", ins)

        self._aa_tes = {}
        self._aa_xxx = {}

        def aa_lnA(g):
            half = (g % 2) * GW
            XGH = P.XG[:, :, half:half + GW]
            xd = self._aa_xd[g]
            te0 = P.cnt["TE"]
            ac0 = P.cnt["AC"]

            @block.scalar
            def _(sc, xd=xd, te0=te0, XGH=XGH):
                P.w(sc, "XP", xd)
                P.w(sc, "DV")
                for ct in range(NCT):
                    if ct >= 2:
                        P.w(sc, "TE", te0 + (ct - 1))
                    P.inc("AC", sc.activation(P.TMPF[ct % 2][:, 0:GW], XGH[:, ct, :],
                                              AF.Square))

            @block.tensor
            def _(te, xd=xd, ac0=ac0):
                P.w(te, "XP", xd)
                P.w(te, "DV")
                for ct in range(NCT):
                    te.matmul(P.PSB[2][0:1, 0:GW], P.ones_f[:], XGH[:, ct, :],
                              start=(ct == 0), stop=(ct == NCT - 1))
                for ct in range(NCT):
                    P.w(te, "AC", ac0 + ct + 1)
                    ins = te.matmul(P.PSB[3][0:1, 0:GW], P.ones_f[:],
                                    P.TMPF[ct % 2][:, 0:GW],
                                    start=(ct == 0), stop=(ct == NCT - 1))
                    P.inc("TE", ins)
            self._aa_tes[g] = P.cnt["TE"]

        def aa_lnB(g):
            half = (g % 2) * GW
            XGH = P.XG[:, :, half:half + GW]
            tes = self._aa_tes[g]

            @block.vector
            def _(v, tes=tes):
                P.w(v, "TE", tes)
                R = P.ROWS
                v.tensor_scalar_mul(R[0:1, 512:512 + GW], P.PSB[3][0:1, 0:GW], 1.0 / D)
                v.tensor_scalar_mul(R[0:1, 0:GW], P.PSB[2][0:1, 0:GW], 1.0 / D)
                v.tensor_tensor(R[0:1, 1024:1024 + GW], R[0:1, 0:GW], R[0:1, 0:GW], ALU.mult)
                ins = v.tensor_sub(R[0:1, 512:512 + GW], R[0:1, 512:512 + GW],
                                   R[0:1, 1024:1024 + GW])
                P.inc("DV", ins)
            dvs = P.cnt["DV"]

            @block.scalar
            def _(sc, dvs=dvs):
                P.w(sc, "DV", dvs)
                R = P.ROWS
                sc.activation(R[0:1, 512:512 + GW], R[0:1, 512:512 + GW], AF.Ln,
                              bias=P.EPSC[0:1, 0:1])
                ins = sc.activation(R[0:1, 512:512 + GW], R[0:1, 512:512 + GW], AF.Exp,
                                    scale=-0.5)
                P.inc("AC", ins)
            ac2 = P.cnt["AC"]

            @block.tensor
            def _(te, ac2=ac2):
                P.w(te, "AC", ac2)
                P.w(te, "DV")
                te.matmul(P.PSB[1][:, 0:GW], P.ones_row[:], P.ROWS[0:1, 512:512 + GW],
                          start=True, stop=True)
                ins = te.matmul(P.PSB[0][:, 0:GW], P.ones_row[:], P.ROWS[0:1, 0:GW],
                                start=True, stop=True)
                P.inc("TE", ins)
            bcd = P.cnt["TE"]

            @block.vector
            def _(v, bcd=bcd, g=g):
                P.w(v, "TE", bcd)
                if g == 0:
                    P.w(v, "WD", smalls_wd)
                else:
                    v.tensor_copy(PREVC[:, :, 0:1], XN1[:, :, GW - 1:GW])
                for ct in range(NCT):
                    v.tensor_sub(P.TMPF[ct % 2][:, 0:GW], XGH[:, ct, :],
                                 P.PSB[0][:, 0:GW])
                    v.tensor_tensor(XN1[:, ct, :], P.TMPF[ct % 2][:, 0:GW],
                                    P.PSB[1][:, 0:GW], ALU.mult)
                for ct in range(NCT):
                    v.tensor_sub(XX[:, ct, 1:GW], XN1[:, ct, 0:GW - 1], XN1[:, ct, 1:GW])
                    v.tensor_sub(XX[:, ct, 0:1], PREVC[:, ct, :], XN1[:, ct, 0:1])
                for ct in range(NCT):
                    ins = v.scalar_tensor_tensor(MIX[0][:, ct, :], XX[:, ct, :],
                                                 TMV[:, 0, ct:ct + 1], XN1[:, ct, :],
                                                 ALU.mult, ALU.add)
                P.inc("DV", ins)
            self._aa_xxx[g] = P.cnt["DV"]

        for g in range(NG):
            g0 = g * GW
            half = (g % 2) * GW
            XGH = P.XG[:, :, half:half + GW]
            if g == 0:
                aa_lnA(0)
                aa_lnB(0)
            xxx_dv = self._aa_xxx[g]

            te0 = P.cnt["TE"]

            @block.tensor
            def _(te, xxx_dv=xxx_dv, g=g):
                P.w(te, "DV", xxx_dv)
                P.w(te, "AC")
                if g == 0:
                    P.w(te, "WD", smalls_wd)
                for mt, (mw, pb) in enumerate(((64, 0), (64, 1), (32, 2))):
                    for kt in range(NCT):
                        ins = te.matmul(P.PS[pb][0:mw, 0:GW],
                                        W1M[:, kt, mt * 64:mt * 64 + mw],
                                        MIX[0][:, kt, :],
                                        start=(kt == 0), stop=(kt == NCT - 1))
                    P.inc("TE", ins)
            m1_te = P.cnt["TE"]

            @block.scalar
            def _(sc, m1_te=m1_te, g=g):
                P.w(sc, "TE", m1_te)
                if g == 0:
                    P.w(sc, "WD", smalls_wd)
                sc.activation(MAA1[0:64, :], P.PS[0][0:64, 0:GW], AF.Tanh)
                sc.activation(MAA1[64:96, :], P.PS[1][0:32, 0:GW], AF.Tanh)
                sc.activation(MAA2[0:32, :], P.PS[1][32:64, 0:GW], AF.Tanh)
                ins = sc.activation(MAA2[32:64, :], P.PS[2][0:32, 0:GW], AF.Tanh)
                P.inc("AC", ins)
            m1_ac = P.cnt["AC"]

            MSLOT = [MIX[0], MIX[1], CBUF, YSB, YNG]
            forder = [(0, 1, 0), (3, 4, 1), (1, 2, 2), (2, 3, 3), (4, 5, 4)]
            m2cfg = []
            for f, tmcol, slot in forder:
                if f == 0:
                    wt, mb = W2P1, 0
                elif f == 1:
                    wt, mb = W2P1, 32
                elif f == 2:
                    wt, mb = W2P1, 64
                elif f == 3:
                    wt, mb = W2P2, 0
                else:
                    wt, mb = W2P2, 32
                m2cfg.append((f, tmcol, MSLOT[slot], wt, mb, MAA1 if f < 3 else MAA2))
            m2_te0 = P.cnt["TE"]
            m2_dv0 = P.cnt["DV"]

            @block.tensor
            def _(te, m2cfg=m2cfg, m2_te0=m2_te0, m2_dv0=m2_dv0):
                P.w(te, "AC")
                P.w(te, "DV")
                idx = 0
                for f, tmcol, slot, wt, mb, mrow in m2cfg:
                    for mt in range(NCT):
                        if idx >= 3:
                            P.w(te, "DV", m2_dv0 + 3 * (idx - 2))
                        ins = te.matmul(P.PS[idx % 3][:, 0:GW],
                                        wt[mb:mb + 32, mt * 128:(mt + 1) * 128],
                                        mrow[mb:mb + 32, :],
                                        start=True, stop=True)
                        P.inc("TE", ins)
                        idx += 1

            @block.vector
            def _(v, m2cfg=m2cfg, m2_te0=m2_te0):
                idx = 0
                for f, tmcol, slot, wt, mb, mrow in m2cfg:
                    for mt in range(NCT):
                        P.w(v, "TE", m2_te0 + idx + 1)
                        v.tensor_scalar(P.TMPF[idx % 2][:, 0:GW], P.PS[idx % 3][:, 0:GW],
                                        TMV[:, tmcol, mt:mt + 1], None, ALU.add)
                        v.tensor_tensor(P.TMPF[idx % 2][:, 0:GW], P.TMPF[idx % 2][:, 0:GW],
                                        XX[:, mt, :], ALU.mult)
                        ins = v.tensor_add(slot[:, mt, :], P.TMPF[idx % 2][:, 0:GW],
                                           XN1[:, mt, :])
                        P.inc("DV", ins, 3)
                        idx += 1
            mixall_dv = P.cnt["DV"]

            mixw_dv = mixall_dv
            te0 = P.cnt["TE"]

            @block.tensor
            def _(te, mixw_dv=mixw_dv, m2_dv0=m2_dv0):
                P.w(te, "DV", m2_dv0 + 24)
                for kt in range(NCT):
                    ins = te.matmul(P.PSB[1][0:64, 0:GW], DW1[:, kt, :], MIX[0][:, kt, :],
                                    start=(kt == 0), stop=(kt == NCT - 1))
                P.inc("TE", ins)
            z_te = P.cnt["TE"]

            @block.scalar
            def _(sc, z_te=z_te):
                P.w(sc, "TE", z_te)
                ins = sc.activation(P.EV[0][0:64, 0:GW], P.PSB[1][0:64, 0:GW], AF.Tanh)
                P.inc("AC", ins)
            z_ac = P.cnt["AC"]
            te0 = P.cnt["TE"]

            @block.tensor
            def _(te, z_ac=z_ac):
                P.w(te, "AC", z_ac)
                for mt in range(NCT):
                    if mt >= 3:
                        P.w(te, "AC", z_ac + (mt - 2))
                    ins = te.matmul(P.PSB[1 + mt % 3][:, 0:GW],
                                    DW2[0:64, mt * 128:(mt + 1) * 128],
                                    P.EV[0][0:64, 0:GW], start=True, stop=True)
                    P.inc("TE", ins)

            @block.scalar
            def _(sc, te0=te0):
                for mt in range(NCT):
                    P.w(sc, "TE", te0 + mt + 1)
                    ins = sc.activation(EWF[:, mt, :], P.PSB[1 + mt % 3][:, 0:GW], AF.Exp,
                                        bias=TMV[:, 6, mt:mt + 1])
                    P.inc("AC", ins)
            ew_ac = P.cnt["AC"]

            @block.vector
            def _(v, ew_ac=ew_ac):
                P.w(v, "AC", ew_ac)
                for ct in range(NCT):
                    for c in range(4):
                        ins = v.tensor_tensor_scan(LA[:, ct, c * 64:(c + 1) * 64],
                                                   EWF[:, ct, c * 64:(c + 1) * 64],
                                                   EWF[:, ct, c * 64:(c + 1) * 64],
                                                   0.0, ALU.add, ALU.bypass)
                ep4 = self.ar_f32(0, 45056, (8, 4, 64))
                ins = v.memset(ep4[:, :, :, 0:1], 1.0)
                P.inc("DV", ins)
            la_dv = P.cnt["DV"]

            @block.scalar
            def _(sc, la_dv=la_dv):
                P.w(sc, "DV", la_dv)
                la4 = self.ar_f32(0, 36864, (8, 4, 64))
                ep4 = self.ar_f32(0, 45056, (8, 4, 64))
                for ct in range(NCT):
                    sc.activation(EWF[:, ct, :], LA[:, ct, :], AF.Exp)
                    sc.activation(ep4[:, ct, :, 1:64], la4[:, ct, :, 0:63], AF.Exp,
                                  scale=-1.0)
                    ins = sc.activation(ALST[:, ct, :], la4[:, ct, :, 63:64], AF.Exp,
                                        scale=-1.0)
                    P.inc("AC", ins)
            ef_ac = P.cnt["AC"]
            EN = EWF

            mixr_dv = m2_dv0 + 48
            te0 = P.cnt["TE"]

            rev_base = P.cnt["DV"]

            @block.tensor
            def _(te, mixr_dv=mixr_dv, g=g, rev_base=rev_base):
                P.w(te, "DV", mixr_dv)
                P.w(te, "AC")
                P.w(te, "WD", wd_after[6 * g + 0])
                for mt in range(8):
                    s, mj = mt // 2, mt % 2
                    if mt >= 3:
                        P.w(te, "DV", rev_base + 2 * (mt - 2))
                    for kt in range(NCT):
                        ins = te.matmul(P.PS[mt % 3][:, 0:GW],
                                        SL[0][:, s * 2048 + kt * 256 + mj * 128:
                                              s * 2048 + kt * 256 + mj * 128 + 128],
                                        MIX[1][:, kt, :],
                                        start=(kt == 0), stop=(kt == NCT - 1))
                    P.inc("TE", ins)
            emit_wload(block, 6 * g + 3, P.cnt["TE"])

            @block.vector
            def _(v, te0=te0, ef_ac=ef_ac):
                P.w(v, "AC", ef_ac)
                for mt in range(8):
                    P.w(v, "TE", te0 + mt + 1)
                    v.tensor_tensor(RT[:, mt, :], P.PS[mt % 3][:, 0:GW], EP[:, mt, :],
                                    ALU.mult)
                    ins = v.tensor_copy(RR[:, mt, :], P.PS[mt % 3][:, 0:GW])
                    P.inc("DV", ins, 2)

            mixk_dv = m2_dv0 + 72
            te0 = P.cnt["TE"]

            @block.tensor
            def _(te, mixk_dv=mixk_dv, g=g):
                P.w(te, "DV")
                P.w(te, "WD", wd_after[6 * g + 1])
                for mt in range(8):
                    s, mj = mt // 2, mt % 2
                    if mt >= 3:
                        P.w(te, "DV", P.cnt["DV"] + 3 * (mt - 2))
                    for kt in range(NCT):
                        ins = te.matmul(P.PS[mt % 3][:, 0:GW],
                                        SL[1][:, s * 2048 + kt * 256 + mj * 128:
                                              s * 2048 + kt * 256 + mj * 128 + 128],
                                        CBUF[:, kt, :],
                                        start=(kt == 0), stop=(kt == NCT - 1))
                    P.inc("TE", ins)
            emit_wload(block, 6 * g + 4, P.cnt["TE"])

            @block.vector
            def _(v, te0=te0):
                for mt in range(8):
                    P.w(v, "TE", te0 + mt + 1)
                    v.tensor_tensor(KT[:, mt, :], P.PS[mt % 3][:, 0:GW], EN[:, mt, :],
                                    ALU.mult)
                    v.tensor_tensor(RK[:, mt, :], RR[:, mt, :], P.PS[mt % 3][:, 0:GW],
                                    ALU.mult)
                    for c in range(4):
                        ins = v.tensor_scalar(KHF[:, mt, c * 64:(c + 1) * 64],
                                              KT[:, mt, c * 64:(c + 1) * 64],
                                              ALST[:, mt, c:c + 1], None, ALU.mult)
                    P.inc("DV", ins, 3)
            khf_dv = P.cnt["DV"]

            PT1 = P.PSB[0][:, 0:64].bitcast(BF16)
            PT2 = P.PSB[2][:, 0:64].bitcast(BF16)
            PT3 = P.PSB[3][:, 0:64].bitcast(BF16)
            PTS4 = [PT0, PT1, PT2, PT3]
            items_t = [(pr, mt) for pr in range(2) for mt in range(8)]
            for j in range(0, 16, 4):
                quad = items_t[j:j + 4]

                @block.tensor
                def _(te, quad=quad):
                    P.w(te, "DV")
                    P.w(te, "AC")
                    for qi, (pr, mt) in enumerate(quad):
                        ins = te.transpose(PTS4[qi], KHF[:, mt, pr * 128:(pr + 1) * 128],
                                           P.id128[:])
                    P.inc("TE", ins)
                trd = P.cnt["TE"]

                @block.vector
                def _(v, quad=quad, trd=trd):
                    P.w(v, "TE", trd)
                    for qi, (pr, mt) in enumerate(quad):
                        ins = v.tensor_copy(KHT[:, pr, mt * 128:(mt + 1) * 128], PTS4[qi][:])
                    P.inc("DV", ins)

            mixv_dv = m2_dv0 + 96
            te0 = P.cnt["TE"]

            vac0 = P.cnt["AC"]

            @block.tensor
            def _(te, mixv_dv=mixv_dv, vac0=vac0, g=g):
                P.w(te, "DV")
                P.w(te, "AC", vac0)
                P.w(te, "WD", wd_after[6 * g + 2])
                for tsub in range(2):
                    for n2 in range(2):
                        i = tsub * 2 + n2
                        if i >= 3:
                            P.w(te, "AC", vac0 + (i - 2))
                        for kt in range(NCT):
                            ins = te.matmul(P.PS[i % 3][:, 0:512],
                                            YSB[:, kt, tsub * 128:(tsub + 1) * 128],
                                            SL[2][:, kt * 1024 + n2 * 512:
                                                  kt * 1024 + (n2 + 1) * 512],
                                            start=(kt == 0), stop=(kt == NCT - 1))
                        P.inc("TE", ins)
            emit_wload(block, 6 * g + 5, P.cnt["TE"])

            @block.scalar
            def _(sc, te0=te0):
                for tsub in range(2):
                    for n2 in range(2):
                        i = tsub * 2 + n2
                        P.w(sc, "TE", te0 + i + 1)
                        ins = sc.copy(VTM[:, tsub, n2 * 512:(n2 + 1) * 512],
                                      P.PS[i % 3][:, 0:512])
                        P.inc("AC", ins)
            vtm_ac = P.cnt["AC"]

            vte0 = P.cnt["TE"]
            vac1 = P.cnt["AC"]

            @block.tensor
            def _(te, vte0=vte0, vac1=vac1, vtm_ac=vtm_ac, g=g):
                P.w(te, "WD", wd_after[6 * g + 3])
                P.w(te, "AC", vtm_ac)
                for mt in range(8):
                    s, mj = mt // 2, mt % 2
                    if mt >= 3:
                        P.w(te, "AC", vac1 + (mt - 2))
                    for kt in range(NCT):
                        ins = te.matmul(P.PS[mt % 3][:, 0:GW],
                                        SL[0][:, s * 2048 + kt * 256 + mj * 128:
                                              s * 2048 + kt * 256 + mj * 128 + 128],
                                        YSB[:, kt, :],
                                        start=(kt == 0), stop=(kt == NCT - 1))
                    P.inc("TE", ins)
            if 6 * g + 6 < 6 * NG:
                emit_wload(block, 6 * g + 6, P.cnt["TE"])

            @block.scalar
            def _(sc, vte0=vte0):
                for mt in range(8):
                    P.w(sc, "TE", vte0 + mt + 1)
                    ins = sc.copy(VF[:, mt, :], P.PS[mt % 3][:, 0:GW])
                    P.inc("AC", ins)

            mixg_dv = m2_dv0 + 120
            te0 = P.cnt["TE"]
            gac = P.cnt["AC"]

            @block.tensor
            def _(te, mixg_dv=mixg_dv, gac=gac, g=g):
                P.w(te, "DV")
                P.w(te, "AC", gac)
                P.w(te, "WD", wd_after[6 * g + 4])
                for mt in range(8):
                    s, mj = mt // 2, mt % 2
                    if mt >= 3:
                        P.w(te, "AC", gac + (mt - 2))
                    for kt in range(NCT):
                        ins = te.matmul(P.PS[mt % 3][:, 0:GW],
                                        SL[1][:, s * 2048 + kt * 256 + mj * 128:
                                              s * 2048 + kt * 256 + mj * 128 + 128],
                                        YNG[:, kt, :],
                                        start=(kt == 0), stop=(kt == NCT - 1))
                    P.inc("TE", ins)
            if 6 * g + 7 < 6 * NG:
                emit_wload(block, 6 * g + 7, P.cnt["TE"])

            @block.scalar
            def _(sc, te0=te0):
                for mt in range(8):
                    P.w(sc, "TE", te0 + mt + 1)
                    ins = sc.activation(GF[:, mt, :], P.PS[mt % 3][:, 0:GW], AF.Silu)
                    P.inc("AC", ins)

            te0 = P.cnt["TE"]

            dac0 = P.cnt["AC"]

            @block.tensor
            def _(te, khf_dv=khf_dv, dac0=dac0):
                P.w(te, "DV")
                P.w(te, "AC", dac0)
                for ct in range(NCT):
                    if ct >= 3:
                        P.w(te, "AC", dac0 + (ct - 2))
                    ins = te.matmul(P.PS[ct % 3][:, 0:GW], BUS[:, ct, :], RK[:, ct, :],
                                    start=True, stop=True)
                    P.inc("TE", ins)

            @block.scalar
            def _(sc, te0=te0):
                for ct in range(NCT):
                    P.w(sc, "TE", te0 + ct + 1)
                    ins = sc.copy(DB[:, ct, :], P.PS[ct % 3][:, 0:GW])
                    P.inc("AC", ins)
            db_ac = P.cnt["AC"]

            if g + 1 < NG:
                @block.gpsimd
                def _(g_, g=g):
                    g1 = (g + 1) * GW
                    h1 = ((g + 1) % 2) * GW
                    if g >= 1:
                        P.w(g_, "DV", dv_resid[g - 1])
                        P.w(g_, "OD", od_after[g - 1])
                    src = rin[:, g1:g1 + GW].rearrange("(ct p) c -> p ct c", p=128)
                    P.inc("XP", g_.dma_start(out=P.XG[:, :, h1:h1 + GW], in_=src), 16)
                self._aa_xd[g + 1] = P.cnt["XP"]
            xres_xd = self._aa_xd[g]

            for cig in range(4):
                par = (cig % 2) * 64
                dv0 = P.cnt["DV"]
                fill_done = [0]
                gstop_done = [0]

                @block.tensor
                def _(te, cig=cig, par=par, dv0=dv0, db_ac=db_ac,
                      fill_done=fill_done, gstop_done=gstop_done):
                    P.w(te, "DV", dv0)
                    P.w(te, "AC", db_ac)
                    Z = ZROW
                    if cig == 0:
                        for bank in range(4):
                            ins = te.matmul(P.PS[bank][:, 0:512], Z[0:1, 0:128],
                                            Z[0:1, 128:640], start=True, stop=False)
                            P.inc("TE", ins)
                    ins = te.matmul(P.PSB[1][:, 0:512], Z[0:1, 0:128], Z[0:1, 128:640],
                                    start=True, stop=False)
                    P.inc("TE", ins)
                    for h in range(HEADS):
                        ph = (h % 2) * 64
                        bank, col = h // 4, ((h // 2) % 2) * 256 + cig * 64
                        ins = te.matmul(P.PS[bank][ph:ph + 64, col:col + 64],
                                        SBF[ph:ph + 64, h // 2, :],
                                        RT[ph:ph + 64, h // 2, cig * 64:(cig + 1) * 64],
                                        start=False, stop=False)
                        P.inc("TE", ins)
                    for h in range(HEADS):
                        ph = (h % 2) * 64
                        bank = 2 + h % 2
                        first = h < 2
                        last = h >= 14
                        ins = te.matmul(P.PSB[bank][0:64, (h // 2) * 64:(h // 2) * 64 + 64],
                                        KT[ph:ph + 64, h // 2, cig * 64:(cig + 1) * 64],
                                        RT[ph:ph + 64, h // 2, cig * 64:(cig + 1) * 64],
                                        start=first, stop=last)
                        fill_done[0] = P.inc("TE", ins)
                    P.w(te, "DV", dv0 + 2)
                    for h in range(HEADS):
                        ph = (h % 2) * 64
                        bank, col = h // 4, ((h // 2) % 2) * 256 + cig * 64
                        ins = te.matmul(P.PS[bank][ph:ph + 64, col:col + 64],
                                        VTM[par:par + 64, cig // 2, h * 64:(h + 1) * 64],
                                        PTSB[par:par + 64, h % 2, (h // 2) * 64:(h // 2) * 64 + 64],
                                        start=False, stop=False)
                        P.inc("TE", ins)
                        ins = te.matmul(P.PSB[1][ph:ph + 64, (h // 2) * 64:(h // 2) * 64 + 64],
                                        KHT[par:par + 64, cig // 2, h * 64:(h + 1) * 64],
                                        VTM[par:par + 64, cig // 2, h * 64:(h + 1) * 64],
                                        start=False, stop=False)
                        P.inc("TE", ins)
                    ins = te.matmul(P.PSB[1][:, 0:512], Z[0:1, 0:128], Z[0:1, 128:640],
                                    start=False, stop=True)
                    gstop_done[0] = P.inc("TE", ins)
                    if cig == 3:
                        for bank in range(4):
                            ins = te.matmul(P.PS[bank][:, 0:512], Z[0:1, 0:128],
                                            Z[0:1, 128:640], start=False, stop=True)
                            P.inc("TE", ins)

                @block.vector
                def _(v, cig=cig, par=par, fill_done=fill_done, gstop_done=gstop_done):
                    P.w(v, "TE", fill_done[0])
                    wkb = WKM[par:par + 64, :].unsqueeze(1).broadcast_to([64, 8, 64])
                    for sl in range(2):
                        pview = P.PSB[2 + sl][0:64, 0:512].rearrange(
                            "p (a b) -> p a b", a=8)
                        dview = PTSB[par:par + 64, sl, :].rearrange(
                            "p (a b) -> p a b", a=8)
                        ins = v.tensor_tensor(dview, pview, wkb, ALU.mult)
                        P.inc("DV", ins)
                    P.w(v, "TE", gstop_done[0])
                    adec = ALST[:, :, cig:cig + 1].broadcast_to([128, 8, 64])
                    t3 = P.TMPF[0][:, 0:512].rearrange("p (a b) -> p a b", a=8)
                    g3 = P.PSB[1][:, 0:512].rearrange("p (a b) -> p a b", a=8)
                    v.tensor_tensor(t3, S, adec, ALU.mult)
                    v.tensor_tensor(S, t3, g3, ALU.add)
                    ins = v.tensor_copy(SBF, S)
                    P.inc("DV", ins)

            wkv_te = P.cnt["TE"]
            wkv_dv = P.cnt["DV"]

            @block.vector
            def _(v, wkv_te=wkv_te):
                P.w(v, "TE", wkv_te)
                for ct in range(NCT):
                    bank, col = ct // 2, (ct % 2) * 256
                    v.tensor_tensor(P.TMPF[ct % 2][:, 0:GW], DB[:, ct, :], VF[:, ct, :],
                                    ALU.mult)
                    ins = v.tensor_add(YSB[:, ct, :], P.TMPF[ct % 2][:, 0:GW],
                                       P.PS[bank][:, col:col + GW])
                    P.inc("DV", ins)
            yev_dv = P.cnt["DV"]

            @block.scalar
            def _(sc, yev_dv=yev_dv):
                P.w(sc, "DV", yev_dv)
                for ct in range(NCT):
                    ins = sc.activation(YSQ[:, ct, :], YSB[:, ct, :], AF.Square)
                    P.inc("AC", ins)
            ysq_ac = P.cnt["AC"]

            RW8 = self.ar_f32(0, 16384, (8, 512))
            banks = [P.PS[0], P.PS[1], P.PS[2], P.PS[3],
                     P.PSB[0], P.PSB[1], P.PSB[2], P.PSB[3]]
            gn_te0 = P.cnt["TE"]

            @block.tensor
            def _(te, ysq_ac=ysq_ac, banks=banks):
                P.w(te, "AC", ysq_ac)
                P.w(te, "DV")
                for ct in range(NCT):
                    te.matmul(banks[ct][0:2, 0:GW], BD2[:, 0:2], YSB[:, ct, :],
                              start=True, stop=False)
                    ins = te.matmul(banks[ct][0:2, 256:256 + GW], BD2[:, 0:2],
                                    YSQ[:, ct, :], start=False, stop=True)
                P.inc("TE", ins)
            gn_st = P.cnt["TE"]

            @block.vector
            def _(v, gn_st=gn_st, banks=banks):
                P.w(v, "TE", gn_st)
                for ct in range(NCT):
                    v.tensor_scalar_mul(RW8[0:2, ct, 0:GW], banks[ct][0:2, 0:GW],
                                        1.0 / 64)
                    v.tensor_scalar_mul(RW8[0:2, ct, 256:256 + GW],
                                        banks[ct][0:2, 256:256 + GW], 1.0 / 64)
                    v.tensor_tensor(P.TMPF[0][0:2, 0:GW], RW8[0:2, ct, 0:GW],
                                    RW8[0:2, ct, 0:GW], ALU.mult)
                    ins = v.tensor_sub(RW8[0:2, ct, 256:256 + GW],
                                       RW8[0:2, ct, 256:256 + GW],
                                       P.TMPF[0][0:2, 0:GW])
                P.inc("DV", ins)
            gn_rows = P.cnt["DV"]

            @block.scalar
            def _(sc, gn_rows=gn_rows):
                P.w(sc, "DV", gn_rows)
                sc.activation(RW8[0:2, :, 256:512], RW8[0:2, :, 256:512], AF.Ln,
                              bias=P.EPSC[0:2, 2:3])
                ins = sc.activation(RW8[0:2, :, 256:512], RW8[0:2, :, 256:512], AF.Exp,
                                    scale=-0.5)
                P.inc("AC", ins)
            gn_rstd = P.cnt["AC"]

            @block.tensor
            def _(te, gn_rstd=gn_rstd, banks=banks):
                P.w(te, "AC", gn_rstd)
                P.w(te, "DV")
                for ct in range(NCT):
                    te.matmul(banks[ct][:, 0:GW], EX2[0:2, :], RW8[0:2, ct, 0:GW],
                              start=True, stop=False)
                    ins = te.matmul(banks[ct][:, 256:256 + GW], EX2[0:2, :],
                                    RW8[0:2, ct, 256:256 + GW], start=False, stop=True)
                P.inc("TE", ins)
            gn_bc = P.cnt["TE"]

            @block.vector
            def _(v, gn_bc=gn_bc, banks=banks):
                P.w(v, "TE", gn_bc)
                for ct in range(NCT):
                    v.tensor_sub(P.TMPF[ct % 2][:, 0:GW], YSB[:, ct, :],
                                 banks[ct][:, 0:GW])
                    v.tensor_tensor(P.TMPF[ct % 2][:, 0:GW], P.TMPF[ct % 2][:, 0:GW],
                                    banks[ct][:, 256:256 + GW], ALU.mult)
                    v.tensor_scalar(P.TMPF[ct % 2][:, 0:GW], P.TMPF[ct % 2][:, 0:GW],
                                    GB[:, 0, ct:ct + 1], GB[:, 1, ct:ct + 1],
                                    ALU.mult, ALU.add)
                    ins = v.tensor_tensor(YNG[:, ct, :], P.TMPF[ct % 2][:, 0:GW],
                                          GF[:, ct, :], ALU.mult)
                P.inc("DV", ins)
            yng_dv = P.cnt["DV"]

            if g + 1 < NG:
                aa_lnA(g + 1)
            woff = 2 if g + 1 < NG else 0
            boff = 1 if g + 1 < NG else 0
            te0 = P.cnt["TE"]

            @block.tensor
            def _(te, yng_dv=yng_dv, te0=te0, g=g):
                P.w(te, "DV", yng_dv)
                P.w(te, "WD", wd_after[6 * g + 5])
                for mt in range(3):
                    s, mj = mt // 2, mt % 2
                    for kt in range(NCT):
                        ins = te.matmul(P.PS[mt % 3][:, 0:GW],
                                        SL[2][:, s * 2048 + kt * 256 + mj * 128:
                                              s * 2048 + kt * 256 + mj * 128 + 128],
                                        YNG[:, kt, :],
                                        start=(kt == 0), stop=(kt == NCT - 1))
                    P.inc("TE", ins)

            if g + 1 < NG:
                aa_lnB(g + 1)

            @block.tensor
            def _(te, yng_dv=yng_dv, te0=te0, g=g, woff=woff):
                for mt in range(3, 8):
                    s, mj = mt // 2, mt % 2
                    P.w(te, "DV", yng_dv + woff + (mt - 2))
                    for kt in range(NCT):
                        ins = te.matmul(P.PS[mt % 3][:, 0:GW],
                                        SL[2][:, s * 2048 + kt * 256 + mj * 128:
                                              s * 2048 + kt * 256 + mj * 128 + 128],
                                        YNG[:, kt, :],
                                        start=(kt == 0), stop=(kt == NCT - 1))
                    P.inc("TE", ins)
            if 6 * g + 8 < 6 * NG:
                emit_wload(block, 6 * g + 8, P.cnt["TE"])

            @block.vector
            def _(v, te0=te0, xres_xd=xres_xd, XGH=XGH, boff=boff):
                P.w(v, "XP", xres_xd)
                for mt in range(8):
                    P.w(v, "TE", te0 + mt + 1 + (boff if mt >= 3 else 0))
                    ins = v.tensor_add(XGH[:, mt, :], XGH[:, mt, :],
                                       P.PS[mt % 3][:, 0:GW])
                    P.inc("DV", ins)
            out_dv = P.cnt["DV"]
            dv_resid[g] = out_dv

            @block.sync
            def _(sy, g0=g0, out_dv=out_dv, XGH=XGH):
                P.w(sy, "DV", out_dv)
                dst = rout[:, g0:g0 + GW].rearrange("(ct p) c -> p ct c", p=128)
                P.inc("OD", sy.dma_start(out=dst, in_=XGH[:, :, :]), 16)
            od_after[g] = P.cnt["OD"]
    def _rope_qknorm(self, block, src_view4, nh, tt, out_bf4, COS, SIN):
        P = self
        T1 = self.ar_f32(1, 47616, (nh, 32))
        T2 = self.ar_f32(1, 49664, (nh, 32))
        T3 = self.ar_f32(1, 51712, (nh, 32))
        S2 = self._shape_view(
            self.WS[1][:, :].bitcast(F32)[:, 1024:1024 + nh * 32], (nh, 32))
        RSTQ = self.ar_f32(1, 53760, (16,))
        e = src_view4[:, :, :, 0]
        o = src_view4[:, :, :, 1]
        cosB = COS[:, tt * 32:(tt + 1) * 32].unsqueeze(1).broadcast_to([128, nh, 32])
        sinB = SIN[:, tt * 32:(tt + 1) * 32].unsqueeze(1).broadcast_to([128, nh, 32])

        @block.vector
        def _(v):
            P.w(v, "AC")
            P.w(v, "TE")
            v.tensor_tensor(T1, e, cosB, ALU.mult)
            v.tensor_tensor(T2, e, sinB, ALU.mult)
            v.tensor_tensor(T3, o, sinB, ALU.mult)
            v.tensor_sub(T1, T1, T3)
            v.tensor_tensor(T3, o, cosB, ALU.mult)
            v.tensor_add(T2, T2, T3)
            v.tensor_tensor(T3, T1, T1, ALU.mult)
            v.tensor_tensor(S2, T2, T2, ALU.mult)
            v.tensor_add(T3, T3, S2)
            ins = v.reduce_sum(RSTQ[:, 0:nh], T3, axis=mybir.AxisListType.X)
            P.inc("DV", ins)
        dv1 = P.cnt["DV"]

        @block.scalar
        def _(sc):
            P.w(sc, "DV", dv1)
            sc.activation(RSTQ[:, 0:nh], RSTQ[:, 0:nh], AF.Ln,
                          bias=P.EPSC[:, 1:2], scale=1.0 / 64.0)
            ins = sc.activation(RSTQ[:, 0:nh], RSTQ[:, 0:nh], AF.Exp, scale=-0.5)
            P.inc("AC", ins)
        ac1 = P.cnt["AC"]

        @block.vector
        def _(v):
            P.w(v, "AC", ac1)
            rstqB = RSTQ[:, 0:nh].unsqueeze(2).broadcast_to([128, nh, 32])
            v.tensor_tensor(out_bf4[:, :, :, 0], T1, rstqB, ALU.mult)
            ins = v.tensor_tensor(out_bf4[:, :, :, 1], T2, rstqB, ALU.mult)
            P.inc("DV", ins)

    def emit_attnb(self, block, rin, rout):
        P = self
        LATF = self.ar_bf(0, 0, (2, 2048))
        KF2 = self.ar_bf(0, 8192, (4, 2048))
        VT = self.ar_bf(0, 24576, (16, 260))
        QFS = [self.ar_bf(0, 32896, (8, 512)), self.ar_bf(0, 41088, (8, 512))]
        PTS = self.ar_bf(0, 58496, (4, 512))
        YF = self.ar_bf(0, 49280, (8, 512))
        YSB = self.ar_bf(0, 57472, (512,))
        WQR = self.ar_bf(1, 0, (8, 1024))
        WO = self.ar_bf(1, 16384, (2, 4096))
        KVD = self.ar_bf(1, 32768, (2048,))
        KUPR = self.ar_bf(1, 36864, (2, 256))
        VUPR = self.ar_bf(1, 37888, (2, 256))
        COS = self.ar_f32(1, 38912, (512,))
        SIN = self.ar_f32(1, 40960, (512,))
        GM = self.ar_bf(1, 43008, (4, 512))
        LWKR = self.ar_bf(1, 32768, (8, 256))
        LWVR = self.ar_bf(1, 43008, (8, 256))
        LM = self.ar_bf(1, 53824 + 2048, (6, 512))
        QNB = self.ar_bf(1, 53824, (1024,))
        WS1F = self.WS[1][:, :].bitcast(F32)
        QTF = WS1F[:, 0:1024]
        RSC = WS1F[0:1, 1536:2048]
        FHB = [self.ar_f32(0, 0, (4, 512)),
               self._shape_view(WS1F[:, 2048:4096], (4, 512))]
        KTF = self.ar_f32(0, 62592, (256,))
        PT0 = self.PS[3][:, 0:64].bitcast(BF16)
        PT1B = self.PS[2][:, 0:64].bitcast(BF16)
        PT2B = self.PS[3][:, 64:128].bitcast(BF16)
        PT3B = self.PS[2][:, 64:128].bitcast(BF16)
        XG2 = self.WS[0][:, :].bitcast(F32).rearrange("p (a b) -> p a b", a=8)
        ab_od = {}

        for phase in getattr(self, "ab_phases", ("g", "l")):
            self.barrier(block)

            @block.gpsimd
            def _(g_, phase=phase):
                if phase == "g":
                    P.inc("WD", g_.dma_start(out=WQR, in_=P.dram["ab_g_wq_r"]
                                             .rearrange("a p b -> p a b")), 16)
                    P.inc("WD", g_.dma_start(out=KVD, in_=P.dram["ab_g_kvd"][0]), 16)
                    P.inc("WD", g_.dma_start(out=KUPR, in_=P.dram["ab_g_kup_r"]
                                             .rearrange("a p b -> p a b")), 16)
                    P.inc("WD", g_.dma_start(out=VUPR, in_=P.dram["ab_g_vup_r"]
                                             .rearrange("a p b -> p a b")), 16)
                    P.inc("WD", g_.dma_start(out=WO, in_=P.dram["ab_g_wo"]
                                             .rearrange("a p b -> p a b")), 16)
                    P.inc("WD", g_.dma_start(out=COS, in_=P.dram["cos_g"]), 16)
                    P.inc("WD", g_.dma_start(out=SIN, in_=P.dram["sin_g"]), 16)
                    P.inc("WD", g_.dma_start(
                        out=GM, in_=P.dram["gmask"].rearrange("(a p) b -> p a b", p=128)), 16)
                    P.inc("WD", g_.dma_start(out=P.SCC[:], in_=P.dram["ab_norm1"][:]), 16)
                else:
                    P.inc("WD", g_.dma_start(out=WQR, in_=P.dram["ab_l_wq_r"]
                                             .rearrange("a p b -> p a b")), 16)
                    P.inc("WD", g_.dma_start(out=LWKR, in_=P.dram["ab_l_wk_r"]
                                             .rearrange("a p b -> p a b")), 16)
                    P.inc("WD", g_.dma_start(out=LWVR, in_=P.dram["ab_l_wv_r"]
                                             .rearrange("a p b -> p a b")), 16)
                    P.inc("WD", g_.dma_start(out=WO, in_=P.dram["ab_l_wo"]
                                             .rearrange("a p b -> p a b")), 16)
                    P.inc("WD", g_.dma_start(out=COS, in_=P.dram["cos_l"]), 16)
                    P.inc("WD", g_.dma_start(out=SIN, in_=P.dram["sin_l"]), 16)
                    P.inc("WD", g_.dma_start(
                        out=LM, in_=P.dram["lmask"].rearrange("(a p) b -> p a b", p=128)), 16)
                    P.inc("WD", g_.dma_start(out=P.SCC[:], in_=P.dram["ab_norm2"][:]), 16)
            wts = P.cnt["WD"]

            @block.vector
            def _(v):
                P.w(v, "TE")
                ins = v.memset(self._shape_view(VT.rearrange("p a b -> p (a b)"),
                                                (16, 4, 65))[:, :, :, 64:65], 1.0)
                P.inc("DV", ins)


            self._ab_xd = {}
            self._prep_done = {}
            self._yh_rd = {0: [], 1: []}
            self._psb_rd = {0: [], 1: []}
            self._prev_evac = [None]
            self._pend_bc = [None]
            self._pend_norm = [None]
            self._bc_te = {}
            self._norm_dv = {}

            def emit_load(g, phase=phase):
                XGB = (P.XG, XG2)[g % 2]
                srcT = rin if phase == "g" else self.dram["ures"]

                @block.sync
                def _(sy, g=g, XGB=XGB, srcT=srcT):
                    if g >= 2:
                        P.w(sy, "OD", ab_od[(phase, g - 2)])
                    else:
                        P.w(sy, "OD")
                    src = srcT[:, g * 512:(g + 1) * 512].rearrange(
                        "(ct p) c -> p ct c", p=128)
                    P.inc("XD", sy.dma_start(out=XGB[:, :, :], in_=src), 16)
                self._ab_xd[g] = P.cnt["XD"]

            def emit_fx_load(g):
                @block.sync
                def _(sy, g=g):
                    for hf in range(2):
                        src = rin[hf * 512:(hf + 1) * 512, g * 512:(g + 1) * 512] \
                            .rearrange("(ct p) c -> p ct c", p=128)
                        P.inc("XD", sy.dma_start(out=FHB[hf][:, :, :], in_=src), 16)
                        self._fx_xd[(g, hf)] = P.cnt["XD"]

            def prep_pieces(g, phase=phase):
                g0 = g * 512
                XGB = (P.XG, XG2)[g % 2]
                QF = QFS[g % 2]
                pieces = []

                def p_ln():
                    xd = self._ab_xd[g]
                    if phase == "g":
                        te0 = P.cnt["TE"]
                        ac0 = P.cnt["AC"]

                        @block.scalar
                        def _(sc, xd=xd, te0=te0, XGB=XGB):
                            P.w(sc, "XD", xd)
                            P.w(sc, "DV")
                            for ct in range(NCT):
                                if ct >= 2:
                                    P.w(sc, "TE", te0 + (ct - 1))
                                P.inc("AC", sc.activation(P.TMPF[ct % 2][:],
                                                          XGB[:, ct, :], AF.Square))

                        @block.tensor
                        def _(te, xd=xd, ac0=ac0, XGB=XGB):
                            P.w(te, "XD", xd)
                            P.w(te, "DV")
                            for ct in range(NCT):
                                te.matmul(P.PS[2][0:1, 0:512], P.ones_f[:],
                                          XGB[:, ct, :],
                                          start=(ct == 0), stop=(ct == NCT - 1))
                            for ct in range(NCT):
                                P.w(te, "AC", ac0 + ct + 1)
                                ins = te.matmul(P.PS[3][0:1, 0:512], P.ones_f[:],
                                                P.TMPF[ct % 2][:],
                                                start=(ct == 0), stop=(ct == NCT - 1))
                                P.inc("TE", ins)
                        tes = P.cnt["TE"]

                        @block.vector
                        def _(v, tes=tes):
                            P.w(v, "TE", tes)
                            R = P.ROWS
                            v.tensor_scalar_mul(R[0:1, 512:1024], P.PS[3][0:1, 0:512],
                                                1.0 / D)
                            v.tensor_scalar_mul(RSC, P.PS[2][0:1, 0:512], 1.0 / D)
                            v.tensor_tensor(R[0:1, 1024:1536], RSC, RSC, ALU.mult)
                            ins = v.tensor_sub(R[0:1, 512:1024], R[0:1, 512:1024],
                                               R[0:1, 1024:1536])
                            P.inc("DV", ins)
                        dvs = P.cnt["DV"]

                        @block.scalar
                        def _(sc, dvs=dvs):
                            P.w(sc, "DV", dvs)
                            R = P.ROWS
                            sc.activation(R[0:1, 512:1024], R[0:1, 512:1024], AF.Ln,
                                          bias=P.EPSC[0:1, 0:1])
                            ins = sc.activation(R[0:1, 512:1024], R[0:1, 512:1024],
                                                AF.Exp, scale=-0.5)
                            P.inc("AC", ins)
                        ac2 = P.cnt["AC"]

                        @block.tensor
                        def _(te, ac2=ac2):
                            P.w(te, "AC", ac2)
                            P.w(te, "DV")
                            te.matmul(P.PS[3][:, 0:512], P.ones_row[:],
                                      P.ROWS[0:1, 512:1024], start=True, stop=True)
                            ins = te.matmul(P.PS[2][:, 0:512], P.ones_row[:],
                                            RSC, start=True, stop=True)
                            P.inc("TE", ins)
                        bcd = P.cnt["TE"]

                        @block.vector
                        def _(v, bcd=bcd, XGB=XGB):
                            P.w(v, "TE", bcd)
                            for ct in range(NCT):
                                v.tensor_sub(P.TMPF[ct % 2][:], XGB[:, ct, :],
                                             P.PS[2][:, 0:512])
                                ins = v.tensor_tensor(XGB[:, ct, :], P.TMPF[ct % 2][:],
                                                      P.PS[3][:, 0:512], ALU.mult)
                            P.inc("DV", ins)

                def p_rms():
                    xd = self._ab_xd[g]
                    te0 = P.cnt["TE"]
                    ac0 = P.cnt["AC"]
                    dv0 = P.cnt["DV"]

                    @block.scalar
                    def _(sc, xd=xd, te0=te0, dv0=dv0, XGB=XGB):
                        P.w(sc, "XD", xd)
                        P.w(sc, "DV", dv0)
                        for ct in range(NCT):
                            if ct >= 2:
                                P.w(sc, "TE", te0 + (ct - 1))
                            P.inc("AC", sc.activation(P.TMPF[ct % 2][:], XGB[:, ct, :],
                                                      AF.Square))

                    @block.tensor
                    def _(te, ac0=ac0, dv0=dv0):
                        P.w(te, "DV", dv0)
                        for ct in range(NCT):
                            P.w(te, "AC", ac0 + ct + 1)
                            ins = te.matmul(P.PS[3][0:1, 0:512], P.ones_f[:],
                                            P.TMPF[ct % 2][:],
                                            start=(ct == 0), stop=(ct == NCT - 1))
                            P.inc("TE", ins)
                    tes = P.cnt["TE"]

                    @block.vector
                    def _(v, tes=tes):
                        P.w(v, "TE", tes)
                        ins = v.tensor_scalar_mul(P.ROWS[0:1, 512:1024],
                                                  P.PS[3][0:1, 0:512], 1.0 / D)
                        P.inc("DV", ins)
                    dvs = P.cnt["DV"]

                    @block.scalar
                    def _(sc, dvs=dvs):
                        P.w(sc, "DV", dvs)
                        R = P.ROWS
                        sc.activation(R[0:1, 512:1024], R[0:1, 512:1024], AF.Ln,
                                      bias=P.EPSC[0:1, 1:2])
                        ins = sc.activation(R[0:1, 512:1024], R[0:1, 512:1024], AF.Exp,
                                            scale=-0.5)
                        P.inc("AC", ins)
                    ac2 = P.cnt["AC"]

                    @block.tensor
                    def _(te, ac2=ac2):
                        P.w(te, "AC", ac2)
                        P.w(te, "DV")
                        ins = te.matmul(P.PS[3][:, 0:512], P.ones_row[:],
                                        P.ROWS[0:1, 512:1024], start=True, stop=True)
                        P.inc("TE", ins)
                    bcd = P.cnt["TE"]

                    @block.vector
                    def _(v, bcd=bcd, XGB=XGB):
                        P.w(v, "TE", bcd)
                        P.w(v, "WD", wts)
                        for ct in range(NCT):
                            ins = v.scalar_tensor_tensor(P.XN[:, ct, 0:512],
                                                         XGB[:, ct, :],
                                                         P.SCC[:, ct:ct + 1],
                                                         P.PS[3][:, 0:512],
                                                         ALU.mult, ALU.mult)
                        P.inc("DV", ins)

                def p_latent():
                    if phase != "g":
                        return
                    xn_ok = P.cnt["DV"]

                    @block.tensor
                    def _(te, xn_ok=xn_ok, g0=g0):
                        P.w(te, "DV", xn_ok)
                        P.w(te, "AC")
                        for lt in range(2):
                            for kt in range(NCT):
                                ins = te.matmul(
                                    P.PS[2 + lt][:, 0:512],
                                    KVD[:, kt * 256 + lt * 128:kt * 256 + lt * 128 + 128],
                                    P.XN[:, kt, 0:512],
                                    start=(kt == 0), stop=(kt == NCT - 1))
                            P.inc("TE", ins)
                    lat_te = P.cnt["TE"]

                    @block.vector
                    def _(v, lat_te=lat_te, g0=g0):
                        P.w(v, "TE", lat_te)
                        for lt in range(2):
                            ins = v.tensor_copy(LATF[:, lt, g0:g0 + 512],
                                                P.PS[2 + lt][:, 0:512])
                            P.inc("DV", ins)

                def mk_tsub(tsub):
                    tt = g * 4 + tsub
                    rec = {}

                    def p_kv_a(tsub=tsub, tt=tt, rec=rec):
                        @block.tensor
                        def _(te, tsub=tsub, g0=g0):
                            P.w(te, "DV")
                            if phase == "g":
                                for lt in range(2):
                                    ins = te.matmul(
                                        P.PS[2][:, 0:256],
                                        LATF[:, lt, g0 + tsub * 128:g0 + tsub * 128 + 128],
                                        KUPR[:, lt, :],
                                        start=(lt == 0), stop=(lt == 1))
                            else:
                                for kt in range(NCT):
                                    ins = te.matmul(
                                        P.PS[2][:, 0:256],
                                        P.XN[:, kt, tsub * 128:tsub * 128 + 128],
                                        LWKR[:, kt, :],
                                        start=(kt == 0), stop=(kt == NCT - 1))
                            P.inc("TE", ins)
                        kf_te = P.cnt["TE"]

                        @block.vector
                        def _(v, kf_te=kf_te):
                            P.w(v, "TE", kf_te)
                            ins = v.tensor_copy(KTF, P.PS[2][:, 0:256])
                            P.inc("DV", ins)
                        rec["kcp"] = P.cnt["DV"]

                        kview = self._shape_view(KTF, (4, 32, 2))
                        kout = self._shape_view(P.EV[0][:, 0:256], (4, 32, 2))
                        self._rope_qknorm(block, kview, 4, tt, kout, COS, SIN)

                    def p_q_a(tsub=tsub, tt=tt, rec=rec):
                        @block.tensor
                        def _(te, tsub=tsub, rec=rec):
                            P.w(te, "DV", rec["kcp"])
                            for n2 in range(2):
                                for kt in range(NCT):
                                    ins = te.matmul(
                                        P.PS[2 + n2][:, 0:512],
                                        P.XN[:, kt, tsub * 128:tsub * 128 + 128],
                                        WQR[:, kt, n2 * 512:(n2 + 1) * 512],
                                        start=(kt == 0), stop=(kt == NCT - 1))
                                P.inc("TE", ins)
                        qte = P.cnt["TE"]

                        @block.vector
                        def _(v, qte=qte):
                            P.w(v, "TE", qte)
                            v.tensor_copy(QTF[:, 0:512], P.PS[2][:, 0:512])
                            ins = v.tensor_copy(QTF[:, 512:1024], P.PS[3][:, 0:512])
                            P.inc("DV", ins)
                        rec["qtf"] = P.cnt["DV"]

                        qview = self._shape_view(QTF, (16, 32, 2))
                        qout = self._shape_view(QNB, (16, 32, 2))
                        self._rope_qknorm(block, qview, 16, tt, qout, COS, SIN)

                    def p_kv_b(tsub=tsub, tt=tt, rec=rec):
                        @block.tensor
                        def _(te, rec=rec):
                            P.w(te, "DV", rec["qtf"])
                            te.transpose(PT0, P.EV[0][:, 0:128], P.id128[:])
                            ins = te.transpose(PT1B, P.EV[0][:, 128:256], P.id128[:])
                            P.inc("TE", ins)
                        trd = P.cnt["TE"]

                        @block.vector
                        def _(v, trd=trd, tt=tt):
                            P.w(v, "TE", trd)
                            c0 = tt * 128
                            for j, PT in ((0, PT0), (1, PT1B)):
                                v.tensor_copy(KF2[0:64, 2 * j, c0:c0 + 128], PT[0:64, :])
                                v.tensor_copy(KF2[64:128, 2 * j, c0:c0 + 128], PT[0:64, :])
                                v.tensor_copy(KF2[0:64, 2 * j + 1, c0:c0 + 128],
                                              PT[64:128, :])
                                ins = v.tensor_copy(KF2[64:128, 2 * j + 1, c0:c0 + 128],
                                                    PT[64:128, :])
                            P.inc("DV", ins)
                        rec["kf2"] = P.cnt["DV"]

                        @block.tensor
                        def _(te, tsub=tsub, g0=g0, rec=rec):
                            P.w(te, "DV", rec["kf2"])
                            if phase == "g":
                                for lt in range(2):
                                    ins = te.matmul(
                                        P.PS[3][:, 0:256],
                                        LATF[:, lt, g0 + tsub * 128:g0 + tsub * 128 + 128],
                                        VUPR[:, lt, :],
                                        start=(lt == 0), stop=(lt == 1))
                            else:
                                for kt in range(NCT):
                                    ins = te.matmul(
                                        P.PS[3][:, 0:256],
                                        P.XN[:, kt, tsub * 128:tsub * 128 + 128],
                                        LWVR[:, kt, :],
                                        start=(kt == 0), stop=(kt == NCT - 1))
                            P.inc("TE", ins)
                        vte = P.cnt["TE"]

                        @block.vector
                        def _(v, vte=vte, tt=tt):
                            P.w(v, "TE", vte)
                            for h in range(4):
                                ins = v.tensor_copy(VT[:, tt, h * 65:h * 65 + 64],
                                                    P.PS[3][:, h * 64:(h + 1) * 64])
                            P.inc("DV", ins)
                        rec["vev"] = P.cnt["DV"]

                    def p_q_b(tsub=tsub, tt=tt, rec=rec):
                        for j in range(0, 8, 2):
                            @block.tensor
                            def _(te, j=j, rec=rec):
                                if j == 0:
                                    P.w(te, "DV", rec["vev"])
                                else:
                                    P.w(te, "DV")
                                te.transpose(PT0, QNB[:, j * 128:(j + 1) * 128],
                                             P.id128[:])
                                ins = te.transpose(PT1B,
                                                   QNB[:, (j + 1) * 128:(j + 2) * 128],
                                                   P.id128[:])
                                P.inc("TE", ins)
                            trd = P.cnt["TE"]

                            @block.vector
                            def _(v, j=j, trd=trd, tsub=tsub, QF=QF):
                                P.w(v, "TE", trd)
                                v.tensor_copy(QF[:, j, tsub * 128:tsub * 128 + 128],
                                              PT0[:])
                                ins = v.tensor_copy(
                                    QF[:, j + 1, tsub * 128:tsub * 128 + 128], PT1B[:])
                                P.inc("DV", ins)
                        if tsub == 3:
                            self._prep_done[(phase, g)] = P.cnt["DV"]
                    return p_kv_a, p_q_a, p_kv_b, p_q_b

                pieces.append(p_ln)
                pieces.append(p_rms)
                pieces.append(p_latent)
                for tsub in range(4):
                    pieces.extend(mk_tsub(tsub))
                return pieces

            def emit_attention(g, feeder, phase=phase):
                g0 = g * 512
                XGB = (P.XG, XG2)[g % 2]
                QF = QFS[g % 2]
                nfeed = len(feeder)
                fed = [0]

                def feed(k):
                    while fed[0] < min(nfeed, k):
                        feeder[fed[0]]()
                        fed[0] += 1

                for h in range(HEADS):
                    kvh = h // 4
                    b = (h % 2) * 64
                    if phase == "g":
                        kts = list(range(0, 4 * g + 4))
                    else:
                        kts = [kt for kt in range(4 * g - 2, 4 * g + 4) if kt >= 0]
                    n = len(kts)
                    nb = (n + 1) // 2
                    te0 = P.cnt["TE"]
                    ac0 = P.cnt["AC"]
                    dv0 = P.cnt["DV"]
                    YH = P.PSB[2 + h % 2][0:65, 0:512]
                    masked = [i for i in range(n)
                              if phase == "l" or kts[i] >= 4 * g]
                    evac_ac = {i: ac0 + i + 1 for i in range(n)}
                    mask_dv = {i: dv0 + masked.index(i) + 1 for i in masked}
                    fill_te = {}
                    pv_te = {}

                    @block.tensor
                    def _(te, h=h, g=g, kvh=kvh, b=b, kts=kts, n=n, nb=nb,
                          evac_ac=evac_ac, mask_dv=mask_dv, fill_te=fill_te,
                          pv_te=pv_te, YH=YH, QF=QF, phase=phase):
                        if h == 0:
                            P.w(te, "DV", self._prep_done[(phase, g)])
                        if self._prev_evac[0] is not None:
                            P.w(te, "AC", self._prev_evac[0])
                        def fills(bi):
                            for i in range(bi * 2, min(n, bi * 2 + 2)):
                                if i >= 2:
                                    P.w(te, "AC", evac_ac[i - 2])
                                ins = te.matmul(
                                    P.PS[i % 2][:, 0:512],
                                    KF2[b:b + 64, kvh, kts[i] * 128:(kts[i] + 1) * 128],
                                    QF[b:b + 64, h // 2, 0:512],
                                    start=True, stop=True)
                                fill_te[i] = P.inc("TE", ins)
                        def pvs(bi):
                            for i in range(bi * 2, min(n, bi * 2 + 2)):
                                if i == 0:
                                    for sem, cnt in self._yh_rd[h % 2]:
                                        P.w(te, sem, cnt)
                                if i in mask_dv:
                                    P.w(te, "DV", mask_dv[i])
                                else:
                                    P.w(te, "AC", evac_ac[i])
                                ins = te.matmul(YH[:, 0:512],
                                                VT[:, kts[i], kvh * 65:kvh * 65 + 65],
                                                PTS[:, i % 4, 0:512],
                                                start=(i == 0), stop=(i == n - 1))
                                pv_te[i] = P.inc("TE", ins)
                        fills(0)
                        for bi in range(nb):
                            if bi + 1 < nb:
                                fills(bi + 1)
                            pvs(bi)
                            if bi == nb - 2 and self._pend_bc[0] is not None:
                                self._pend_bc[0](te)
                                self._pend_bc[0] = None
                    self._prev_evac[0] = evac_ac[n - 1]

                    @block.scalar
                    def _(sc, kts=kts, n=n, h=h, fill_te=fill_te, pv_te=pv_te):
                        for i in range(n):
                            P.w(sc, "TE", fill_te[i])
                            if i >= 4:
                                P.w(sc, "TE", pv_te[i - 4])
                            ins = sc.activation(PTS[:, i % 4, 0:512],
                                                P.PS[i % 2][:, 0:512],
                                                AF.Exp, scale=0.125)
                            P.inc("AC", ins)

                    @block.vector
                    def _(v, kts=kts, n=n, g=g, phase=phase, evac_ac=evac_ac,
                          masked=masked):
                        for i in masked:
                            kt = kts[i]
                            P.w(v, "AC", evac_ac[i])
                            if phase == "g":
                                ins = v.tensor_tensor(PTS[:, i % 4, :], PTS[:, i % 4, :],
                                                      GM[:, kt - 4 * g, :], ALU.mult)
                            else:
                                ins = v.tensor_tensor(PTS[:, i % 4, :], PTS[:, i % 4, :],
                                                      LM[:, kt - 4 * g + 2, :], ALU.mult)
                            P.inc("DV", ins)

                    if self._pend_norm[0] is not None:
                        self._pend_norm[0]()
                        self._pend_norm[0] = None

                    pv_done = P.cnt["TE"]

                    @block.scalar
                    def _(sc, pv_done=pv_done, h=h, YH=YH):
                        P.w(sc, "TE", pv_done)
                        sc.activation(P.ROWS[0:1, 0:512], YH[64:65, :], AF.Ln)
                        ins = sc.activation(P.ROWS[0:1, 0:512], P.ROWS[0:1, 0:512],
                                            AF.Exp, scale=-1.0)
                        P.inc("AC", ins)
                    sums_ac = P.cnt["AC"]

                    @block.vector
                    def _(v, pv_done=pv_done, h=h, YH=YH):
                        P.w(v, "TE", pv_done)
                        ins = v.tensor_copy(YSB[0:64, 0:512], YH[0:64, :])
                        P.inc("DV", ins)
                    rec_dv = P.cnt["DV"]
                    self._yh_rd[h % 2] = [("DV", rec_dv), ("AC", sums_ac)]

                    def mk_bc(h=h, rec_dv=rec_dv, sums_ac=sums_ac):
                        def emit_bc(te):
                            P.w(te, "DV", rec_dv)
                            P.w(te, "AC", sums_ac)
                            for sem, cnt in self._psb_rd[h % 2]:
                                P.w(te, sem, cnt)
                            ins = te.matmul(P.PSB[h % 2][0:64, 0:512],
                                            P.ones_row[0:1, 0:64],
                                            P.ROWS[0:1, 0:512], start=True, stop=True)
                            P.inc("TE", ins)
                            self._bc_te[h] = P.cnt["TE"]
                        return emit_bc

                    def mk_norm(h=h):
                        def emit_norm():
                            @block.vector
                            def _(v, h=h):
                                P.w(v, "TE", self._bc_te[h])
                                ins = v.tensor_tensor(
                                    YF[(h % 2) * 64:(h % 2) * 64 + 64, h // 2, :],
                                    YSB[0:64, 0:512],
                                    P.PSB[h % 2][0:64, 0:512], ALU.mult)
                                P.inc("DV", ins)
                            self._norm_dv[h] = P.cnt["DV"]
                            self._psb_rd[h % 2] = [("DV", self._norm_dv[h])]
                        return emit_norm

                    self._pend_bc[0] = mk_bc()
                    self._pend_norm[0] = mk_norm()

                    if h >= 5:
                        feed(((h - 4) * nfeed + (HEADS - 6)) // (HEADS - 5))

                @block.tensor
                def _(te):
                    self._pend_bc[0](te)
                    self._pend_bc[0] = None
                self._pend_norm[0]()
                self._pend_norm[0] = None

                feed(nfeed)

                te0 = P.cnt["TE"]
                dvy0 = P.cnt["DV"]
                norm_last = self._norm_dv[HEADS - 1]

                @block.tensor
                def _(te, te0=te0, dvy0=dvy0, norm_last=norm_last):
                    P.w(te, "DV", norm_last)
                    for mt in range(8):
                        s, mj = mt // 4, mt % 4
                        pa = P.PSB[mt % 2]
                        if mt >= 2:
                            P.w(te, "DV", dvy0 + mt - 1)
                        for kt in range(NCT):
                            ins = te.matmul(
                                pa[:, 0:512],
                                WO[:, s, kt * 512 + mj * 128:kt * 512 + mj * 128 + 128],
                                YF[:, kt, 0:512],
                                start=(kt == 0), stop=(kt == NCT - 1))
                        P.inc("TE", ins)

                @block.vector
                def _(v, te0=te0, XGB=XGB):
                    for mt in range(8):
                        P.w(v, "TE", te0 + mt + 1)
                        ins = v.tensor_add(XGB[:, mt, :], XGB[:, mt, :],
                                           P.PSB[mt % 2][:, 0:512])
                        P.inc("DV", ins)
                dvu = P.cnt["DV"]
                self._psb_rd[0] = [("DV", dvy0 + 7)]
                self._psb_rd[1] = [("DV", dvy0 + 8)]

                if phase == "g":
                    @block.sync
                    def _(sy, g0=g0, dvu=dvu, XGB=XGB):
                        P.w(sy, "DV", dvu)
                        dst = self.dram["ures"][:, g0:g0 + 512].rearrange(
                            "(ct p) c -> p ct c", p=128)
                        P.inc("OD", sy.dma_start(out=dst, in_=XGB[:, :, :]), 16)
                else:
                    @block.vector
                    def _(v, g=g, XGB=XGB):
                        for hf in range(2):
                            P.w(v, "XD", self._fx_xd[(g, hf)])
                            for j in range(4):
                                ins = v.tensor_add(XGB[:, hf * 4 + j, :],
                                                   XGB[:, hf * 4 + j, :],
                                                   FHB[hf][:, j, :])
                        P.inc("DV", ins)
                    dvf = P.cnt["DV"]

                    @block.sync
                    def _(sy, g0=g0, dvf=dvf, XGB=XGB):
                        P.w(sy, "DV", dvf)
                        dst = rout[:, g0:g0 + 512].rearrange(
                            "(ct p) c -> p ct c", p=128)
                        P.inc("OD", sy.dma_start(out=dst, in_=XGB[:, :, :]), 16)
                    if g + 1 < 4:
                        emit_fx_load(g + 1)
                ab_od[(phase, g)] = P.cnt["OD"]

            if phase == "l":
                self._fx_xd = {}
                emit_fx_load(0)
            emit_load(0)
            emit_load(1)
            for p in prep_pieces(0):
                p()
            for g in range(4):
                feeder = prep_pieces(g + 1) if g + 1 < 4 else []
                emit_attention(g, feeder)
                if g + 2 < 4:
                    emit_load(g + 2)
    def emit_tail(self, block):
        P = self

        @block.sync
        def _(sy):
            P.w(sy, "OD")
            P.w(sy, "WD")
            P.w(sy, "XD")

    def build(self):
        with self.nc.Block() as block:
            self.emit_consts(block)
            for st in self.stages:
                if st == "ffn_only":
                    self.emit_ffn(block, self.dram["xin"], self.dram["out"])
                elif st == "aa_only":
                    self.emit_attna(block, self.dram["xin"], self.dram["out"])
                elif st == "aa":
                    self.emit_attna(block, self.dram["xin"], self.dram["xres"])
                elif st == "ab_only":
                    self.emit_attnb(block, self.dram["xin"], self.dram["out"])
                elif st == "ab":
                    self.emit_attnb(block, self.dram["xres"], self.dram["xres"])
                elif st == "ffn1":
                    self.emit_ffn(block, self.dram["xres"], self.dram["xres"])
                elif st == "ffn2":
                    self.emit_ffn(block, self.dram["xres"], self.dram["out"])
            self.emit_tail(block)
        self.close()
        return self.nc


def kernel(**inputs):
    g = prep_global(inputs)
    x = inputs["L0_attn_a_x"]
    xt = np.ascontiguousarray(x.transpose(0, 2, 1))
    prog = Prog(stages=("aa", "ffn1", "ab", "ffn2"))
    nc = prog.build()
    in_maps = []
    for b in range(8):
        m = dict(g)
        m["xin"] = xt[b]
        in_maps.append(m)
    res = run_bass_kernel_spmd(nc, in_maps, core_ids=list(range(8)))
    outs = np.stack([r["out"] for r in res.results], axis=0)
    return np.ascontiguousarray(outs.transpose(0, 2, 1))
